# version 01
# baseline (speedup 1.0000x reference)
import numpy as np
import concourse.bass as bass
import concourse.mybir as mybir
from concourse.bass_utils import run_bass_kernel_spmd

F32 = mybir.dt.float32
BF16 = mybir.dt.bfloat16
AF = mybir.ActivationFunctionType
OP = mybir.AluOpType
AXL = mybir.AxisListType

D = 2048
DI = 4096
H = 64
P = 64
NST = 128
CH = 64
NCH = 16
T = 1024
TH = T + 4
XP = 4416
NH = 8192
KTOP = 819
RG = [[0, 1], [2, 3], [4, 5], [6, 7]]
BIG = 1e30

DEBUG_TAPS = []


def tsl(i, n):
    return slice(i * n, (i + 1) * n)


class B:
    ENGINES = ("sync", "gpsimd", "tensor", "vector", "scalar")

    def __init__(self, nc):
        self.nc = nc
        self.ops = {e: [] for e in self.ENGINES}
        self.cm = []
        self.sems = {}
        self.tick = {}
        for e in self.ENGINES:
            c = nc.semaphore("tk_" + e)
            self.sems[e] = c.__enter__()
            self.cm.append(c)
            self.tick[e] = 0

    def sbuf(self, name, shape, dtype=F32, side="left"):
        c = self.nc.sbuf_tensor(name, shape, dtype, side=side)
        t = c.__enter__()
        self.cm.append(c)
        self.by_handle = getattr(self, "by_handle", {})
        self.by_handle[id(t)] = c
        return t

    def release(self, t):
        c = self.by_handle.pop(id(t))
        self.cm.remove(c)
        c.__exit__(None, None, None)

    def psum(self, name, shape, dtype=F32):
        c = self.nc.psum_tensor(name, shape, dtype)
        t = c.__enter__()
        self.cm.append(c)
        self.by_handle = getattr(self, "by_handle", {})
        self.by_handle[id(t)] = c
        return t

    def step(self, eng, fn, deps=(), dma=False):
        inc = 16 if dma else 1
        self.tick[eng] += inc
        val = self.tick[eng]
        sems = self.sems

        def run(e, fn=fn, deps=tuple(deps), eng=eng, val=val, dma=dma):
            for de, dv in deps:
                if de == eng and dv <= val - inc:
                    if eng in ("vector", "scalar", "gpsimd"):
                        e.wait_ge(sems[de], dv)
                    continue
                e.wait_ge(sems[de], dv)
            last = fn(e)
            if dma:
                assert last is not None, "dma step must return the dma instr"
                last.then_inc(sems[eng], 16)
            else:
                if last is not None:
                    last.then_inc(sems[eng], 1)
                else:
                    e.nop().then_inc(sems[eng], 1)
        self.ops[eng].append(run)
        return (eng, val)

    def barrier(self):
        snap = dict(self.tick)
        out = []
        for e in self.ENGINES:
            deps = [(o, snap[o]) for o in self.ENGINES if o != e]
            out.append(self.step(e, lambda ee: None, deps=deps))
        return dict(self.tick)

    def finalize(self):
        nc = self.nc
        with nc.Block() as block:
            @block.sync
            def _(e):
                for f in self.ops["sync"]:
                    f(e)

            @block.gpsimd
            def _(e):
                for f in self.ops["gpsimd"]:
                    f(e)

            @block.tensor
            def _(e):
                for f in self.ops["tensor"]:
                    f(e)

            @block.vector
            def _(e):
                for f in self.ops["vector"]:
                    f(e)

            @block.scalar
            def _(e):
                for f in self.ops["scalar"]:
                    f(e)

    def close(self):
        for c in reversed(self.cm):
            c.__exit__(None, None, None)


def build_nc(taps=(), upto='ALL', ncx=16):
    TAP = set(taps)
    nc = bass.Bass("TRN2", target_bir_lowering=False)
    b = B(nc)
    g = {}

    def din(name, shape, dtype=F32):
        g[name] = nc.dram_tensor(name, shape, dtype, kind="ExternalInput").ap()

    def dout(name, shape, dtype=F32):
        g[name] = nc.dram_tensor(name, shape, dtype, kind="ExternalOutput").ap()

    def dint(name, shape, dtype=F32):
        g[name] = nc.dram_tensor(name, shape, dtype, kind="Internal").ap()

    din("x_in", [D, TH])
    din("halo_mask", [128, 1])
    din("role", [128, 1])
    din("w_in", [D, 2 * DI])
    din("conv_w", [128, 128])
    din("conv_b", [128, 32])
    din("w_xp", [DI, XP])
    din("w_dt", [H, H])
    din("b_dt", [H])
    din("a_log", [H])
    din("d_vec", [H])
    din("w_out", [DI, D])
    din("w_syn", [D, NH])
    din("b_syn", [NH])
    din("w_o2", [NH, D])
    din("b_out", [D])
    din("gamma", [128, 16])
    din("beta", [128, 16])
    din("iota32", [128, 32])
    din("d_row", [1, H])
    dout("y_out", [D, T])
    dint("silu_z_dram", [DI, T])
    g["gated_dram"] = nc.dram_tensor("gated_dram", [DI, T], BF16, kind="Internal").ap()
    dint("h1_dram", [D, T])
    dint("z_dram", [T, NH])
    dint("cc_in", [NST, H * P])
    dint("cc_out", [2 * NST, H * P])

    def tap(name, shape, dtype=F32):
        if name in TAP:
            dout(name, shape, dtype)
            return g[name]
        return None

    convw = b.sbuf("convw", [128, 32 * 4])
    convb = b.sbuf("convb", [128, 32])
    bdtn = b.sbuf("bdtn", [H, 1])
    expa = b.sbuf("expa", [H, 1])
    dvec = b.sbuf("dvec", [H, 1])
    wdtb = b.sbuf("wdtb", [H, H], BF16)
    gam = b.sbuf("gam", [128, 16])
    bet = b.sbuf("bet", [128, 16])
    hmask = b.sbuf("hmask_sb", [128, 1])
    role = b.sbuf("role_sb", [128, 1])
    onesr = b.sbuf("onesr", [1, 128])
    epsr = b.sbuf("epsr", [1, 1])
    mhalfr = b.sbuf("mhalfr", [1, 1])
    halfp = b.sbuf("halfp", [128, 1])
    halfn = b.sbuf("halfn", [128, 1])
    onesK = b.sbuf("onesK", [128, 128])
    tril = b.sbuf("tril", [CH, CH])
    ident = b.sbuf("ident", [128, 128])
    identb128 = b.sbuf("identb128", [128, 128], BF16)

    lds = []
    pairs = [
        (convw, g["conv_w"][:, :]),
        (convb, g["conv_b"][:, :]),
        (expa, g["a_log"].rearrange("(h one) -> h one", one=1)),
        (dvec, g["d_vec"].rearrange("(h one) -> h one", one=1)),
        (wdtb, None),
        (gam, g["gamma"][:, :]),
        (bet, g["beta"][:, :]),
        (hmask, g["halo_mask"][:, :]),
        (role, g["role"][:, :]),
        (bdtn, g["b_dt"].rearrange("(h one) -> h one", one=1)),
    ]

    def f_lda(e):
        last = None
        for dst, src in pairs:
            if src is None:
                continue
            last = e.dma_start(out=dst[:], in_=src)
        return last
    a_deps = []
    for dst, src in pairs:
        if src is None:
            continue
        a_deps.append(b.step("sync", (lambda dst, src: lambda e: e.dma_start(out=dst[:], in_=src))(dst, src), dma=True))
    wdtf = b.sbuf("wdtf", [H, H])
    a_wdt = b.step("sync", lambda e: e.dma_start(out=wdtf[:], in_=g["w_dt"][:, :]), dma=True)
    a_wdtc = b.step("vector", lambda e: e.tensor_copy(out=wdtb[:], in_=wdtf[:]), deps=[a_wdt])

    def f_const(e):
        e.memset(onesr[:], 1.0)
        e.memset(epsr[:], 1e-5)
        e.memset(mhalfr[:], -0.5)
        e.memset(halfp[:], 0.5)
        e.memset(halfn[:], -0.5)
        e.memset(onesK[:], 1.0)
        e.memset(tril[:], 0.0)
        e.affine_select(out=tril[:], in_=tril[:], compare_op=OP.is_gt,
                        fill=1.0, base=0, pattern=[[-1, CH]], channel_multiplier=1)
        e.memset(ident[:], 0.0)
        e.affine_select(out=ident[:], in_=ident[:], compare_op=OP.not_equal,
                        fill=1.0, base=0, pattern=[[-1, 128]], channel_multiplier=1)
        return None
    a_const = b.step("gpsimd", f_const)

    a_all = list(a_deps) + [a_wdtc, a_const]
    a_expa = b.step("scalar", lambda e: e.activation(out=expa[:], in_=expa[:], func=AF.Exp),
                    deps=a_deps)
    a_bdtn = b.step("vector", lambda e: e.tensor_scalar_mul(bdtn[:], bdtn[:], -1.0),
                    deps=a_deps)

    AX = b.sbuf("AX", [128, 32 * T], BF16)
    x_bf16 = AX[:, 0:16 * TH]

    wst = b.sbuf("wst", [128, 8192], side="right")
    wsb = b.sbuf("wsb", [128, 8192], BF16, side="right")

    xst = b.sbuf("xst", [128, TH], side="right")
    xbr = b.sbuf("xbr", [128, TH], side="right")
    uev = b.sbuf("uev", [128, 1024], side="right")

    B_fm = b.sbuf("B_fm", [NST, T], BF16)
    B_tok = b.sbuf("B_tok", [128, 8 * NST], BF16)
    C_fm = b.sbuf("C_fm", [NST, T], BF16)
    dtneg = b.sbuf("dtneg", [H, T])
    a_sc = b.sbuf("a_sc", [H, T])
    dt_bf = b.sbuf("dt_bf", [H, T], BF16)
    x_conv = b.sbuf("x_conv", [128, 32 * T], BF16)

    prev_cast = None
    cast_ticks = []
    for i in range(16):
        deps = [a_deps[-1]] if i == 0 else [cast_ticks[i - 1]]
        ldt = b.step("sync", (lambda i: lambda e: e.dma_start(
            out=xst[:, :], in_=g["x_in"][tsl(i, 128), :]))(i),
            deps=deps, dma=True)

        mkt = b.step("vector", lambda e: e.tensor_scalar(
            out=xst[:, 0:4], in0=xst[:, 0:4], scalar1=hmask[:], scalar2=None, op0=OP.mult),
            deps=[ldt] + ([a_deps[-1]] if i == 0 else []))
        ct = b.step("vector", (lambda i: lambda e: e.tensor_copy(
            out=x_bf16[:, i * TH:(i + 1) * TH], in_=xst[:, :]))(i), deps=[mkt])
        cast_ticks.append(ct)
    b_xcast = cast_ticks[-1]

    ppB0 = b.psum("ppB0", [128, 512])
    ppB1 = b.psum("ppB1", [128, 512])
    ppB2 = b.psum("ppB2", [128, 4])

    t_silu_z = tap("t_silu_z", [DI, T])
    t_xconv = tap("t_xconv", [DI, T])

    ev_t = None
    for oc in range(16):
        is_z = oc >= 8
        wl = b.step("sync", (lambda oc: lambda e: e.dma_start(
            out=wst[:].rearrange("p (k f) -> p k f", k=16),
            in_=g["w_in"][:, tsl(oc, 512)].rearrange("(k p) f -> p k f", p=128)))(oc),
            deps=[ev_t] if ev_t else [], dma=True)
        wc = b.step("vector", lambda e: e.tensor_copy(out=wsb[:], in_=wst[:]), deps=[wl])

        for sub in range(4):
            def f_mm(e, sub=sub):
                last = None
                for grp, (f0, fn, pp) in enumerate(((0, 512, ppB0), (512, 512, ppB1), (1024, 4, ppB2))):
                    for k in range(16):
                        lhsT = wsb[:, k * 512 + sub * 128:k * 512 + sub * 128 + 128]
                        rhs = x_bf16[:, k * TH + f0:k * TH + f0 + fn]
                        last = e.matmul(pp[:, 0:fn], lhsT, rhs, start=(k == 0), stop=(k == 15))
                return last
            mm_deps = [wc, b_xcast]
            if ev_t is not None:
                mm_deps.append(ev_t)
            mm = b.step("tensor", f_mm, deps=mm_deps)

            ch = (oc % 8) * 4 + sub

            if not is_z:
                def f_conv(e, ch=ch):
                    e.tensor_copy(out=xbr[:, 0:512], in_=ppB0[:])
                    e.tensor_copy(out=xbr[:, 512:1024], in_=ppB1[:])
                    e.tensor_copy(out=xbr[:, 1024:1028], in_=ppB2[:])
                    acc = uev[:, 0:1024]
                    e.tensor_scalar(out=acc, in0=xbr[:, 1:1025],
                                    scalar1=convw[:, ch * 4:ch * 4 + 1],
                                    scalar2=None, op0=OP.mult)
                    last = None
                    for j in (1, 2, 3):
                        last = e.scalar_tensor_tensor(out=acc, in0=xbr[:, 1 + j:1 + j + 1024],
                                               scalar=convw[:, ch * 4 + j:ch * 4 + j + 1],
                                               in1=acc, op0=OP.mult, op1=OP.add)
                    return last
                cv = b.step("vector", f_conv, deps=[mm])
                sl = b.step("scalar", (lambda ch: lambda e: e.activation(
                    out=x_conv[:, ch * T:(ch + 1) * T], in_=uev[:, 0:1024],
                    func=AF.Silu, bias=convb[:, ch:ch + 1], scale=1.0))(ch),
                    deps=[cv])
                ev_t = sl
                if t_xconv is not None:
                    b.step("gpsimd", (lambda ch: lambda e: e.dma_start(
                        out=g["t_xconv"][tsl(ch, 128), :], in_=x_conv[:, ch * T:(ch + 1) * T]))(ch),
                        deps=[sl], dma=True)
            else:
                def f_zev(e, ch=ch):
                    dz = uev[:, 0:1024]
                    e.activation(out=dz[:, 0:508], in_=ppB0[:, 4:512], func=AF.Silu)
                    e.activation(out=dz[:, 508:1020], in_=ppB1[:], func=AF.Silu)
                    return e.activation(out=dz[:, 1020:1024], in_=ppB2[:], func=AF.Silu)
                sl = b.step("scalar", f_zev, deps=[mm])
                st = b.step("gpsimd", (lambda ch: lambda e: e.dma_start(
                    out=g["silu_z_dram"][tsl(ch, 128), :], in_=uev[:, 0:1024]))(ch),
                    deps=[sl], dma=True)
                ev_t = st
                if t_silu_z is not None:
                    b.step("gpsimd", (lambda ch: lambda e: e.dma_start(
                        out=g["t_silu_z"][tsl(ch, 128), :], in_=uev[:, 0:1024]))(ch),
                        deps=[sl], dma=True)

    if upto == 'B':
        b.finalize(); b.close(); return nc, g
    bar_B = b.barrier()
    b.release(ppB2)
    b.release(ppB1)
    b.release(ppB0)

    ppC0h = b.psum("ppC0", [128, 512])
    ppC1h = b.psum("ppC1", [128, 512])
    ppC0 = ppC0h[:, 0:256]
    ppC1 = ppC1h[:, 0:256]

    t_X = tap("t_X", [8 * 128, DI], BF16)

    ev_t = None
    wc_prev = None
    for oc in range(ncx):
        wl = b.step("sync", (lambda oc: lambda e: e.dma_start(
            out=wst[:].rearrange("p (k f) -> p k f", k=32),
            in_=g["w_xp"][:, tsl(oc, 256)].rearrange("(k p) f -> p k f", p=128)))(oc),
            deps=[ev_t] if ev_t else [], dma=True)
        wc = b.step("vector", lambda e: e.tensor_copy(out=wsb[:], in_=wst[:]), deps=[wl])

        for tt in range(8):
            pp = (ppC0, ppC1)[tt % 2]

            def f_mm(e, oc=oc, tt=tt, pp=pp):
                last = None
                for k in range(32):
                    lhsT = x_conv[:, k * T + tt * 128:k * T + tt * 128 + 128]
                    rhs = wsb[:, k * 256:(k + 1) * 256]
                    last = e.matmul(pp[:], lhsT, rhs, start=(k == 0), stop=(k == 31))
                return last
            deps = [wc]
            if ev_t is not None:
                deps.append(ev_t)
            mm = b.step("tensor", f_mm, deps=deps)
            ev = b.step("vector", (lambda oc, tt, pp: lambda e: e.tensor_copy(
                out=AX[:, tt * DI + oc * 256: tt * DI + oc * 256 + 256], in_=pp[:]))(oc, tt, pp),
                deps=[mm])
            ev_t = ev
    x_done = ev_t
    if t_X is not None:
        for tt in range(8):
            b.step("gpsimd", (lambda tt: lambda e: e.dma_start(
                out=g["t_X"][tsl(tt, 128), :], in_=AX[:, tt * DI:(tt + 1) * DI]))(tt),
                deps=[x_done], dma=True)

    if upto == 'CX':
        b.finalize(); b.close(); return nc, g

    b.release(ppC1h)
    b.release(ppC0h)
    pp6 = [b.psum("pp6_%d" % i, [128, 512]) for i in range(6)]
    ppBt = [b.psum("ppBt%d" % i, [128, 512]) for i in range(2)]
    t_Bfm = tap("t_Bfm", [NST, T])
    t_Cfm = tap("t_Cfm", [NST, T])
    t_dt = tap("t_dt", [H, T])

    parts = ((0, 64, "dt"), (64, 128, "B"), (192, 128, "C"))
    mm_half = []
    wc = None
    for kh in range(2):
        deps = [x_done] if kh == 0 else list(mm_half)
        wl = b.step("sync", (lambda kh: lambda e: e.dma_start(
            out=wst[:, 0:16 * 320].rearrange("p (k f) -> p k f", k=16),
            in_=g["w_xp"][kh * 2048:(kh + 1) * 2048, DI:XP].rearrange("(k p) f -> p k f", p=128)))(kh),
            deps=deps, dma=True)
        wc = b.step("vector", lambda e: e.tensor_copy(
            out=wsb[:, 0:16 * 320], in_=wst[:, 0:16 * 320]), deps=[wl])
        mm_half = []
        for pi, (off, width, pname) in enumerate(parts):
            for tg in range(2):
                def f_mm(e, off=off, width=width, tg=tg, kh=kh, pi=pi):
                    pp = pp6[pi * 2 + tg]
                    last = None
                    for kk in range(16):
                        lhsT = wsb[:, kk * 320 + off: kk * 320 + off + width]
                        rhs = x_conv[:, (kh * 16 + kk) * T + tg * 512: (kh * 16 + kk) * T + tg * 512 + 512]
                        last = e.matmul(pp[0:width, :], lhsT, rhs,
                                 start=(kh == 0 and kk == 0), stop=(kh == 1 and kk == 15))
                    return last
                mm = b.step("tensor", f_mm, deps=[wc])
                mm_half.append(mm)
        def f_bt(e, kh=kh):
            last = None
            for tt in range(8):
                pp = ppBt[tt // 4]
                for kk in range(16):
                    k = kh * 16 + kk
                    lhsT = x_conv[:, k * T + tt * 128:k * T + tt * 128 + 128]
                    rhs = wsb[:, kk * 320 + 64: kk * 320 + 192]
                    last = e.matmul(pp[:, (tt % 4) * 128:(tt % 4) * 128 + 128], lhsT, rhs,
                                    start=(kh == 0 and kk == 0), stop=(kh == 1 and kk == 15))
            return last
        mmbt = b.step("tensor", f_bt, deps=[wc])
        mm_half.append(mmbt)

    part_ev = []
    for pi, (off, width, pname) in enumerate(parts):
        for tg in range(2):
            def f_ev(e, off=off, width=width, tg=tg, pname=pname, pi=pi):
                pp = pp6[pi * 2 + tg]
                if pname == "dt":
                    return e.tensor_copy(out=dtneg[:, tg * 512:(tg + 1) * 512], in_=pp[0:64, :])
                dst = B_fm if pname == "B" else C_fm
                return e.tensor_copy(out=dst[:, tg * 512:(tg + 1) * 512], in_=pp[0:128, :])
            ev = b.step("vector", f_ev, deps=[mm_half[pi * 2 + tg]])
            part_ev.append(ev)

    for i in range(2):
        part_ev.append(b.step("vector", (lambda i: lambda e: e.tensor_copy(
            out=B_tok[:, i * 512:(i + 1) * 512], in_=ppBt[i][:, :]))(i),
            deps=[mm_half[-1]]))

    c_dtc = b.step("vector", lambda e: e.tensor_copy(out=dt_bf[:], in_=dtneg[:]),
                   deps=[part_ev[1]])

    ppdt = pp6[0]
    dt_mms = []
    for tg in range(2):
        mm = b.step("tensor", (lambda tg: lambda e: e.matmul(
            ppdt[0:H, :], wdtb[:], dt_bf[:, tg * 512:(tg + 1) * 512], start=True, stop=True))(tg),
            deps=[c_dtc, a_wdtc, part_ev[-1]] + ([] if tg == 0 else [dt_mms[-1][1]]))
        sg = b.step("scalar", (lambda tg: lambda e: e.activation(
            out=dtneg[:, tg * 512:(tg + 1) * 512], in_=ppdt[0:H, :], func=AF.Sigmoid,
            bias=bdtn[:], scale=-1.0))(tg), deps=[mm, a_bdtn])
        dt_mms.append((mm, sg))
    c_ln = b.step("scalar", lambda e: e.activation(out=dtneg[:], in_=dtneg[:], func=AF.Ln),
                  deps=[dt_mms[-1][1]])
    c_ln = b.step("vector", lambda e: e.tensor_scalar_max(dtneg[:], dtneg[:], -87.0),
                  deps=[c_ln])
    c_asc = b.step("vector", lambda e: e.tensor_scalar(
        out=a_sc[:], in0=dtneg[:], scalar1=expa[:], scalar2=None, op0=OP.mult),
        deps=[c_ln, a_expa])
    if t_dt is not None:
        b.step("gpsimd", lambda e: e.dma_start(out=g["t_dt"][:, :], in_=dtneg[:]),
               deps=[c_ln], dma=True)
    if t_Bfm is not None:
        b.step("gpsimd", lambda e: e.dma_start(out=g["t_Bfm"][:, :], in_=B_fm[:]),
               deps=[part_ev[3]], dma=True)
    if t_Cfm is not None:
        b.step("gpsimd", lambda e: e.dma_start(out=g["t_Cfm"][:, :], in_=C_fm[:]),
               deps=[part_ev[5]], dma=True)

    bar_C = b.barrier()
    bar_C = b.barrier()
    if upto == 'C':
        b.finalize(); b.close(); return nc, g

    b.release(uev)
    b.release(xbr)
    b.release(xst)
    b.release(wsb)
    b.release(wst)
    b.release(x_conv)
    for t in reversed(ppBt):
        b.release(t)
    for t in reversed(pp6):
        b.release(t)
    psA = b.psum("psA", [64, 1024])
    psB = b.psum("psB", [64, 1024])
    psE = b.psum("psE", [64, 512])
    psT = b.psum("psT", [128, 512])
    psS = b.psum("psS", [128, 1024])

    Dm = b.sbuf("Dm", [64, 1024], side="right")
    Gt = b.sbuf("Gt", [64, DI], BF16, side="right")
    XB1 = b.sbuf("XB1", [64, DI], BF16, side="right")
    XB2 = b.sbuf("XB2", [64, DI], BF16, side="right")
    XB3 = b.sbuf("XB3", [64, DI], BF16, side="right")
    Ytc = b.sbuf("Ytc", [64, DI], BF16, side="right")
    aT_all = b.sbuf("aT_all", [64, NCH * CH], side="right")
    cs_lh = b.sbuf("cs_lh", [64, NCH * CH], side="right")
    cs_hl = b.sbuf("cs_hl", [64, NCH * CH], side="right")
    sdo_lh = b.sbuf("sdo_lh", [64, NCH * CH], side="right")
    decT = b.sbuf("decT", [64, NCH * CH], side="right")
    dtT = b.sbuf("dtT", [64, NCH * CH], side="right")
    dtpos = b.sbuf("dtpos", [H, T], side="right")
    lamlog = b.sbuf("lamlog", [H, NCH], side="right")
    lamv = b.sbuf("lamv", [H, NCH], side="right")
    zer16 = b.sbuf("zer16", [H, NCH], side="right")
    cdecl = b.sbuf("cdecl", [H, NCH], side="right")
    lam_tab = b.sbuf("lam_tab", [128, H * NCH], side="right")
    cdec_tab = b.sbuf("cdec_tab", [64, H * NCH], side="right")
    d_tab = b.sbuf("d_tab", [64, H], side="right")
    Mm_sb = b.sbuf("Mm_sb", [64, CH], side="right")
    identb = b.sbuf("identb", [64, 64], BF16, side="right")
    BzT = b.sbuf("BzT", [64, NST], BF16, side="right")
    rb = b.sbuf("rb", [128, 1024], BF16, side="right")
    SCR = b.sbuf("SCR", [128, DI], side="right")
    r_run = b.sbuf("r_run", [128, DI], side="right")
    S_in = b.sbuf("S_in", [128, DI], BF16, side="right")
    szt = SCR[:, 0:2048]
    uevY = SCR[0:64, 2048:3072]
    lrow = SCR[0:1, 0:1024]
    crow = SCR[0:1, 1024:2048]
    drow = SCR[0:1, 2048:2048 + H]
    gst = b.sbuf("gst", [128, 512], BF16, side="right")

    t_Y = tap("t_Y", [CH, NCH * DI], BF16)
    t_Yf = tap("t_Yf", [CH, NCH * DI], BF16)
    t_gated = tap("t_gated", [DI, T], BF16)
    g["ytok_dram"] = nc.dram_tensor("ytok_dram", [CH, NCH * DI], BF16, kind="Internal").ap()

    c_dtp = b.step("vector", lambda e: e.tensor_scalar_mul(dtpos[:], dtneg[:], -1.0),
                   deps=[c_asc])
    c_zer = b.step("vector", lambda e: e.memset(zer16[:], 0.0), deps=[])
    c_idb = b.step("vector", lambda e: e.tensor_copy(out=identb[:], in_=ident[0:64, 0:64]),
                   deps=[a_const])
    c_idb2 = b.step("vector", lambda e: e.tensor_copy(out=identb128[:], in_=ident[:]),
                    deps=[a_const])
    c_drow = b.step("sync", lambda e: e.dma_start(out=drow[:], in_=g["d_row"][:, :]), dma=True)

    pre_dep = [c_asc, c_dtp]
    for half in range(2):
        def f_tr_a(e, half=half):
            last = None
            for j in range(8):
                z = half * 8 + j
                last = e.transpose(psT[0:64, j * 64:(j + 1) * 64],
                                   a_sc[:, z * CH:(z + 1) * CH], ident[0:64, 0:64])
            return last
        mm1 = b.step("tensor", f_tr_a, deps=pre_dep)
        ev1 = b.step("vector", (lambda half: lambda e: e.tensor_copy(
            out=aT_all[:, half * 512:(half + 1) * 512], in_=psT[0:64, :]))(half), deps=[mm1])

        def f_cs(e, half=half):
            last = None
            for j in range(8):
                z = half * 8 + j
                last = e.matmul(psT[0:64, j * 64:(j + 1) * 64], tril[:],
                                aT_all[:, z * 64:(z + 1) * 64], start=True, stop=True)
            return last
        mm2 = b.step("tensor", f_cs, deps=[ev1])
        ev2 = b.step("vector", (lambda half: lambda e: e.tensor_copy(
            out=cs_lh[:, half * 512:(half + 1) * 512], in_=psT[0:64, :]))(half), deps=[mm2])
        sd2 = b.step("scalar", (lambda half: lambda e: e.activation(
            out=sdo_lh[:, half * 512:(half + 1) * 512],
            in_=cs_lh[:, half * 512:(half + 1) * 512], func=AF.Exp))(half), deps=[ev2])

        def f_tr_cs(e, half=half):
            last = None
            for j in range(8):
                z = half * 8 + j
                last = e.transpose(psT[0:64, j * 64:(j + 1) * 64],
                                   cs_lh[:, z * 64:(z + 1) * 64], ident[0:64, 0:64])
            return last
        mm3 = b.step("tensor", f_tr_cs, deps=[ev2])
        ev3 = b.step("vector", (lambda half: lambda e: e.tensor_copy(
            out=cs_hl[:, half * 512:(half + 1) * 512], in_=psT[0:64, :]))(half), deps=[mm3])

        def f_ll(e, half=half):
            last = None
            for j in range(8):
                z = half * 8 + j
                last = e.tensor_copy(out=lamlog[:, z:z + 1],
                                     in_=cs_hl[:, z * 64 + 63:z * 64 + 64])
            return last
        ev3b = b.step("vector", f_ll, deps=[ev3])

        def f_dec(e, half=half):
            last = None
            for j in range(8):
                z = half * 8 + j
                last = e.activation(out=Dm[:, j * 64:(j + 1) * 64],
                                    in_=cs_hl[:, z * 64:(z + 1) * 64], func=AF.Exp,
                                    bias=lamlog[:, z:z + 1], scale=-1.0)
            return last
        ev4 = b.step("scalar", f_dec, deps=[ev3b])

        def f_tr_dec(e, half=half):
            last = None
            for j in range(8):
                last = e.transpose(psT[0:64, j * 64:(j + 1) * 64],
                                   Dm[:, j * 64:(j + 1) * 64], ident[0:64, 0:64])
            return last
        mm4 = b.step("tensor", f_tr_dec, deps=[ev4])
        ev5 = b.step("vector", (lambda half: lambda e: e.tensor_copy(
            out=decT[:, half * 512:(half + 1) * 512], in_=psT[0:64, :]))(half), deps=[mm4])

        def f_tr_dt(e, half=half):
            last = None
            for j in range(8):
                z = half * 8 + j
                last = e.transpose(psT[0:64, j * 64:(j + 1) * 64],
                                   dtpos[:, z * CH:(z + 1) * CH], ident[0:64, 0:64])
            return last
        mm5 = b.step("tensor", f_tr_dt, deps=[c_dtp, ev5])
        ev6 = b.step("vector", (lambda half: lambda e: e.tensor_copy(
            out=dtT[:, half * 512:(half + 1) * 512], in_=psT[0:64, :]))(half), deps=[mm5])
        pre_dep = [ev6]

    c_cd0 = b.step("vector", lambda e: e.memset(cdecl[:, 0:1], 0.0), deps=pre_dep)
    c_cd1 = b.step("vector", lambda e: e.tensor_tensor_scan(
        out=cdecl[:, 1:NCH], data0=lamlog[:, 0:NCH - 1], data1=zer16[0:H, 1:NCH],
        initial=0.0, op0=OP.add, op1=OP.add), deps=[c_cd0, c_zer])
    c_cd2 = b.step("scalar", lambda e: e.activation(out=cdecl[:], in_=cdecl[:], func=AF.Exp),
                   deps=[c_cd1])
    c_cd3 = b.step("vector", lambda e: e.tensor_scalar(
        out=cdecl[:], in0=cdecl[:], scalar1=role[0:H, :], scalar2=None, op0=OP.mult),
        deps=[c_cd2, a_deps[-1]])
    c_lam = b.step("scalar", lambda e: e.activation(out=lamv[:], in_=lamlog[:], func=AF.Exp),
                   deps=[c_cd1])

    c_fl1 = b.step("gpsimd", lambda e: e.dma_start(
        out=lrow[:].rearrange("one (h z) -> one h z", z=NCH), in_=lamv[:, :]),
        deps=[c_lam], dma=True)
    c_fl2 = b.step("gpsimd", lambda e: e.dma_start(
        out=crow[:].rearrange("one (h z) -> one h z", z=NCH), in_=cdecl[:, :]),
        deps=[c_cd3], dma=True)

    bc_deps = [c_fl1, c_fl2, c_drow, a_const]
    for j in range(2):
        mmb = b.step("tensor", (lambda j: lambda e: e.matmul(
            psT[:, 0:512], onesr[:], lrow[:, j * 512:(j + 1) * 512], start=True, stop=True))(j),
            deps=bc_deps)
        evb = b.step("vector", (lambda j: lambda e: e.tensor_copy(
            out=lam_tab[:, j * 512:(j + 1) * 512], in_=psT[:, :]))(j), deps=[mmb])
        mmc = b.step("tensor", (lambda j: lambda e: e.matmul(
            psT[0:64, 0:512], onesr[0:1, 0:64], crow[:, j * 512:(j + 1) * 512],
            start=True, stop=True))(j), deps=[evb])
        evc = b.step("vector", (lambda j: lambda e: e.tensor_copy(
            out=cdec_tab[:, j * 512:(j + 1) * 512], in_=psT[0:64, :]))(j), deps=[mmc])
        bc_deps = [evc]
    mmd = b.step("tensor", lambda e: e.matmul(psT[0:64, 0:64], onesr[0:1, 0:64], drow[:, :],
                                              start=True, stop=True), deps=bc_deps)
    evd = b.step("vector", lambda e: e.tensor_copy(out=d_tab[:], in_=psT[0:64, 0:64]),
                 deps=[mmd])
    c_r0 = b.step("vector", lambda e: e.memset(r_run[:], 0.0), deps=[])

    def hrep(viewsrc, nh, np_=64):
        return viewsrc.rearrange("p (h one) -> p h one", one=1).to_broadcast([64, nh, np_])

    last_mm = evd
    last_v = evd
    last_park = None
    scan_done = None
    for z in range(NCH):
        zl = tsl(z, CH)
        dep_xz = [x_done]
        if last_park is not None:
            dep_xz.append(last_park)
        xz_cp = b.step("gpsimd", (lambda z: lambda e: e.dma_start(
            out=XB1[:, :], in_=AX[(z % 2) * 64:(z % 2) * 64 + 64,
                                  (z // 2) * DI:(z // 2 + 1) * DI]))(z),
            deps=dep_xz, dma=True)
        mmM = b.step("tensor", (lambda zl=zl: lambda e: e.matmul(
            psE[0:64, 0:64], B_fm[:, zl], C_fm[:, zl], start=True, stop=True))(),
            deps=[part_ev[-1], last_v])
        evM = b.step("vector", lambda e: e.tensor_tensor(
            out=Mm_sb[:], in0=psE[0:64, 0:64], in1=tril[:], op=OP.mult), deps=[mmM])
        evBt = b.step("gpsimd", (lambda z: lambda e: e.dma_start(
            out=BzT[:, :], in_=B_tok[(z % 2) * 64:(z % 2) * 64 + 64,
                                     (z // 2) * NST:(z // 2 + 1) * NST]))(z),
            deps=[part_ev[-1], xz_cp], dma=True)
        am_st = b.step("vector", (lambda z: lambda e: e.tensor_tensor(
            out=SCR[0:64, :].rearrange("p (h l) -> p h l", l=64),
            in0=hrep(aT_all[:, z * 64:(z + 1) * 64], 64),
            in1=tril[:].rearrange("p (one l) -> p one l", one=1).to_broadcast([64, 64, 64]),
            op=OP.mult))(z), deps=[evBt, last_v])
        x_d = b.step("vector", (lambda z: lambda e: e.tensor_tensor(
            out=XB2[:, :].rearrange("p (h q) -> p h q", q=64),
            in0=XB1[:, :].rearrange("p (h q) -> p h q", q=64),
            in1=hrep(d_tab[:, :], 64), op=OP.mult))(z), deps=[xz_cp, evd, am_st])
        x_s = b.step("vector", (lambda z: lambda e: e.tensor_tensor(
            out=XB3[:, :].rearrange("p (h q) -> p h q", q=64),
            in0=XB1[:, :].rearrange("p (h q) -> p h q", q=64),
            in1=hrep(dtT[:, z * 64:(z + 1) * 64], 64), op=OP.mult))(z), deps=[x_d])
        x_dc = b.step("vector", (lambda z: lambda e: e.tensor_tensor(
            out=XB1[:, :].rearrange("p (h q) -> p h q", q=64),
            in0=XB3[:, :].rearrange("p (h q) -> p h q", q=64),
            in1=hrep(decT[:, z * 64:(z + 1) * 64], 64), op=OP.mult))(z), deps=[x_s])
        g_dep = [am_st, evM]
        for gg in range(8):
            h0 = gg * 8
            mmE = b.step("tensor", (lambda gg: lambda e: e.matmul(
                psE[0:64, :], onesK[0:64, 0:64], SCR[0:64, gg * 512:(gg + 1) * 512],
                start=True, stop=True))(gg), deps=g_dep)
            gpar = (gg % 2) * 512
            evE = b.step("vector", (lambda z, gg, h0, gpar=gpar: lambda e: e.tensor_tensor(
                out=Dm[:, gpar:gpar + 512].rearrange("p (h l) -> p h l", l=64),
                in0=psE[0:64, :].rearrange("p (h l) -> p h l", l=64),
                in1=hrep(cs_lh[:, z * 64 + h0:z * 64 + h0 + 8], 8),
                op=OP.subtract))(z, gg, h0), deps=[mmE])
            exE = b.step("scalar", (lambda gg, gpar=gpar: lambda e: e.activation(
                out=Dm[:, gpar:gpar + 512], in_=Dm[:, gpar:gpar + 512],
                func=AF.Exp))(gg), deps=[evE])
            gfin = b.step("vector", (lambda gg, gpar=gpar: lambda e: e.scalar_tensor_tensor(
                out=Gt[:, gg * 512:(gg + 1) * 512].rearrange("p (h l) -> p h l", l=64),
                in0=Dm[:, gpar:gpar + 512].rearrange("p (h l) -> p h l", l=64),
                scalar=1.0,
                in1=Mm_sb[:].rearrange("p (one l) -> p one l", one=1).to_broadcast([64, 8, 64]),
                op0=OP.min, op1=OP.mult))(gg), deps=[exE])
            g_dep = [gfin]
        g_done = g_dep[0]
        for gq in range(4):
            gsl = tsl(gq, 1024)
            rbc = b.step("vector", (lambda gsl=gsl: lambda e: e.tensor_copy(
                out=rb[:, :], in_=r_run[:, gsl]))(), deps=[last_v if z == 0 else scan_done, g_done])
            mmO = b.step("tensor", (lambda zl=zl: lambda e: [
                e.matmul(psB[:, 0:512], C_fm[:, zl], rb[:, 0:512], start=True, stop=True),
                e.matmul(psB[:, 512:1024], C_fm[:, zl], rb[:, 512:1024], start=True, stop=True)][-1])(),
                deps=[rbc])
            mmS = b.step("tensor", (lambda gsl=gsl: lambda e: [
                e.matmul(psS[:, 0:512], BzT[:], XB1[:, gsl][:, 0:512], start=True, stop=True),
                e.matmul(psS[:, 512:1024], BzT[:], XB1[:, gsl][:, 512:1024], start=True, stop=True)][-1])(),
                deps=[x_dc, evBt, mmO])
            t1s = b.step("vector", (lambda z, gq, gsl=gsl: lambda e: e.tensor_tensor(
                out=SCR[:, 0:1024].rearrange("p (h q) -> p h q", q=64),
                in0=r_run[:, gsl].rearrange("p (h q) -> p h q", q=64),
                in1=lam_tab[:].rearrange("p (h z2) -> p h z2", z2=NCH)
                    [:, gq * 16:(gq + 1) * 16, z:z + 1].to_broadcast([128, 16, 64]),
                op=OP.mult))(z, gq), deps=[rbc, g_done])
            scan_done = b.step("vector", (lambda gsl=gsl: lambda e: e.tensor_tensor(
                out=r_run[:, gsl], in0=SCR[:, 0:1024], in1=psS[:, :], op=OP.add))(),
                deps=[t1s, mmS])
            def f_yd(e, z=z, gq=gq, gsl=gsl):
                last = e.matmul(psA[:, 0:512], identb[:], XB2[:, gsl][:, 0:512],
                                start=True, stop=False)
                last = e.matmul(psA[:, 512:1024], identb[:], XB2[:, gsl][:, 512:1024],
                                start=True, stop=False)
                for hl in range(16):
                    h = gq * 16 + hl
                    last = e.matmul(psA[:, hl * 64:(hl + 1) * 64],
                                    Gt[:, h * 64:(h + 1) * 64],
                                    XB3[:, h * 64:(h + 1) * 64],
                                    start=False, stop=(hl == 15))
                return last
            mmY = b.step("tensor", f_yd, deps=[x_s, g_done, scan_done])
            t2s = b.step("vector", (lambda z, gq: lambda e: e.tensor_tensor(
                out=uevY[:].rearrange("p (h q) -> p h q", q=64),
                in0=psB[:, :].rearrange("p (h q) -> p h q", q=64),
                in1=hrep(sdo_lh[:, z * 64 + gq * 16:z * 64 + gq * 16 + 16], 16),
                op=OP.mult))(z, gq), deps=[mmO, mmY])
            evY = b.step("vector", (lambda gsl=gsl: lambda e: e.tensor_tensor(
                out=Ytc[:, gsl], in0=uevY[:], in1=psA[:, :], op=OP.add))(),
                deps=[t2s])
            last_v = evY
        last_park = b.step("gpsimd", (lambda z: lambda e: e.dma_start(
            out=g["ytok_dram"][:, z * DI:(z + 1) * DI], in_=Ytc[:, :]))(z),
            deps=[last_v], dma=True)
        if t_Y is not None:
            b.step("gpsimd", (lambda z: lambda e: e.dma_start(
                out=g["t_Y"][:, z * DI:(z + 1) * DI], in_=Ytc[:, :]))(z),
                deps=[last_v], dma=True)

    cc_up = b.step("gpsimd", lambda e: e.dma_start(out=g["cc_in"][:, :], in_=r_run[:, :]),
                   deps=[scan_done], dma=True)
    cc_go = b.step("gpsimd", lambda e: e.collective_compute(
        "AllGather", OP.bypass, replica_groups=RG, ins=[g["cc_in"][:, :]],
        outs=[g["cc_out"][:, :]]), deps=[cc_up])
    cc_dn = b.step("gpsimd", lambda e: e.dma_start(out=SCR[:, :], in_=g["cc_out"][0:NST, :]),
                   deps=[cc_go], dma=True)
    cc_cast = b.step("vector", lambda e: e.tensor_copy(out=S_in[:], in_=SCR[:, :]),
                     deps=[cc_dn])

    last_g = cc_cast
    for z in range(NCH):
        zl = tsl(z, CH)
        yld = b.step("gpsimd", (lambda z: lambda e: e.dma_start(
            out=Ytc[:, :], in_=g["ytok_dram"][:, z * DI:(z + 1) * DI]))(z),
            deps=[last_g], dma=True)
        szl = b.step("gpsimd", (lambda zl=zl: lambda e: e.dma_start(
            out=szt[:].rearrange("p (j l) -> p j l", l=64),
            in_=g["silu_z_dram"][:, zl].rearrange("(j p) l -> p j l", p=128)))(),
            deps=[yld], dma=True)
        for gq in range(4):
            gsl = tsl(gq, 1024)
            mmC = b.step("tensor", (lambda zl=zl, gsl=gsl: lambda e: [
                e.matmul(psB[:, 0:512], C_fm[:, zl], S_in[:, gsl][:, 0:512],
                         start=True, stop=True),
                e.matmul(psB[:, 512:1024], C_fm[:, zl], S_in[:, gsl][:, 512:1024],
                         start=True, stop=True)][-1])(), deps=[cc_cast, yld, last_g])
            t2c = b.step("vector", (lambda z, gq: lambda e: e.tensor_tensor(
                out=uevY[:].rearrange("p (h q) -> p h q", q=64),
                in0=psB[:, :].rearrange("p (h q) -> p h q", q=64),
                in1=cdec_tab[:].rearrange("p (h z2) -> p h z2", z2=NCH)
                    [:, gq * 16:(gq + 1) * 16, z:z + 1].to_broadcast([64, 16, 64]),
                op=OP.mult))(z, gq), deps=[mmC])
            ycf = b.step("vector", (lambda gsl=gsl: lambda e: e.tensor_tensor(
                out=Ytc[:, gsl], in0=uevY[:], in1=Ytc[:, gsl], op=OP.add))(),
                deps=[t2c])
            last_g = ycf
        if t_Yf is not None:
            b.step("gpsimd", (lambda z: lambda e: e.dma_start(
                out=g["t_Yf"][:, z * DI:(z + 1) * DI], in_=Ytc[:, :]))(z),
                deps=[last_g], dma=True)
        for jb in range(4):
            cst = b.step("vector", (lambda jb: lambda e: e.tensor_copy(
                out=SCR[0:64, 3072:4096], in_=Ytc[:, jb * 1024:(jb + 1) * 1024]))(jb),
                deps=[last_g])

            def f_trY(e, jb=jb):
                last = None
                for i in range(8):
                    last = e.transpose(psT[:, i * 64:(i + 1) * 64],
                                       SCR[0:64, 3072 + i * 128:3072 + (i + 1) * 128],
                                       ident[0:64, 0:64])
                return last
            mmT = b.step("tensor", f_trY, deps=[cst])
            gt = b.step("vector", (lambda jb: lambda e: e.tensor_tensor(
                out=gst[:, :], in0=psT[:, :], in1=szt[:, jb * 512:(jb + 1) * 512],
                op=OP.mult))(jb), deps=[mmT, szl])
            gp = b.step("gpsimd", (lambda z, jb, zl=zl: lambda e: e.dma_start(
                out=g["gated_dram"][jb * 1024:(jb + 1) * 1024, zl]
                    .rearrange("(i p) l -> p i l", p=128),
                in_=gst[:].rearrange("p (i l) -> p i l", l=64)))(z, jb),
                deps=[gt], dma=True)
            last_g = gp
    if t_gated is not None:
        for jj in range(32):
            b.step("gpsimd", (lambda jj: lambda e: e.dma_start(
                out=g["t_gated"][tsl(jj, 128), :],
                in_=g["gated_dram"][tsl(jj, 128), :]))(jj), deps=[last_g], dma=True)

    bar_D = b.barrier()
    if upto == 'D':
        b.finalize(); b.close(); return nc, g

    for t in [gst, S_in, r_run, SCR, rb, BzT,
              identb, Mm_sb, d_tab, cdec_tab, lam_tab, cdecl, zer16, lamv, lamlog,
              dtpos, dtT, decT, sdo_lh, cs_hl, cs_lh, aT_all, Ytc, XB3, XB2, XB1, Gt, Dm]:
        b.release(t)
    for t in [psS, psT, psE, psB, psA]:
        b.release(t)
    for t in [dt_bf, a_sc, dtneg, C_fm, B_tok, B_fm, AX]:
        b.release(t)

    po = [b.psum("po%d" % i, [128, 512]) for i in range(8)]
    h1b = b.sbuf("h1b", [128, 16 * 1024], BF16, side="right")
    wstE = b.sbuf("wstE", [128, 1024], side="right")
    wsbE = b.sbuf("wsbE", [128, 1024], BF16, side="right")
    gstE = b.sbuf("gstE", [128, 2048], BF16, side="right")
    h_pre = b.sbuf("h_pre", [128, 16 * 1024], side="right")
    xrel = b.sbuf("xrel", [128, 1024], side="right")
    sqb = b.sbuf("sqb", [128, 1024], BF16)
    rmu = b.sbuf("rmu", [1, 1024])
    rvar = b.sbuf("rvar", [1, 1024])
    mub = b.sbuf("mub", [128, 1024])
    ivb = b.sbuf("ivb", [128, 1024])
    rtmp = b.sbuf("rtmp", [1, 1024])

    t_h1 = tap("t_h1", [D, T])

    last_ev = None
    for slab in range(4):
        last_mm = None
        for k in range(32):
            par = k % 2
            wl = b.step("sync", (lambda slab, k, par: lambda e: e.dma_start(
                out=wstE[:, par * 512:(par + 1) * 512],
                in_=g["w_out"][k * 128:(k + 1) * 128, tsl(slab, 512)]))(slab, k, par),
                deps=([last_mm] if last_mm else ([last_ev] if last_ev else [])), dma=True)
            wc = b.step("vector", (lambda par: lambda e: e.tensor_copy(
                out=wsbE[:, par * 512:(par + 1) * 512],
                in_=wstE[:, par * 512:(par + 1) * 512]))(par), deps=[wl])
            gl = b.step("gpsimd", (lambda k, par: lambda e: e.dma_start(
                out=gstE[:, par * 1024:(par + 1) * 1024],
                in_=g["gated_dram"][k * 128:(k + 1) * 128, :]))(k, par),
                deps=[wl], dma=True)

            def f_mm(e, slab=slab, k=k, par=par):
                last = None
                for oc in range(4):
                    for th in range(2):
                        last = e.matmul(po[oc * 2 + th][:, :],
                                        wsbE[:, par * 512 + oc * 128:par * 512 + oc * 128 + 128],
                                        gstE[:, par * 1024 + th * 512:par * 1024 + th * 512 + 512],
                                        start=(k == 0), stop=(k == 31))
                return last
            deps = [wc, gl]
            if last_mm is not None:
                deps.append(last_mm)
            if k == 0 and last_ev is not None:
                deps.append(last_ev)
            last_mm = b.step("tensor", f_mm, deps=deps)
        for oc in range(4):
            dt_i = slab * 4 + oc
            xdeps = [last_mm]
            if last_ev is not None:
                xdeps.append(last_ev)
            xld = b.step("sync", (lambda dt_i: lambda e: e.dma_start(
                out=xrel[:, :], in_=g["x_in"][tsl(dt_i, 128), 4:4 + T]))(dt_i),
                deps=xdeps, dma=True)
            for th in range(2):
                ev = b.step("vector", (lambda dt_i, oc, th: lambda e: e.tensor_tensor(
                    out=h_pre[:, dt_i * 1024 + th * 512:dt_i * 1024 + th * 512 + 512],
                    in0=po[oc * 2 + th][:, :], in1=xrel[:, th * 512:(th + 1) * 512],
                    op=OP.add))(dt_i, oc, th), deps=[xld, last_mm])
                last_ev = ev

    hb_c = None
    for i in range(16):
        hb_c = b.step("vector", (lambda i: lambda e: e.tensor_copy(
            out=h1b[:, tsl(i, 1024)], in_=h_pre[:, tsl(i, 1024)]))(i), deps=[last_ev])
    for t in reversed(po):
        b.release(t)
    psq = b.psum("psq", [1, 1024])
    ps1 = b.psum("ps1", [1, 1024])
    psb2 = b.psum("psb2", [128, 512])
    ones1c = b.sbuf("ones1c", [128, 1], BF16)
    ones1f = b.sbuf("ones1f", [128, 1])
    c_o1 = b.step("vector", lambda e: e.memset(ones1c[:], 1.0), deps=[])
    c_o1f = b.step("vector", lambda e: e.memset(ones1f[:], 1.0), deps=[])

    def ln_stats(src_bf, dep, nfeat=D):
        mm_prev = None
        for i in range(16):
            c2 = b.step("scalar", (lambda i: lambda e: e.activation(
                out=sqb[:], in_=src_bf[:, tsl(i, 1024)], func=AF.Square))(i),
                deps=[dep] + ([mm_prev] if mm_prev else []))
            mm_prev = b.step("tensor", (lambda i: lambda e: [
                e.matmul(psq[0:1, 0:512], ones1c[:], sqb[:, 0:512],
                         start=(i == 0), stop=(i == 15)),
                e.matmul(psq[0:1, 512:1024], ones1c[:], sqb[:, 512:1024],
                         start=(i == 0), stop=(i == 15)),
                e.matmul(ps1[0:1, 0:512], ones1c[:], src_bf[:, i * 1024:i * 1024 + 512],
                         start=(i == 0), stop=(i == 15)),
                e.matmul(ps1[0:1, 512:1024], ones1c[:], src_bf[:, i * 1024 + 512:(i + 1) * 1024],
                         start=(i == 0), stop=(i == 15))][-1])(i),
                deps=[c2, c_o1])
        s1 = b.step("vector", lambda e: e.tensor_scalar_mul(rmu[:], ps1[0:1, :], 1.0 / nfeat),
                    deps=[mm_prev])
        s2 = b.step("vector", lambda e: e.tensor_scalar_mul(rvar[:], psq[0:1, :], 1.0 / nfeat),
                    deps=[s1])
        s3 = b.step("vector", lambda e: e.tensor_tensor(out=rtmp[:], in0=rmu[:], in1=rmu[:],
                                                        op=OP.mult), deps=[s2])
        s4 = b.step("vector", lambda e: e.tensor_tensor(out=rvar[:], in0=rvar[:], in1=rtmp[:],
                                                        op=OP.subtract), deps=[s3])
        s5 = b.step("scalar", lambda e: e.activation(out=rtmp[:], in_=rvar[:], func=AF.Ln,
                                                     bias=epsr[:], scale=1.0), deps=[s4])
        s6 = b.step("scalar", lambda e: e.activation(out=rvar[:], in_=rtmp[:], func=AF.Exp,
                                                     scale=mhalfr[:]), deps=[s5])
        bdep = [s6]
        for (row, dst) in ((rmu, mub), (rvar, ivb)):
            for j in range(2):
                mmx = b.step("tensor", (lambda row, j: lambda e: e.matmul(
                    psb2[:, :], onesr[:], row[:, j * 512:(j + 1) * 512],
                    start=True, stop=True))(row, j), deps=bdep)
                evx = b.step("vector", (lambda dst, j: lambda e: e.tensor_copy(
                    out=dst[:, j * 512:(j + 1) * 512], in_=psb2[:, :]))(dst, j), deps=[mmx])
                bdep = [evx]
        return bdep[0]

    lnst = ln_stats(h1b, hb_c)
    last_ap = lnst
    t_h1 = g.get("t_h1")
    for i in range(16):
        a1 = b.step("vector", (lambda i: lambda e: e.tensor_tensor(
            out=h_pre[:, tsl(i, 1024)], in0=h_pre[:, tsl(i, 1024)], in1=mub[:],
            op=OP.subtract))(i), deps=[last_ap])
        a2 = b.step("vector", (lambda i: lambda e: e.tensor_tensor(
            out=h_pre[:, tsl(i, 1024)], in0=h_pre[:, tsl(i, 1024)], in1=ivb[:],
            op=OP.mult))(i), deps=[a1])
        a3 = b.step("vector", (lambda i: lambda e: e.tensor_copy(
            out=h1b[:, tsl(i, 1024)], in_=h_pre[:, tsl(i, 1024)]))(i), deps=[a2])
        a4 = b.step("gpsimd", (lambda i: lambda e: e.dma_start(
            out=g["h1_dram"][tsl(i, 128), :], in_=h_pre[:, tsl(i, 1024)]))(i),
            deps=[a3], dma=True)
        if t_h1 is not None:
            b.step("gpsimd", (lambda i: lambda e: e.dma_start(
                out=g["t_h1"][tsl(i, 128), :], in_=h_pre[:, tsl(i, 1024)]))(i),
                deps=[a3], dma=True)
        last_ap = a4

    bar_E = b.barrier()
    if upto == 'E':
        b.finalize(); b.close(); return nc, g

    for t in [xrel, h_pre, gstE, wsbE, wstE]:
        b.release(t)
    b.release(psb2)
    b.release(ps1)
    b.release(psq)
    A1 = b.sbuf("A1", [128, 8 * NH], BF16)
    invb2 = b.sbuf("invb2", [128, 1])
    czb2 = b.sbuf("czb2", [128, 1])
    czr = b.sbuf("czr", [1, 2])
    zst2 = b.sbuf("zst2", [128, 256], side="right")
    sabs = b.sbuf("sabs", [128, 1], side="right")
    srow = b.sbuf("srow", [1, 4], side="right")
    invb = b.sbuf("invb", [128, 1], side="right")
    czrow = b.sbuf("czrow", [1, 4], side="right")
    czb = b.sbuf("czb", [128, 1], side="right")
    bsrow = b.sbuf("bsrow", [1, 256], side="right")
    wstF = b.sbuf("wstF", [128, 2048], side="right")
    sgn1 = b.sbuf("sgn1", [128, 2048], BF16, side="right")
    tern = b.sbuf("tern", [128, 4096], BF16, side="right")
    pz = [b.psum("pz%d" % i, [128, 512]) for i in range(8)]

    t_s = tap("t_s", [1, 4])
    t_z = tap("t_z", [T, NH])
    t_thr = tap("t_thr", [128, 8 * 4])

    b._accs = []

    def scale_pass3(wname, rows, cols, dep0, sidx):
        accv = b.sbuf("acc_" + wname, [128, 1], side="right")
        b._accs.append(accv)
        sa1 = b.step("vector", lambda e: e.memset(accv[:], 0.0), deps=[dep0])
        prev = [sa1]
        for rb_ in range(rows // 128):
            for cb_ in range(cols // 2048):
                ld = b.step("sync", (lambda rb_, cb_: lambda e: e.dma_start(
                    out=wstF[:, :],
                    in_=g[wname][rb_ * 128:(rb_ + 1) * 128,
                                 cb_ * 2048:(cb_ + 1) * 2048]))(rb_, cb_),
                    deps=prev, dma=True)
                ab = b.step("scalar", lambda e: e.activation(
                    out=wstF[:, :], in_=wstF[:, :], func=AF.Abs, accum_out=sabs[:]),
                    deps=[ld])
                ac = b.step("vector", lambda e: e.tensor_tensor(
                    out=accv[:], in0=accv[:], in1=sabs[:], op=OP.add), deps=[ab])
                prev = [ac, ab]
        mmt = b.step("tensor", lambda e: e.matmul(
            pz[0][0:1, 0:1], ones1f[:], accv[:], start=True, stop=True),
            deps=prev + [c_o1f])
        evt = b.step("vector", (lambda sidx: lambda e: e.tensor_scalar(
            out=srow[0:1, sidx:sidx + 1], in0=pz[0][0:1, 0:1],
            scalar1=1.0 / (rows * cols), scalar2=None, op0=OP.mult))(sidx), deps=[mmt])
        return evt

    sp1 = b.step("vector", lambda e: e.memset(srow[:], 0.0), deps=[])
    e1 = scale_pass3("w_syn", D, NH, sp1, 0)
    e2 = scale_pass3("w_o2", NH, D, e1, 1)
    sprep = b.step("vector", lambda e: e.tensor_scalar(
        out=czrow[0:1, 0:2], in0=srow[0:1, 0:2], scalar1=0.5, scalar2=None,
        op0=OP.mult), deps=[e2])
    s_eps = b.step("vector", lambda e: e.tensor_scalar(
        out=srow[0:1, 2:4], in0=srow[0:1, 0:2], scalar1=1e-8, scalar2=None, op0=OP.add),
        deps=[sprep])
    s_rec = b.step("vector", lambda e: e.reciprocal(out=srow[0:1, 2:4], in_=srow[0:1, 2:4]),
                   deps=[s_eps])
    if t_s is not None:
        b.step("gpsimd", lambda e: e.dma_start(out=g["t_s"][:, :], in_=srow[:, :]),
               deps=[s_rec], dma=True)
    s_rc2 = b.step("vector", lambda e: e.reciprocal(out=czr[:, 0:2], in_=czrow[0:1, 0:2]),
                   deps=[sprep, s_rec])
    bc_pairs = [(srow[0:1, 2:3], invb), (srow[0:1, 3:4], invb2),
                (czrow[0:1, 0:1], czb), (czrow[0:1, 1:2], czb2)]
    bdep = [s_rc2]
    for (rw, dst) in bc_pairs:
        mmi = b.step("tensor", (lambda rw: lambda e: e.matmul(
            pz[1][:, 0:1], onesr[:], rw, start=True, stop=True))(rw), deps=bdep)
        evi = b.step("vector", (lambda dst: lambda e: e.tensor_copy(
            out=dst[:], in_=pz[1][:, 0:1]))(dst), deps=[mmi])
        bdep = [evi]
    sc_done = bdep[0]

    last_ev = sc_done
    last_st = None
    last_mmz = None
    for og in range(32):
        halves = []
        for hh in range(2):
            wdeps = [last_ev]
            if last_mmz is not None:
                wdeps.append(last_mmz)
            wl = b.step("sync", (lambda og, hh: lambda e: e.dma_start(
                out=wstF[:, :].rearrange("p (k f) -> p k f", k=8),
                in_=g["w_syn"][hh * 1024:(hh + 1) * 1024, tsl(og, 256)]
                .rearrange("(k p) f -> p k f", p=128)))(og, hh),
                deps=wdeps, dma=True)
            sg = b.step("scalar", (lambda hh: lambda e: [
                e.activation(out=sgn1[:, :], in_=wstF[:, :], func=AF.Sign,
                             bias=halfn[:], scale=invb[:]),
                e.activation(out=tern[:, hh * 2048:(hh + 1) * 2048], in_=wstF[:, :],
                             func=AF.Sign, bias=halfp[:], scale=invb[:])][-1])(hh),
                deps=[wl, sc_done])
            tn = b.step("vector", (lambda hh: lambda e: e.tensor_tensor(
                out=tern[:, hh * 2048:(hh + 1) * 2048], in0=sgn1[:, :],
                in1=tern[:, hh * 2048:(hh + 1) * 2048], op=OP.add))(hh), deps=[sg])
            halves.append(tn)
            last_ev = tn
        bl = b.step("sync", (lambda og: lambda e: e.dma_start(
            out=bsrow[0:1, 0:256], in_=g["b_syn"].rearrange("(one n) -> one n", one=1)
            [:, tsl(og, 256)]))(og), deps=[last_ev], dma=True)
        bs = b.step("vector", lambda e: e.tensor_scalar(
            out=bsrow[0:1, 0:256], in0=bsrow[0:1, 0:256], scalar1=czr[0:1, 0:1],
            scalar2=None, op0=OP.mult), deps=[bl, s_rc2])

        def f_mm(e, og=og):
            last = None
            for tt in range(8):
                for k in range(16):
                    kk, khh = k % 8, k // 8
                    last = e.matmul(pz[tt][:, 0:256],
                                    h1b[:, k * 1024 + tt * 128:k * 1024 + tt * 128 + 128],
                                    tern[:, khh * 2048 + kk * 256:khh * 2048 + (kk + 1) * 256],
                                    start=(k == 0), stop=False)
                last = e.matmul(pz[tt][:, 0:256], onesr[0:1, :], bsrow[0:1, 0:256],
                                start=False, stop=True)
            return last
        mm_deps = halves + [bs]
        if last_st is not None:
            mm_deps.append(last_st)
        mmz = b.step("tensor", f_mm, deps=mm_deps)
        last_mmz = mmz
        for tt in range(8):
            evz = b.step("vector", (lambda tt, og: lambda e: [
                e.tensor_scalar(out=zst2[:, 0:256], in0=pz[tt][:, 0:256], scalar1=czb[:],
                                scalar2=None, op0=OP.mult),
                e.tensor_copy(out=A1[:, tt * NH + og * 256:tt * NH + (og + 1) * 256],
                              in_=zst2[:, 0:256])][-1])(tt, og),
                deps=[mmz] + ([last_st] if last_st else []))
            stz = b.step("gpsimd", (lambda tt, og: lambda e: e.dma_start(
                out=g["z_dram"][tt * 128:(tt + 1) * 128, tsl(og, 256)],
                in_=zst2[:, 0:256]))(tt, og), deps=[evz], dma=True)
            last_st = stz
    if t_z is not None:
        for tt in range(8):
            b.step("gpsimd", (lambda tt: lambda e: e.dma_start(
                out=g["t_z"][tt * 128:(tt + 1) * 128, :],
                in_=g["z_dram"][tt * 128:(tt + 1) * 128, :]))(tt), deps=[last_st], dma=True)

    bar_F1 = b.barrier()
    if upto == 'F1':
        b.finalize(); b.close(); return nc, g

    for t in [b._accs[1], b._accs[0], tern, sgn1, wstF, bsrow, czb, czrow, invb,
              srow, sabs, zst2, h1b]:
        b.release(t)
    msk = b.sbuf("msk", [128, NH], mybir.dt.float8e4, side="right")
    lo_t = b.sbuf("lo_t", [128, 8], side="right")
    hi_t = b.sbuf("hi_t", [128, 8], side="right")
    mid_t = b.sbuf("mid_t", [128, 8], side="right")
    cnt_t = b.sbuf("cnt_t", [128, 8], side="right")
    cge_t = b.sbuf("cge_t", [128, 8], mybir.dt.uint8, side="right")
    Tuse = b.sbuf("Tuse", [128, 8], side="right")
    Tst = b.sbuf("Tst", [128, 8], side="right")
    c2_t = b.sbuf("c2_t", [128, 8], side="right")
    jr_t = b.sbuf("jr_t", [128, 8], side="right")
    mr_t = b.sbuf("mr_t", [128, 8], mybir.dt.uint8, side="right")
    iot = b.sbuf("iot", [128, 32], side="right")
    zfs = b.sbuf("zfs", [128, 2048], side="right")
    zneg = b.sbuf("zneg", [128, 2048], side="right")
    hidb = b.sbuf("hidb", [128, 2048], BF16, side="right")
    cand2 = b.sbuf("cand2", [128, 8 * 512], side="right")
    srt32 = b.sbuf("srt32", [128, 8 * 32], side="right")

    c_io = b.step("sync", lambda e: e.dma_start(out=iot[:], in_=g["iota32"][:, :]), dma=True)
    sel0 = b.step("vector", lambda e: e.memset(lo_t[:], 0.0), deps=[])
    prevh = sel0
    for tt in range(8):
        prevh = b.step("vector", (lambda tt: lambda e: e.reduce_max(
            out=hi_t[:, tt:tt + 1], in_=A1[:, tt * NH:(tt + 1) * NH],
            axis=AXL.X))(tt), deps=[prevh])
    selh = b.step("vector", lambda e: e.tensor_scalar(
        out=hi_t[:], in0=hi_t[:], scalar1=1.0, scalar2=None, op0=OP.add), deps=[prevh])

    ITERS = 14
    prev_it = [selh, sel0]
    for it in range(ITERS):
        m1a = b.step("vector", lambda e: e.tensor_tensor(
            out=mid_t[:], in0=lo_t[:], in1=hi_t[:], op=OP.add), deps=prev_it)
        m1 = b.step("vector", lambda e: e.tensor_scalar_mul(mid_t[:], mid_t[:], 0.5),
                    deps=[m1a])
        cts = []
        for tt in range(8):
            ct = b.step("vector", (lambda tt: lambda e: e.tensor_scalar(
                out=msk[:, :], in0=A1[:, tt * NH:(tt + 1) * NH],
                scalar1=mid_t[:, tt:tt + 1], scalar2=None,
                op0=OP.is_ge, op1=OP.add, accum_out=cnt_t[:, tt:tt + 1]))(tt), deps=[m1])
            cts.append(ct)
        u1 = b.step("vector", lambda e: e.tensor_scalar(
            out=cge_t[:], in0=cnt_t[:], scalar1=float(KTOP) - 0.5,
            scalar2=None, op0=OP.is_ge), deps=cts)
        u2 = b.step("vector", lambda e: e.select(
            out=lo_t[:], mask=cge_t[:], on_true=mid_t[:], on_false=lo_t[:]), deps=[u1])
        up = b.step("vector", lambda e: e.select(
            out=hi_t[:], mask=cge_t[:], on_true=hi_t[:], on_false=mid_t[:]), deps=[u2])
        prev_it = [up]

    selp = prev_it[0]
    c20 = b.step("vector", lambda e: e.memset(c2_t[:], 0.0), deps=[selp])
    prevs = c20
    for tt in range(8):
        for cc in range(4):
            zl2 = b.step("gpsimd", (lambda tt, cc: lambda e: e.dma_start(
                out=zfs[:, :], in_=g["z_dram"][tt * 128:(tt + 1) * 128, tsl(cc, 2048)]))(tt, cc),
                deps=[prevs], dma=True)
            cpa = b.step("vector", (lambda tt: lambda e: e.tensor_scalar(
                out=msk[:, 0:2048], in0=zfs[:, :],
                scalar1=hi_t[:, tt:tt + 1], scalar2=None,
                op0=OP.is_ge, op1=OP.add, accum_out=cnt_t[:, tt:tt + 1]))(tt), deps=[zl2])
            cpb = b.step("vector", (lambda tt: lambda e: e.tensor_tensor(
                out=c2_t[:, tt:tt + 1], in0=c2_t[:, tt:tt + 1],
                in1=cnt_t[:, tt:tt + 1], op=OP.add))(tt), deps=[cpa])
            cp = b.step("vector", (lambda tt: lambda e: e.tensor_scalar(
                out=zneg[:, :], in0=zfs[:, :],
                scalar1=hi_t[:, tt:tt + 1], scalar2=-BIG,
                op0=OP.is_ge, op1=OP.mult))(tt), deps=[cpb])
            cp2 = b.step("vector", lambda e: e.tensor_tensor(
                out=zfs[:, :], in0=zfs[:, :], in1=zneg[:, :], op=OP.add), deps=[cp])

            def f_m8(e, tt=tt, cc=cc):
                last = None
                for w in range(16):
                    last = e.max(out=cand2[:, tt * 512 + cc * 128 + w * 8:
                                          tt * 512 + cc * 128 + w * 8 + 8],
                                 in_=zfs[:, w * 128:(w + 1) * 128])
                return last
            prevs = b.step("vector", f_m8, deps=[cp2])

    rnd_prev = prevs
    for rnd in range(4):
        for tt in range(8):
            mx = b.step("vector", (lambda rnd, tt: lambda e: e.max(
                out=srt32[:, tt * 32 + rnd * 8:tt * 32 + rnd * 8 + 8],
                in_=cand2[:, tsl(tt, 512)]))(rnd, tt), deps=[rnd_prev])
            if rnd < 3:
                rma = b.step("vector", (lambda rnd, tt: lambda e: e.tensor_scalar(
                    out=zfs[:, 0:512], in0=cand2[:, tsl(tt, 512)],
                    scalar1=srt32[:, tt * 32 + rnd * 8 + 7:tt * 32 + rnd * 8 + 8],
                    scalar2=-BIG, op0=OP.is_ge, op1=OP.mult))(rnd, tt), deps=[mx])
                rm = b.step("vector", (lambda rnd, tt: lambda e: e.tensor_tensor(
                    out=cand2[:, tsl(tt, 512)], in0=cand2[:, tsl(tt, 512)],
                    in1=zfs[:, 0:512], op=OP.add))(rnd, tt), deps=[rma])
                rnd_prev = rm
            else:
                rnd_prev = mx
    r1 = b.step("vector", lambda e: e.tensor_scalar(
        out=jr_t[:], in0=c2_t[:], scalar1=-1.0, scalar2=float(KTOP - 1),
        op0=OP.mult, op1=OP.add), deps=[rnd_prev])
    r2 = None
    for tt in range(8):
        r2a = b.step("vector", (lambda tt: lambda e: e.tensor_scalar(
            out=zfs[:, 0:32], in0=iot[:], scalar1=jr_t[:, tt:tt + 1],
            scalar2=None, op0=OP.is_equal))(tt), deps=[r1, c_io] + ([r2] if r2 else []))
        r2b = b.step("vector", (lambda tt: lambda e: e.tensor_tensor(
            out=zfs[:, 32:64], in0=zfs[:, 0:32],
            in1=srt32[:, tsl(tt, 32)], op=OP.mult))(tt), deps=[r2a])
        r2 = b.step("vector", (lambda tt: lambda e: e.reduce_sum(
            out=Tst[:, tt:tt + 1], in_=zfs[:, 32:64], axis=AXL.X))(tt), deps=[r2b])
    r3a = b.step("vector", lambda e: e.tensor_scalar(
        out=mr_t[:], in0=jr_t[:], scalar1=-0.5, scalar2=None, op0=OP.is_ge), deps=[r2])
    r3b = b.step("vector", lambda e: e.tensor_scalar(
        out=cge_t[:], in0=jr_t[:], scalar1=31.5, scalar2=None, op0=OP.is_le), deps=[r3a])
    r3c = b.step("vector", lambda e: e.tensor_tensor(
        out=mr_t[:], in0=mr_t[:], in1=cge_t[:], op=OP.mult), deps=[r3b])
    r3 = b.step("vector", lambda e: e.select(
        out=Tuse[:], mask=mr_t[:], on_true=Tst[:], on_false=hi_t[:]), deps=[r3c])
    if t_thr is not None:
        b.step("gpsimd", lambda e: e.dma_start(out=g["t_thr"][:, 0:8], in_=Tuse[:, :]),
               deps=[r3], dma=True)
        b.step("gpsimd", lambda e: e.dma_start(out=g["t_thr"][:, 8:16], in_=c2_t[:, :]),
               deps=[r3], dma=True)
        b.step("gpsimd", lambda e: e.dma_start(out=g["t_thr"][:, 16:24], in_=lo_t[:, :]),
               deps=[r3], dma=True)
        b.step("gpsimd", lambda e: e.dma_start(out=g["t_thr"][:, 24:32], in_=hi_t[:, :]),
               deps=[r3], dma=True)

    b.release(srt32)
    b.release(cand2)
    hid_prev = r3
    for tt in range(8):
        for cc in range(4):
            zl3 = b.step("gpsimd", (lambda tt, cc: lambda e: e.dma_start(
                out=zfs[:, :], in_=g["z_dram"][tt * 128:(tt + 1) * 128, tsl(cc, 2048)]))(tt, cc),
                deps=[hid_prev], dma=True)
            hm = b.step("vector", (lambda tt: lambda e: e.scalar_tensor_tensor(
                out=zneg[:, :], in0=zfs[:, :], scalar=Tuse[:, tt:tt + 1], in1=zfs[:, :],
                op0=OP.is_ge, op1=OP.mult))(tt), deps=[zl3, r3])
            hr = b.step("scalar", lambda e: e.activation(out=zneg[:, :], in_=zneg[:, :],
                                                         func=AF.Relu), deps=[hm])

            def f_trh(e, tt=tt, cc=cc):
                insts = []
                for bb in range(16):
                    pzi = pz[bb % 4]
                    insts.append(e.transpose(pzi[:, (bb // 4) * 128:(bb // 4) * 128 + 128],
                                             zneg[:, bb * 128:(bb + 1) * 128], ident[:]))
                return insts[-1]
            trh = b.step("tensor", f_trh, deps=[hr, hid_prev])
            def f_evh(e, tt=tt, cc=cc):
                last = None
                for bb in range(16):
                    kk = cc * 16 + bb
                    pzi = pz[bb % 4]
                    last = e.tensor_copy(
                        out=A1[:, kk * 1024 + tt * 128:kk * 1024 + tt * 128 + 128],
                        in_=pzi[:, (bb // 4) * 128:(bb // 4) * 128 + 128])
                return last
            hid_prev = b.step("vector", f_evh, deps=[trh])

    bar_F2 = b.barrier()
    if upto == 'F':
        b.finalize(); b.close(); return nc, g

    for t in [hidb, zneg, zfs, iot, mr_t, jr_t, c2_t, Tst, Tuse, cge_t, cnt_t,
              mid_t, hi_t, lo_t, msk]:
        b.release(t)
    v2b = b.sbuf("v2b", [128, 16 * 1024], BF16)
    h1rel = b.sbuf("h1rel", [128, 1024], side="right")
    wstG = b.sbuf("wstG", [128, 1024], side="right")
    ternG = b.sbuf("ternG", [128, 512], BF16, side="right")
    sgnG = b.sbuf("sgnG", [128, 512], BF16, side="right")
    borow = b.sbuf("borow", [1, 512], side="right")
    ones512 = b.sbuf("ones512", [1, 512], side="right")
    ysto = b.sbuf("ysto", [128, 1024], side="right")
    lnscr = b.sbuf("lnscr", [128, 1024], side="right")
    c_o512 = b.step("vector", lambda e: e.memset(ones512[:], 1.0), deps=[])

    t_v2 = tap("t_v2", [D, T])

    last_evG = None
    for slab in range(4):
        bl = b.step("sync", (lambda slab: lambda e: e.dma_start(
            out=borow[0:1, :], in_=g["b_out"].rearrange("(one n) -> one n", one=1)
            [:, tsl(slab, 512)]))(slab), deps=[], dma=True)
        bs = b.step("vector", lambda e: e.tensor_scalar(
            out=borow[0:1, :], in0=borow[0:1, :], scalar1=czr[0:1, 1:2],
            scalar2=None, op0=OP.mult), deps=[bl, s_rc2])
        last_mm = None
        for k in range(64):
            wl = b.step("sync", (lambda slab, k: lambda e: e.dma_start(
                out=wstG[:, 0:512],
                in_=g["w_o2"][k * 128:(k + 1) * 128, tsl(slab, 512)]))(slab, k),
                deps=[last_mm] if last_mm else ([last_evG] if last_evG else []), dma=True)
            sg = b.step("scalar", lambda e: [
                e.activation(out=sgnG[:, :], in_=wstG[:, 0:512], func=AF.Sign,
                             bias=halfn[:], scale=invb2[:]),
                e.activation(out=ternG[:, :], in_=wstG[:, 0:512], func=AF.Sign,
                             bias=halfp[:], scale=invb2[:])][-1], deps=[wl])
            tn = b.step("vector", lambda e: e.tensor_tensor(
                out=ternG[:, :], in0=sgnG[:, :], in1=ternG[:, :], op=OP.add), deps=[sg])

            def f_mm(e, slab=slab, k=k):
                last = None
                for oc in range(4):
                    for th in range(2):
                        last = e.matmul(pz[oc * 2 + th][:, :],
                                        ternG[:, oc * 128:(oc + 1) * 128],
                                        A1[:, k * 1024 + th * 512:k * 1024 + th * 512 + 512],
                                        start=(k == 0), stop=False)
                if k == 63:
                    for oc in range(4):
                        for th in range(2):
                            last = e.matmul(pz[oc * 2 + th][:, :],
                                            borow[0:1, oc * 128:(oc + 1) * 128],
                                            ones512[0:1, 0:512],
                                            start=False, stop=True)
                return last
            deps = [tn, bs, c_o512]
            if last_mm is not None:
                deps.append(last_mm)
            if k == 0 and last_evG is not None:
                deps.append(last_evG)
            last_mm = b.step("tensor", f_mm, deps=deps)
        for oc in range(4):
            dt_i = slab * 4 + oc
            hdeps = [last_mm]
            if last_evG is not None:
                hdeps.append(last_evG)
            hl = b.step("gpsimd", (lambda dt_i: lambda e: e.dma_start(
                out=h1rel[:, :], in_=g["h1_dram"][tsl(dt_i, 128), :]))(dt_i),
                deps=hdeps, dma=True)
            for th in range(2):
                ev = b.step("vector", (lambda dt_i, oc, th: lambda e: e.scalar_tensor_tensor(
                    out=v2b[:, dt_i * 1024 + th * 512:dt_i * 1024 + th * 512 + 512],
                    in0=pz[oc * 2 + th][:, :], scalar=czb2[:],
                    in1=h1rel[:, th * 512:(th + 1) * 512],
                    op0=OP.mult, op1=OP.add))(dt_i, oc, th), deps=[hl, last_mm])
                last_evG = ev
    if t_v2 is not None:
        for i in range(16):
            b.step("gpsimd", (lambda i: lambda e: e.dma_start(
                out=g["t_v2"][tsl(i, 128), :], in_=v2b[:, tsl(i, 1024)]))(i),
                deps=[last_evG], dma=True)

    for t in reversed(pz):
        b.release(t)
    psq2 = b.psum("psq2", [1, 1024])
    ps12 = b.psum("ps12", [1, 1024])
    psb22 = b.psum("psb22", [128, 512])

    def ln_stats2(src_bf, dep):
        mm_prev = None
        for i in range(16):
            c2 = b.step("scalar", (lambda i: lambda e: e.activation(
                out=sqb[:], in_=src_bf[:, tsl(i, 1024)], func=AF.Square))(i),
                deps=[dep] + ([mm_prev] if mm_prev else []))
            mm_prev = b.step("tensor", (lambda i: lambda e: [
                e.matmul(psq2[0:1, 0:512], ones1c[:], sqb[:, 0:512],
                         start=(i == 0), stop=(i == 15)),
                e.matmul(psq2[0:1, 512:1024], ones1c[:], sqb[:, 512:1024],
                         start=(i == 0), stop=(i == 15)),
                e.matmul(ps12[0:1, 0:512], ones1c[:], src_bf[:, i * 1024:i * 1024 + 512],
                         start=(i == 0), stop=(i == 15)),
                e.matmul(ps12[0:1, 512:1024], ones1c[:],
                         src_bf[:, i * 1024 + 512:(i + 1) * 1024],
                         start=(i == 0), stop=(i == 15))][-1])(i),
                deps=[c2, c_o1])
        s1 = b.step("vector", lambda e: e.tensor_scalar_mul(rmu[:], ps12[0:1, :], 1.0 / D),
                    deps=[mm_prev])
        s2 = b.step("vector", lambda e: e.tensor_scalar_mul(rvar[:], psq2[0:1, :], 1.0 / D),
                    deps=[s1])
        s3 = b.step("vector", lambda e: e.tensor_tensor(out=rtmp[:], in0=rmu[:], in1=rmu[:],
                                                        op=OP.mult), deps=[s2])
        s4 = b.step("vector", lambda e: e.tensor_tensor(out=rvar[:], in0=rvar[:], in1=rtmp[:],
                                                        op=OP.subtract), deps=[s3])
        s5 = b.step("scalar", lambda e: e.activation(out=rtmp[:], in_=rvar[:], func=AF.Ln,
                                                     bias=epsr[:], scale=1.0), deps=[s4])
        s6 = b.step("scalar", lambda e: e.activation(out=rvar[:], in_=rtmp[:], func=AF.Exp,
                                                     scale=mhalfr[:]), deps=[s5])
        bdep = [s6]
        for (row, dst) in ((rmu, mub), (rvar, ivb)):
            for j in range(2):
                mmx = b.step("tensor", (lambda row, j: lambda e: e.matmul(
                    psb22[:, :], onesr[:], row[:, j * 512:(j + 1) * 512],
                    start=True, stop=True))(row, j), deps=bdep)
                evx = b.step("vector", (lambda dst, j: lambda e: e.tensor_copy(
                    out=dst[:, j * 512:(j + 1) * 512], in_=psb22[:, :]))(dst, j), deps=[mmx])
                bdep = [evx]
        return bdep[0]

    ln2 = ln_stats2(v2b, last_evG)
    lastg = ln2
    for i in range(16):
        hl = b.step("gpsimd", (lambda i: lambda e: e.dma_start(
            out=h1rel[:, :], in_=g["h1_dram"][tsl(i, 128), :]))(i), deps=[lastg], dma=True)
        g1 = b.step("vector", (lambda i: lambda e: [
            e.tensor_scalar(out=lnscr[:, :], in0=ivb[:, :], scalar1=gam[:, i:i + 1],
                            scalar2=None, op0=OP.mult),
            e.tensor_tensor(out=v2b[:, tsl(i, 1024)], in0=v2b[:, tsl(i, 1024)], in1=mub[:],
                            op=OP.subtract),
            e.tensor_tensor(out=v2b[:, tsl(i, 1024)], in0=v2b[:, tsl(i, 1024)], in1=lnscr[:],
                            op=OP.mult),
            e.tensor_scalar(out=v2b[:, tsl(i, 1024)], in0=v2b[:, tsl(i, 1024)],
                            scalar1=bet[:, i:i + 1], scalar2=None, op0=OP.add),
            e.tensor_tensor(out=v2b[:, tsl(i, 1024)], in0=v2b[:, tsl(i, 1024)], in1=h1rel[:],
                            op=OP.add)][-1])(i), deps=[hl, lastg])
        lastg = g1
    ln3 = ln_stats2(v2b, lastg)
    lasty = ln3
    for i in range(16):
        y1 = b.step("vector", (lambda i: lambda e: [
            e.tensor_tensor(out=ysto[:, :], in0=v2b[:, tsl(i, 1024)], in1=mub[:],
                            op=OP.subtract),
            e.tensor_tensor(out=ysto[:, :], in0=ysto[:, :], in1=ivb[:],
                            op=OP.mult)][-1])(i), deps=[lasty])
        yd = b.step("gpsimd", (lambda i: lambda e: e.dma_start(
            out=g["y_out"][tsl(i, 128), :], in_=ysto[:, :]))(i), deps=[y1], dma=True)
        lasty = yd
    b.barrier()
    b.finalize()
    b.close()
    return nc, g


def _prep_inputs(inputs):
    x = np.ascontiguousarray(inputs["L0_attn_x"])
    w_in = np.ascontiguousarray(inputs["L0_attn_in_proj_w"].T)
    w_xp = np.ascontiguousarray(inputs["L0_attn_x_proj_w"].T)
    w_dt = np.ascontiguousarray(inputs["L0_attn_dt_proj_w"].T)
    w_out = np.ascontiguousarray(inputs["L0_attn_out_proj_w"].T)
    w_syn = np.ascontiguousarray(inputs["L0_ffn_W_syn"].T)
    w_o2 = np.ascontiguousarray(inputs["L0_ffn_W_out"].T)
    common = dict(
        w_in=w_in, w_xp=w_xp, w_dt=w_dt, w_out=w_out, w_syn=w_syn, w_o2=w_o2,
        conv_w=np.ascontiguousarray(
            inputs["L0_attn_conv_w"].reshape(32, 128, 4).transpose(1, 0, 2).reshape(128, 128)),
        conv_b=np.ascontiguousarray(inputs["L0_attn_conv_b"].reshape(32, 128).T),
        b_dt=np.ascontiguousarray(inputs["L0_attn_dt_proj_b"]),
        a_log=np.ascontiguousarray(inputs["L0_attn_A_log"]),
        d_vec=np.ascontiguousarray(inputs["L0_attn_D"]),
        b_syn=np.ascontiguousarray(inputs["L0_ffn_b_syn"]),
        b_out=np.ascontiguousarray(inputs["L0_ffn_b_out"]),
        gamma=np.ascontiguousarray(inputs["L0_ffn_gamma"].reshape(16, 128).T),
        beta=np.ascontiguousarray(inputs["L0_ffn_beta"].reshape(16, 128).T),
        iota32=np.tile(np.arange(32, dtype=np.float32), (128, 1)),
        d_row=np.ascontiguousarray(inputs["L0_attn_D"].reshape(1, 64)),
    )
    in_maps = []
    for c in range(8):
        seq, half = c // 2, c % 2
        if half == 0:
            xs = np.concatenate([x[seq, 0:4], x[seq, 0:1024]], axis=0).T
            hm = np.zeros((128, 1), np.float32)
            rl = np.zeros((128, 1), np.float32)
        else:
            xs = x[seq, 1020:2048].T
            hm = np.ones((128, 1), np.float32)
            rl = np.ones((128, 1), np.float32)
        m = dict(common)
        m["x_in"] = np.ascontiguousarray(xs)
        m["halo_mask"] = hm
        m["role"] = rl
        in_maps.append(m)
    return in_maps


def kernel(**inputs):
    in_maps = _prep_inputs(inputs)
    nc, _ = build_nc(taps=DEBUG_TAPS)
    res = run_bass_kernel_spmd(nc, in_maps, core_ids=list(range(8)))
    out = np.empty((4, 2048, 2048), np.float32)
    for c in range(8):
        seq, half = c // 2, c % 2
        out[seq, half * 1024:(half + 1) * 1024, :] = res.results[c]["y_out"].T
    return out


# version 02
# speedup vs baseline: 1.2225x; 1.2225x over previous
import numpy as np
import concourse.bass as bass
import concourse.mybir as mybir
from concourse.bass_utils import run_bass_kernel_spmd

F32 = mybir.dt.float32
BF16 = mybir.dt.bfloat16
AF = mybir.ActivationFunctionType
OP = mybir.AluOpType
AXL = mybir.AxisListType

D = 2048
DI = 4096
H = 64
P = 64
NST = 128
CH = 64
NCH = 16
T = 1024
TH = T + 4
XP = 4416
NH = 8192
KTOP = 819
RG = [[0, 1], [2, 3], [4, 5], [6, 7]]
BIG = 1e30

DEBUG_TAPS = []


def tsl(i, n):
    return slice(i * n, (i + 1) * n)


class B:
    ENGINES = ("sync", "gpsimd", "tensor", "vector", "scalar")

    def __init__(self, nc):
        self.nc = nc
        self.ops = {e: [] for e in self.ENGINES}
        self.cm = []
        self.sems = {}
        self.tick = {}
        for e in self.ENGINES:
            c = nc.semaphore("tk_" + e)
            self.sems[e] = c.__enter__()
            self.cm.append(c)
            self.tick[e] = 0

    def sbuf(self, name, shape, dtype=F32, side="left"):
        c = self.nc.sbuf_tensor(name, shape, dtype, side=side)
        t = c.__enter__()
        self.cm.append(c)
        self.by_handle = getattr(self, "by_handle", {})
        self.by_handle[id(t)] = c
        return t

    def release(self, t):
        c = self.by_handle.pop(id(t))
        self.cm.remove(c)
        c.__exit__(None, None, None)

    def psum(self, name, shape, dtype=F32):
        c = self.nc.psum_tensor(name, shape, dtype)
        t = c.__enter__()
        self.cm.append(c)
        self.by_handle = getattr(self, "by_handle", {})
        self.by_handle[id(t)] = c
        return t

    def step(self, eng, fn, deps=(), dma=False):
        inc = 16 if dma else 1
        self.tick[eng] += inc
        val = self.tick[eng]
        sems = self.sems

        def run(e, fn=fn, deps=tuple(deps), eng=eng, val=val, dma=dma):
            for de, dv in deps:
                if de == eng and dv <= val - inc:
                    if eng in ("vector", "scalar", "gpsimd"):
                        e.wait_ge(sems[de], dv)
                    continue
                e.wait_ge(sems[de], dv)
            last = fn(e)
            if dma:
                assert last is not None, "dma step must return the dma instr"
                last.then_inc(sems[eng], 16)
            else:
                if last is not None:
                    last.then_inc(sems[eng], 1)
                else:
                    e.nop().then_inc(sems[eng], 1)
        self.ops[eng].append(run)
        return (eng, val)

    def barrier(self):
        snap = dict(self.tick)
        out = []
        for e in self.ENGINES:
            deps = [(o, snap[o]) for o in self.ENGINES if o != e]
            out.append(self.step(e, lambda ee: None, deps=deps))
        return dict(self.tick)

    def finalize(self):
        nc = self.nc
        with nc.Block() as block:
            @block.sync
            def _(e):
                for f in self.ops["sync"]:
                    f(e)

            @block.gpsimd
            def _(e):
                for f in self.ops["gpsimd"]:
                    f(e)

            @block.tensor
            def _(e):
                for f in self.ops["tensor"]:
                    f(e)

            @block.vector
            def _(e):
                for f in self.ops["vector"]:
                    f(e)

            @block.scalar
            def _(e):
                for f in self.ops["scalar"]:
                    f(e)

    def close(self):
        for c in reversed(self.cm):
            c.__exit__(None, None, None)


def build_nc(taps=(), upto='ALL', ncx=16):
    TAP = set(taps)
    nc = bass.Bass("TRN2", target_bir_lowering=False)
    b = B(nc)
    g = {}

    def din(name, shape, dtype=F32):
        g[name] = nc.dram_tensor(name, shape, dtype, kind="ExternalInput").ap()

    def dout(name, shape, dtype=F32):
        g[name] = nc.dram_tensor(name, shape, dtype, kind="ExternalOutput").ap()

    def dint(name, shape, dtype=F32):
        g[name] = nc.dram_tensor(name, shape, dtype, kind="Internal").ap()

    din("x_in", [D, TH])
    din("halo_mask", [128, 1])
    din("role", [128, 1])
    din("w_in", [D, 2 * DI])
    din("conv_w", [128, 128])
    din("conv_b", [128, 32])
    din("w_xp", [DI, XP])
    din("w_dt", [H, H])
    din("b_dt", [H])
    din("a_log", [H])
    din("d_vec", [H])
    din("w_out", [DI, D])
    din("w_syn", [D, NH])
    din("b_syn", [NH])
    din("w_o2", [NH, D])
    din("b_out", [D])
    din("gamma", [128, 16])
    din("beta", [128, 16])
    din("iota32", [128, 32])
    din("d_row", [1, H])
    dout("y_out", [D, T])
    dint("silu_z_dram", [DI, T])
    g["gated_dram"] = nc.dram_tensor("gated_dram", [DI, T], BF16, kind="Internal").ap()
    dint("h1_dram", [D, T])
    dint("z_dram", [T, NH])
    dint("cc_in", [NST, H * P])
    dint("cc_out", [2 * NST, H * P])

    def tap(name, shape, dtype=F32):
        if name in TAP:
            dout(name, shape, dtype)
            return g[name]
        return None

    convw = b.sbuf("convw", [128, 32 * 4])
    convb = b.sbuf("convb", [128, 32])
    bdtn = b.sbuf("bdtn", [H, 1])
    expa = b.sbuf("expa", [H, 1])
    dvec = b.sbuf("dvec", [H, 1])
    wdtb = b.sbuf("wdtb", [H, H], BF16)
    gam = b.sbuf("gam", [128, 16])
    bet = b.sbuf("bet", [128, 16])
    hmask = b.sbuf("hmask_sb", [128, 1])
    role = b.sbuf("role_sb", [128, 1])
    onesr = b.sbuf("onesr", [1, 128])
    epsr = b.sbuf("epsr", [1, 1])
    mhalfr = b.sbuf("mhalfr", [1, 1])
    halfp = b.sbuf("halfp", [128, 1])
    halfn = b.sbuf("halfn", [128, 1])
    onesK = b.sbuf("onesK", [128, 128])
    tril = b.sbuf("tril", [CH, CH])
    ident = b.sbuf("ident", [128, 128])
    identb128 = b.sbuf("identb128", [128, 128], BF16)

    lds = []
    pairs = [
        (convw, g["conv_w"][:, :]),
        (convb, g["conv_b"][:, :]),
        (expa, g["a_log"].rearrange("(h one) -> h one", one=1)),
        (dvec, g["d_vec"].rearrange("(h one) -> h one", one=1)),
        (wdtb, None),
        (gam, g["gamma"][:, :]),
        (bet, g["beta"][:, :]),
        (hmask, g["halo_mask"][:, :]),
        (role, g["role"][:, :]),
        (bdtn, g["b_dt"].rearrange("(h one) -> h one", one=1)),
    ]

    def f_lda(e):
        last = None
        for dst, src in pairs:
            if src is None:
                continue
            last = e.dma_start(out=dst[:], in_=src)
        return last
    a_deps = []
    for dst, src in pairs:
        if src is None:
            continue
        a_deps.append(b.step("sync", (lambda dst, src: lambda e: e.dma_start(out=dst[:], in_=src))(dst, src), dma=True))
    wdtf = b.sbuf("wdtf", [H, H])
    a_wdt = b.step("sync", lambda e: e.dma_start(out=wdtf[:], in_=g["w_dt"][:, :]), dma=True)
    a_wdtc = b.step("vector", lambda e: e.tensor_copy(out=wdtb[:], in_=wdtf[:]), deps=[a_wdt])

    def f_const(e):
        e.memset(onesr[:], 1.0)
        e.memset(epsr[:], 1e-5)
        e.memset(mhalfr[:], -0.5)
        e.memset(halfp[:], 0.5)
        e.memset(halfn[:], -0.5)
        e.memset(onesK[:], 1.0)
        e.memset(tril[:], 0.0)
        e.affine_select(out=tril[:], in_=tril[:], compare_op=OP.is_gt,
                        fill=1.0, base=0, pattern=[[-1, CH]], channel_multiplier=1)
        e.memset(ident[:], 0.0)
        e.affine_select(out=ident[:], in_=ident[:], compare_op=OP.not_equal,
                        fill=1.0, base=0, pattern=[[-1, 128]], channel_multiplier=1)
        return None
    a_const = b.step("gpsimd", f_const)

    a_all = list(a_deps) + [a_wdtc, a_const]
    a_expa = b.step("scalar", lambda e: e.activation(out=expa[:], in_=expa[:], func=AF.Exp),
                    deps=a_deps)
    a_bdtn = b.step("vector", lambda e: e.tensor_scalar_mul(bdtn[:], bdtn[:], -1.0),
                    deps=a_deps)

    AX = b.sbuf("AX", [128, 32 * T], BF16)
    x_bf16 = AX[:, 0:16 * TH]

    wst = b.sbuf("wst", [128, 8192], side="right")
    wsb = b.sbuf("wsb", [128, 8192], BF16, side="right")

    xst = b.sbuf("xst", [128, TH], side="right")
    xbr = b.sbuf("xbr", [128, TH], side="right")
    uev = b.sbuf("uev", [128, 1024], side="right")

    B_fm = b.sbuf("B_fm", [NST, T], BF16)
    B_tok = b.sbuf("B_tok", [128, 8 * NST], BF16)
    C_fm = b.sbuf("C_fm", [NST, T], BF16)
    dtneg = b.sbuf("dtneg", [H, T])
    a_sc = b.sbuf("a_sc", [H, T])
    dt_bf = b.sbuf("dt_bf", [H, T], BF16)
    x_conv = b.sbuf("x_conv", [128, 32 * T], BF16)

    prev_cast = None
    cast_ticks = []
    for i in range(16):
        deps = [a_deps[-1]] if i == 0 else [cast_ticks[i - 1]]
        ldt = b.step("sync", (lambda i: lambda e: e.dma_start(
            out=xst[:, :], in_=g["x_in"][tsl(i, 128), :]))(i),
            deps=deps, dma=True)

        mkt = b.step("vector", lambda e: e.tensor_scalar(
            out=xst[:, 0:4], in0=xst[:, 0:4], scalar1=hmask[:], scalar2=None, op0=OP.mult),
            deps=[ldt] + ([a_deps[-1]] if i == 0 else []))
        ct = b.step("vector", (lambda i: lambda e: e.tensor_copy(
            out=x_bf16[:, i * TH:(i + 1) * TH], in_=xst[:, :]))(i), deps=[mkt])
        cast_ticks.append(ct)
    b_xcast = cast_ticks[-1]

    ppB0 = b.psum("ppB0", [128, 512])
    ppB1 = b.psum("ppB1", [128, 512])
    ppB2 = b.psum("ppB2", [128, 4])

    t_silu_z = tap("t_silu_z", [DI, T])
    t_xconv = tap("t_xconv", [DI, T])

    ev_t = None
    prev_wc = None
    prev_mm_last = None
    prev_sl = None
    for oc in range(16):
        is_z = oc >= 8
        wl = b.step("sync", (lambda oc: lambda e: e.dma_start(
            out=wst[:].rearrange("p (k f) -> p k f", k=16),
            in_=g["w_in"][:, tsl(oc, 512)].rearrange("(k p) f -> p k f", p=128)))(oc),
            deps=[prev_wc] if prev_wc else [], dma=True)
        wc = b.step("vector", lambda e: e.tensor_copy(out=wsb[:], in_=wst[:]),
                    deps=[wl] + ([prev_mm_last] if prev_mm_last else []))
        prev_wc = wc

        for sub in range(4):
            def f_mm(e, sub=sub):
                last = None
                for grp, (f0, fn, pp) in enumerate(((0, 512, ppB0), (512, 512, ppB1), (1024, 4, ppB2))):
                    for k in range(16):
                        lhsT = wsb[:, k * 512 + sub * 128:k * 512 + sub * 128 + 128]
                        rhs = x_bf16[:, k * TH + f0:k * TH + f0 + fn]
                        last = e.matmul(pp[:, 0:fn], lhsT, rhs, start=(k == 0), stop=(k == 15))
                return last
            mm_deps = [wc, b_xcast]
            if ev_t is not None:
                mm_deps.append(ev_t)
            mm = b.step("tensor", f_mm, deps=mm_deps)
            if sub == 3:
                prev_mm_last = mm

            ch = (oc % 8) * 4 + sub

            if not is_z:
                def f_cp(e, ch=ch):
                    e.tensor_copy(out=xbr[:, 0:512], in_=ppB0[:])
                    e.tensor_copy(out=xbr[:, 512:1024], in_=ppB1[:])
                    return e.tensor_copy(out=xbr[:, 1024:1028], in_=ppB2[:])
                cv0 = b.step("vector", f_cp, deps=[mm] + ([prev_sl] if prev_sl else []))

                def f_conv(e, ch=ch):
                    acc = uev[:, 0:1024]
                    e.tensor_scalar(out=acc, in0=xbr[:, 1:1025],
                                    scalar1=convw[:, ch * 4:ch * 4 + 1],
                                    scalar2=None, op0=OP.mult)
                    last = None
                    for j in (1, 2, 3):
                        last = e.scalar_tensor_tensor(out=acc, in0=xbr[:, 1 + j:1 + j + 1024],
                                               scalar=convw[:, ch * 4 + j:ch * 4 + j + 1],
                                               in1=acc, op0=OP.mult, op1=OP.add)
                    return last
                cv = b.step("vector", f_conv, deps=[cv0])
                sl = b.step("scalar", (lambda ch: lambda e: e.activation(
                    out=x_conv[:, ch * T:(ch + 1) * T], in_=uev[:, 0:1024],
                    func=AF.Silu, bias=convb[:, ch:ch + 1], scale=1.0))(ch),
                    deps=[cv])
                prev_sl = sl
                ev_t = cv0
                if t_xconv is not None:
                    b.step("gpsimd", (lambda ch: lambda e: e.dma_start(
                        out=g["t_xconv"][tsl(ch, 128), :], in_=x_conv[:, ch * T:(ch + 1) * T]))(ch),
                        deps=[sl], dma=True)
            else:
                def f_zev(e, ch=ch):
                    dz = uev[:, 0:1024]
                    e.activation(out=dz[:, 0:508], in_=ppB0[:, 4:512], func=AF.Silu)
                    e.activation(out=dz[:, 508:1020], in_=ppB1[:], func=AF.Silu)
                    return e.activation(out=dz[:, 1020:1024], in_=ppB2[:], func=AF.Silu)
                sl = b.step("scalar", f_zev, deps=[mm])
                st = b.step("gpsimd", (lambda ch: lambda e: e.dma_start(
                    out=g["silu_z_dram"][tsl(ch, 128), :], in_=uev[:, 0:1024]))(ch),
                    deps=[sl], dma=True)
                ev_t = st
                if t_silu_z is not None:
                    b.step("gpsimd", (lambda ch: lambda e: e.dma_start(
                        out=g["t_silu_z"][tsl(ch, 128), :], in_=uev[:, 0:1024]))(ch),
                        deps=[sl], dma=True)

    if upto == 'B':
        b.finalize(); b.close(); return nc, g
    bar_B = b.barrier()
    b.release(ppB2)
    b.release(ppB1)
    b.release(ppB0)

    ppC0h = b.psum("ppC0", [128, 512])
    ppC1h = b.psum("ppC1", [128, 512])
    ppC0 = ppC0h[:, 0:256]
    ppC1 = ppC1h[:, 0:256]

    t_X = tap("t_X", [8 * 128, DI], BF16)

    ev_t = None
    wc_prev = None
    prev_mm_last = None
    for oc in range(ncx):
        wl = b.step("sync", (lambda oc: lambda e: e.dma_start(
            out=wst[:].rearrange("p (k f) -> p k f", k=32),
            in_=g["w_xp"][:, tsl(oc, 256)].rearrange("(k p) f -> p k f", p=128)))(oc),
            deps=[wc_prev] if wc_prev else [], dma=True)
        wc = b.step("vector", lambda e: e.tensor_copy(out=wsb[:], in_=wst[:]),
                    deps=[wl] + ([prev_mm_last] if prev_mm_last else []))
        wc_prev = wc

        for tt in range(8):
            pp = (ppC0, ppC1)[tt % 2]

            def f_mm(e, oc=oc, tt=tt, pp=pp):
                last = None
                for k in range(32):
                    lhsT = x_conv[:, k * T + tt * 128:k * T + tt * 128 + 128]
                    rhs = wsb[:, k * 256:(k + 1) * 256]
                    last = e.matmul(pp[:], lhsT, rhs, start=(k == 0), stop=(k == 31))
                return last
            deps = [wc]
            if ev_t is not None:
                deps.append(ev_t)
            mm = b.step("tensor", f_mm, deps=deps)
            if tt == 7:
                prev_mm_last = mm
            ev = b.step("vector", (lambda oc, tt, pp: lambda e: e.tensor_copy(
                out=AX[:, tt * DI + oc * 256: tt * DI + oc * 256 + 256], in_=pp[:]))(oc, tt, pp),
                deps=[mm])
            ev_t = ev
    x_done = ev_t
    if t_X is not None:
        for tt in range(8):
            b.step("gpsimd", (lambda tt: lambda e: e.dma_start(
                out=g["t_X"][tsl(tt, 128), :], in_=AX[:, tt * DI:(tt + 1) * DI]))(tt),
                deps=[x_done], dma=True)

    if upto == 'CX':
        b.finalize(); b.close(); return nc, g

    b.release(ppC1h)
    b.release(ppC0h)
    pp6 = [b.psum("pp6_%d" % i, [128, 512]) for i in range(6)]
    ppBt = [b.psum("ppBt%d" % i, [128, 512]) for i in range(2)]
    t_Bfm = tap("t_Bfm", [NST, T])
    t_Cfm = tap("t_Cfm", [NST, T])
    t_dt = tap("t_dt", [H, T])

    parts = ((0, 64, "dt"), (64, 128, "B"), (192, 128, "C"))
    mm_half = []
    wc = None
    for kh in range(2):
        deps = [x_done] if kh == 0 else list(mm_half)
        wl = b.step("sync", (lambda kh: lambda e: e.dma_start(
            out=wst[:, 0:16 * 320].rearrange("p (k f) -> p k f", k=16),
            in_=g["w_xp"][kh * 2048:(kh + 1) * 2048, DI:XP].rearrange("(k p) f -> p k f", p=128)))(kh),
            deps=deps, dma=True)
        wc = b.step("vector", lambda e: e.tensor_copy(
            out=wsb[:, 0:16 * 320], in_=wst[:, 0:16 * 320]), deps=[wl])
        mm_half = []
        for pi, (off, width, pname) in enumerate(parts):
            for tg in range(2):
                def f_mm(e, off=off, width=width, tg=tg, kh=kh, pi=pi):
                    pp = pp6[pi * 2 + tg]
                    last = None
                    for kk in range(16):
                        lhsT = wsb[:, kk * 320 + off: kk * 320 + off + width]
                        rhs = x_conv[:, (kh * 16 + kk) * T + tg * 512: (kh * 16 + kk) * T + tg * 512 + 512]
                        last = e.matmul(pp[0:width, :], lhsT, rhs,
                                 start=(kh == 0 and kk == 0), stop=(kh == 1 and kk == 15))
                    return last
                mm = b.step("tensor", f_mm, deps=[wc])
                mm_half.append(mm)
        def f_bt(e, kh=kh):
            last = None
            for tt in range(8):
                pp = ppBt[tt // 4]
                for kk in range(16):
                    k = kh * 16 + kk
                    lhsT = x_conv[:, k * T + tt * 128:k * T + tt * 128 + 128]
                    rhs = wsb[:, kk * 320 + 64: kk * 320 + 192]
                    last = e.matmul(pp[:, (tt % 4) * 128:(tt % 4) * 128 + 128], lhsT, rhs,
                                    start=(kh == 0 and kk == 0), stop=(kh == 1 and kk == 15))
            return last
        mmbt = b.step("tensor", f_bt, deps=[wc])
        mm_half.append(mmbt)

    part_ev = []
    for pi, (off, width, pname) in enumerate(parts):
        for tg in range(2):
            def f_ev(e, off=off, width=width, tg=tg, pname=pname, pi=pi):
                pp = pp6[pi * 2 + tg]
                if pname == "dt":
                    return e.tensor_copy(out=dtneg[:, tg * 512:(tg + 1) * 512], in_=pp[0:64, :])
                dst = B_fm if pname == "B" else C_fm
                return e.tensor_copy(out=dst[:, tg * 512:(tg + 1) * 512], in_=pp[0:128, :])
            ev = b.step("vector", f_ev, deps=[mm_half[pi * 2 + tg]])
            part_ev.append(ev)

    for i in range(2):
        part_ev.append(b.step("vector", (lambda i: lambda e: e.tensor_copy(
            out=B_tok[:, i * 512:(i + 1) * 512], in_=ppBt[i][:, :]))(i),
            deps=[mm_half[-1]]))

    c_dtc = b.step("vector", lambda e: e.tensor_copy(out=dt_bf[:], in_=dtneg[:]),
                   deps=[part_ev[1]])

    ppdt = pp6[0]
    dt_mms = []
    for tg in range(2):
        mm = b.step("tensor", (lambda tg: lambda e: e.matmul(
            ppdt[0:H, :], wdtb[:], dt_bf[:, tg * 512:(tg + 1) * 512], start=True, stop=True))(tg),
            deps=[c_dtc, a_wdtc, part_ev[-1]] + ([] if tg == 0 else [dt_mms[-1][1]]))
        sg = b.step("scalar", (lambda tg: lambda e: e.activation(
            out=dtneg[:, tg * 512:(tg + 1) * 512], in_=ppdt[0:H, :], func=AF.Sigmoid,
            bias=bdtn[:], scale=-1.0))(tg), deps=[mm, a_bdtn])
        dt_mms.append((mm, sg))
    c_ln = b.step("scalar", lambda e: e.activation(out=dtneg[:], in_=dtneg[:], func=AF.Ln),
                  deps=[dt_mms[-1][1]])
    c_ln = b.step("vector", lambda e: e.tensor_scalar_max(dtneg[:], dtneg[:], -87.0),
                  deps=[c_ln])
    c_asc = b.step("vector", lambda e: e.tensor_scalar(
        out=a_sc[:], in0=dtneg[:], scalar1=expa[:], scalar2=None, op0=OP.mult),
        deps=[c_ln, a_expa])
    if t_dt is not None:
        b.step("gpsimd", lambda e: e.dma_start(out=g["t_dt"][:, :], in_=dtneg[:]),
               deps=[c_ln], dma=True)
    if t_Bfm is not None:
        b.step("gpsimd", lambda e: e.dma_start(out=g["t_Bfm"][:, :], in_=B_fm[:]),
               deps=[part_ev[3]], dma=True)
    if t_Cfm is not None:
        b.step("gpsimd", lambda e: e.dma_start(out=g["t_Cfm"][:, :], in_=C_fm[:]),
               deps=[part_ev[5]], dma=True)

    bar_C = b.barrier()
    bar_C = b.barrier()
    if upto == 'C':
        b.finalize(); b.close(); return nc, g

    b.release(uev)
    b.release(xbr)
    b.release(xst)
    b.release(wsb)
    b.release(wst)
    b.release(x_conv)
    for t in reversed(ppBt):
        b.release(t)
    for t in reversed(pp6):
        b.release(t)
    psA = b.psum("psA", [64, 1024])
    psB = b.psum("psB", [64, 1024])
    psE = b.psum("psE", [64, 512])
    psT = b.psum("psT", [128, 512])
    psS = b.psum("psS", [128, 1024])

    Dm = b.sbuf("Dm", [64, 1024], side="right")
    Gt = b.sbuf("Gt", [64, DI], BF16, side="right")
    XB1 = b.sbuf("XB1", [64, DI], BF16, side="right")
    XB2 = b.sbuf("XB2", [64, DI], BF16, side="right")
    XB3 = b.sbuf("XB3", [64, DI], BF16, side="right")
    Ytc = b.sbuf("Ytc", [64, DI], BF16, side="right")
    aT_all = b.sbuf("aT_all", [64, NCH * CH], side="right")
    cs_lh = b.sbuf("cs_lh", [64, NCH * CH], side="right")
    cs_hl = b.sbuf("cs_hl", [64, NCH * CH], side="right")
    sdo_lh = b.sbuf("sdo_lh", [64, NCH * CH], side="right")
    decT = b.sbuf("decT", [64, NCH * CH], side="right")
    dtT = b.sbuf("dtT", [64, NCH * CH], side="right")
    dtpos = b.sbuf("dtpos", [H, T], side="right")
    lamlog = b.sbuf("lamlog", [H, NCH], side="right")
    lamv = b.sbuf("lamv", [H, NCH], side="right")
    zer16 = b.sbuf("zer16", [H, NCH], side="right")
    cdecl = b.sbuf("cdecl", [H, NCH], side="right")
    lam_tab = b.sbuf("lam_tab", [128, H * NCH], side="right")
    cdec_tab = b.sbuf("cdec_tab", [64, H * NCH], side="right")
    d_tab = b.sbuf("d_tab", [64, H], side="right")
    Mm_sb = b.sbuf("Mm_sb", [64, CH], side="right")
    identb = b.sbuf("identb", [64, 64], BF16, side="right")
    BzT = b.sbuf("BzT", [64, NST], BF16, side="right")
    rb = b.sbuf("rb", [128, 1024], BF16, side="right")
    SCR = b.sbuf("SCR", [128, DI], side="right")
    r_run = b.sbuf("r_run", [128, DI], side="right")
    S_in = b.sbuf("S_in", [128, DI], BF16, side="right")
    szt = SCR[:, 0:2048]
    uevY = SCR[0:64, 2048:3072]
    lrow = SCR[0:1, 0:1024]
    crow = SCR[0:1, 1024:2048]
    drow = SCR[0:1, 2048:2048 + H]
    gst = b.sbuf("gst", [128, 512], BF16, side="right")

    t_Y = tap("t_Y", [CH, NCH * DI], BF16)
    t_Yf = tap("t_Yf", [CH, NCH * DI], BF16)
    t_gated = tap("t_gated", [DI, T], BF16)
    g["ytok_dram"] = nc.dram_tensor("ytok_dram", [CH, NCH * DI], BF16, kind="Internal").ap()

    c_dtp = b.step("vector", lambda e: e.tensor_scalar_mul(dtpos[:], dtneg[:], -1.0),
                   deps=[c_asc])
    c_zer = b.step("vector", lambda e: e.memset(zer16[:], 0.0), deps=[])
    c_idb = b.step("vector", lambda e: e.tensor_copy(out=identb[:], in_=ident[0:64, 0:64]),
                   deps=[a_const])
    c_idb2 = b.step("vector", lambda e: e.tensor_copy(out=identb128[:], in_=ident[:]),
                    deps=[a_const])
    c_drow = b.step("sync", lambda e: e.dma_start(out=drow[:], in_=g["d_row"][:, :]), dma=True)

    pre_dep = [c_asc, c_dtp]
    for half in range(2):
        def f_tr_a(e, half=half):
            last = None
            for j in range(8):
                z = half * 8 + j
                last = e.transpose(psT[0:64, j * 64:(j + 1) * 64],
                                   a_sc[:, z * CH:(z + 1) * CH], ident[0:64, 0:64])
            return last
        mm1 = b.step("tensor", f_tr_a, deps=pre_dep)
        ev1 = b.step("vector", (lambda half: lambda e: e.tensor_copy(
            out=aT_all[:, half * 512:(half + 1) * 512], in_=psT[0:64, :]))(half), deps=[mm1])

        def f_cs(e, half=half):
            last = None
            for j in range(8):
                z = half * 8 + j
                last = e.matmul(psT[0:64, j * 64:(j + 1) * 64], tril[:],
                                aT_all[:, z * 64:(z + 1) * 64], start=True, stop=True)
            return last
        mm2 = b.step("tensor", f_cs, deps=[ev1])
        ev2 = b.step("vector", (lambda half: lambda e: e.tensor_copy(
            out=cs_lh[:, half * 512:(half + 1) * 512], in_=psT[0:64, :]))(half), deps=[mm2])
        sd2 = b.step("scalar", (lambda half: lambda e: e.activation(
            out=sdo_lh[:, half * 512:(half + 1) * 512],
            in_=cs_lh[:, half * 512:(half + 1) * 512], func=AF.Exp))(half), deps=[ev2])

        def f_tr_cs(e, half=half):
            last = None
            for j in range(8):
                z = half * 8 + j
                last = e.transpose(psT[0:64, j * 64:(j + 1) * 64],
                                   cs_lh[:, z * 64:(z + 1) * 64], ident[0:64, 0:64])
            return last
        mm3 = b.step("tensor", f_tr_cs, deps=[ev2])
        ev3 = b.step("vector", (lambda half: lambda e: e.tensor_copy(
            out=cs_hl[:, half * 512:(half + 1) * 512], in_=psT[0:64, :]))(half), deps=[mm3])

        def f_ll(e, half=half):
            last = None
            for j in range(8):
                z = half * 8 + j
                last = e.tensor_copy(out=lamlog[:, z:z + 1],
                                     in_=cs_hl[:, z * 64 + 63:z * 64 + 64])
            return last
        ev3b = b.step("vector", f_ll, deps=[ev3])

        def f_dec(e, half=half):
            last = None
            for j in range(8):
                z = half * 8 + j
                last = e.activation(out=Dm[:, j * 64:(j + 1) * 64],
                                    in_=cs_hl[:, z * 64:(z + 1) * 64], func=AF.Exp,
                                    bias=lamlog[:, z:z + 1], scale=-1.0)
            return last
        ev4 = b.step("scalar", f_dec, deps=[ev3b])

        def f_tr_dec(e, half=half):
            last = None
            for j in range(8):
                last = e.transpose(psT[0:64, j * 64:(j + 1) * 64],
                                   Dm[:, j * 64:(j + 1) * 64], ident[0:64, 0:64])
            return last
        mm4 = b.step("tensor", f_tr_dec, deps=[ev4])
        ev5 = b.step("vector", (lambda half: lambda e: e.tensor_copy(
            out=decT[:, half * 512:(half + 1) * 512], in_=psT[0:64, :]))(half), deps=[mm4])

        def f_tr_dt(e, half=half):
            last = None
            for j in range(8):
                z = half * 8 + j
                last = e.transpose(psT[0:64, j * 64:(j + 1) * 64],
                                   dtpos[:, z * CH:(z + 1) * CH], ident[0:64, 0:64])
            return last
        mm5 = b.step("tensor", f_tr_dt, deps=[c_dtp, ev5])
        ev6 = b.step("vector", (lambda half: lambda e: e.tensor_copy(
            out=dtT[:, half * 512:(half + 1) * 512], in_=psT[0:64, :]))(half), deps=[mm5])
        pre_dep = [ev6]

    c_cd0 = b.step("vector", lambda e: e.memset(cdecl[:, 0:1], 0.0), deps=pre_dep)
    c_cd1 = b.step("vector", lambda e: e.tensor_tensor_scan(
        out=cdecl[:, 1:NCH], data0=lamlog[:, 0:NCH - 1], data1=zer16[0:H, 1:NCH],
        initial=0.0, op0=OP.add, op1=OP.add), deps=[c_cd0, c_zer])
    c_cd2 = b.step("scalar", lambda e: e.activation(out=cdecl[:], in_=cdecl[:], func=AF.Exp),
                   deps=[c_cd1])
    c_cd3 = b.step("vector", lambda e: e.tensor_scalar(
        out=cdecl[:], in0=cdecl[:], scalar1=role[0:H, :], scalar2=None, op0=OP.mult),
        deps=[c_cd2, a_deps[-1]])
    c_lam = b.step("scalar", lambda e: e.activation(out=lamv[:], in_=lamlog[:], func=AF.Exp),
                   deps=[c_cd1])

    c_fl1 = b.step("gpsimd", lambda e: e.dma_start(
        out=lrow[:].rearrange("one (h z) -> one h z", z=NCH), in_=lamv[:, :]),
        deps=[c_lam], dma=True)
    c_fl2 = b.step("gpsimd", lambda e: e.dma_start(
        out=crow[:].rearrange("one (h z) -> one h z", z=NCH), in_=cdecl[:, :]),
        deps=[c_cd3], dma=True)

    bc_deps = [c_fl1, c_fl2, c_drow, a_const]
    for j in range(2):
        mmb = b.step("tensor", (lambda j: lambda e: e.matmul(
            psT[:, 0:512], onesr[:], lrow[:, j * 512:(j + 1) * 512], start=True, stop=True))(j),
            deps=bc_deps)
        evb = b.step("vector", (lambda j: lambda e: e.tensor_copy(
            out=lam_tab[:, j * 512:(j + 1) * 512], in_=psT[:, :]))(j), deps=[mmb])
        mmc = b.step("tensor", (lambda j: lambda e: e.matmul(
            psT[0:64, 0:512], onesr[0:1, 0:64], crow[:, j * 512:(j + 1) * 512],
            start=True, stop=True))(j), deps=[evb])
        evc = b.step("vector", (lambda j: lambda e: e.tensor_copy(
            out=cdec_tab[:, j * 512:(j + 1) * 512], in_=psT[0:64, :]))(j), deps=[mmc])
        bc_deps = [evc]
    mmd = b.step("tensor", lambda e: e.matmul(psT[0:64, 0:64], onesr[0:1, 0:64], drow[:, :],
                                              start=True, stop=True), deps=bc_deps)
    evd = b.step("vector", lambda e: e.tensor_copy(out=d_tab[:], in_=psT[0:64, 0:64]),
                 deps=[mmd])
    c_r0 = b.step("vector", lambda e: e.memset(r_run[:], 0.0), deps=[])

    def hrep(viewsrc, nh, np_=64):
        return viewsrc.rearrange("p (h one) -> p h one", one=1).to_broadcast([64, nh, np_])

    last_mm = evd
    last_v = evd
    last_park = None
    scan_done = None
    for z in range(NCH):
        zl = tsl(z, CH)
        dep_xz = [x_done]
        if last_park is not None:
            dep_xz.append(last_park)
        xz_cp = b.step("gpsimd", (lambda z: lambda e: e.dma_start(
            out=XB1[:, :], in_=AX[(z % 2) * 64:(z % 2) * 64 + 64,
                                  (z // 2) * DI:(z // 2 + 1) * DI]))(z),
            deps=dep_xz, dma=True)
        mmM = b.step("tensor", (lambda zl=zl: lambda e: e.matmul(
            psE[0:64, 0:64], B_fm[:, zl], C_fm[:, zl], start=True, stop=True))(),
            deps=[part_ev[-1], last_v])
        evM = b.step("vector", lambda e: e.tensor_tensor(
            out=Mm_sb[:], in0=psE[0:64, 0:64], in1=tril[:], op=OP.mult), deps=[mmM])
        evBt = b.step("gpsimd", (lambda z: lambda e: e.dma_start(
            out=BzT[:, :], in_=B_tok[(z % 2) * 64:(z % 2) * 64 + 64,
                                     (z // 2) * NST:(z // 2 + 1) * NST]))(z),
            deps=[part_ev[-1], xz_cp], dma=True)
        am_st = b.step("vector", (lambda z: lambda e: e.tensor_tensor(
            out=SCR[0:64, :].rearrange("p (h l) -> p h l", l=64),
            in0=hrep(aT_all[:, z * 64:(z + 1) * 64], 64),
            in1=tril[:].rearrange("p (one l) -> p one l", one=1).to_broadcast([64, 64, 64]),
            op=OP.mult))(z), deps=[evBt, last_v])
        x_d = b.step("vector", (lambda z: lambda e: e.tensor_tensor(
            out=XB2[:, :].rearrange("p (h q) -> p h q", q=64),
            in0=XB1[:, :].rearrange("p (h q) -> p h q", q=64),
            in1=hrep(d_tab[:, :], 64), op=OP.mult))(z), deps=[xz_cp, evd, am_st])
        x_s = b.step("vector", (lambda z: lambda e: e.tensor_tensor(
            out=XB3[:, :].rearrange("p (h q) -> p h q", q=64),
            in0=XB1[:, :].rearrange("p (h q) -> p h q", q=64),
            in1=hrep(dtT[:, z * 64:(z + 1) * 64], 64), op=OP.mult))(z), deps=[x_d])
        x_dc = b.step("vector", (lambda z: lambda e: e.tensor_tensor(
            out=XB1[:, :].rearrange("p (h q) -> p h q", q=64),
            in0=XB3[:, :].rearrange("p (h q) -> p h q", q=64),
            in1=hrep(decT[:, z * 64:(z + 1) * 64], 64), op=OP.mult))(z), deps=[x_s])
        g_dep = [am_st, evM]
        for gg in range(8):
            h0 = gg * 8
            mmE = b.step("tensor", (lambda gg: lambda e: e.matmul(
                psE[0:64, :], onesK[0:64, 0:64], SCR[0:64, gg * 512:(gg + 1) * 512],
                start=True, stop=True))(gg), deps=g_dep)
            gpar = (gg % 2) * 512
            evE = b.step("vector", (lambda z, gg, h0, gpar=gpar: lambda e: e.tensor_tensor(
                out=Dm[:, gpar:gpar + 512].rearrange("p (h l) -> p h l", l=64),
                in0=psE[0:64, :].rearrange("p (h l) -> p h l", l=64),
                in1=hrep(cs_lh[:, z * 64 + h0:z * 64 + h0 + 8], 8),
                op=OP.subtract))(z, gg, h0), deps=[mmE])
            exE = b.step("scalar", (lambda gg, gpar=gpar: lambda e: e.activation(
                out=Dm[:, gpar:gpar + 512], in_=Dm[:, gpar:gpar + 512],
                func=AF.Exp))(gg), deps=[evE])
            gfin = b.step("vector", (lambda gg, gpar=gpar: lambda e: e.scalar_tensor_tensor(
                out=Gt[:, gg * 512:(gg + 1) * 512].rearrange("p (h l) -> p h l", l=64),
                in0=Dm[:, gpar:gpar + 512].rearrange("p (h l) -> p h l", l=64),
                scalar=1.0,
                in1=Mm_sb[:].rearrange("p (one l) -> p one l", one=1).to_broadcast([64, 8, 64]),
                op0=OP.min, op1=OP.mult))(gg), deps=[exE])
            g_dep = [gfin]
        g_done = g_dep[0]
        for gq in range(4):
            gsl = tsl(gq, 1024)
            rbc = b.step("vector", (lambda gsl=gsl: lambda e: e.tensor_copy(
                out=rb[:, :], in_=r_run[:, gsl]))(), deps=[last_v if z == 0 else scan_done, g_done])
            mmO = b.step("tensor", (lambda zl=zl: lambda e: [
                e.matmul(psB[:, 0:512], C_fm[:, zl], rb[:, 0:512], start=True, stop=True),
                e.matmul(psB[:, 512:1024], C_fm[:, zl], rb[:, 512:1024], start=True, stop=True)][-1])(),
                deps=[rbc])
            mmS = b.step("tensor", (lambda gsl=gsl: lambda e: [
                e.matmul(psS[:, 0:512], BzT[:], XB1[:, gsl][:, 0:512], start=True, stop=True),
                e.matmul(psS[:, 512:1024], BzT[:], XB1[:, gsl][:, 512:1024], start=True, stop=True)][-1])(),
                deps=[x_dc, evBt, mmO])
            t1s = b.step("vector", (lambda z, gq, gsl=gsl: lambda e: e.tensor_tensor(
                out=SCR[:, 0:1024].rearrange("p (h q) -> p h q", q=64),
                in0=r_run[:, gsl].rearrange("p (h q) -> p h q", q=64),
                in1=lam_tab[:].rearrange("p (h z2) -> p h z2", z2=NCH)
                    [:, gq * 16:(gq + 1) * 16, z:z + 1].to_broadcast([128, 16, 64]),
                op=OP.mult))(z, gq), deps=[rbc, g_done])
            scan_done = b.step("vector", (lambda gsl=gsl: lambda e: e.tensor_tensor(
                out=r_run[:, gsl], in0=SCR[:, 0:1024], in1=psS[:, :], op=OP.add))(),
                deps=[t1s, mmS])
            def f_yd(e, z=z, gq=gq, gsl=gsl):
                last = e.matmul(psA[:, 0:512], identb[:], XB2[:, gsl][:, 0:512],
                                start=True, stop=False)
                last = e.matmul(psA[:, 512:1024], identb[:], XB2[:, gsl][:, 512:1024],
                                start=True, stop=False)
                for hl in range(16):
                    h = gq * 16 + hl
                    last = e.matmul(psA[:, hl * 64:(hl + 1) * 64],
                                    Gt[:, h * 64:(h + 1) * 64],
                                    XB3[:, h * 64:(h + 1) * 64],
                                    start=False, stop=(hl == 15))
                return last
            mmY = b.step("tensor", f_yd, deps=[x_s, g_done, scan_done])
            t2s = b.step("vector", (lambda z, gq: lambda e: e.tensor_tensor(
                out=uevY[:].rearrange("p (h q) -> p h q", q=64),
                in0=psB[:, :].rearrange("p (h q) -> p h q", q=64),
                in1=hrep(sdo_lh[:, z * 64 + gq * 16:z * 64 + gq * 16 + 16], 16),
                op=OP.mult))(z, gq), deps=[mmO, mmY])
            evY = b.step("vector", (lambda gsl=gsl: lambda e: e.tensor_tensor(
                out=Ytc[:, gsl], in0=uevY[:], in1=psA[:, :], op=OP.add))(),
                deps=[t2s])
            last_v = evY
        last_park = b.step("gpsimd", (lambda z: lambda e: e.dma_start(
            out=g["ytok_dram"][:, z * DI:(z + 1) * DI], in_=Ytc[:, :]))(z),
            deps=[last_v], dma=True)
        if t_Y is not None:
            b.step("gpsimd", (lambda z: lambda e: e.dma_start(
                out=g["t_Y"][:, z * DI:(z + 1) * DI], in_=Ytc[:, :]))(z),
                deps=[last_v], dma=True)

    cc_up = b.step("gpsimd", lambda e: e.dma_start(out=g["cc_in"][:, :], in_=r_run[:, :]),
                   deps=[scan_done], dma=True)
    cc_go = b.step("gpsimd", lambda e: e.collective_compute(
        "AllGather", OP.bypass, replica_groups=RG, ins=[g["cc_in"][:, :]],
        outs=[g["cc_out"][:, :]]), deps=[cc_up])
    cc_dn = b.step("gpsimd", lambda e: e.dma_start(out=SCR[:, :], in_=g["cc_out"][0:NST, :]),
                   deps=[cc_go], dma=True)
    cc_cast = b.step("vector", lambda e: e.tensor_copy(out=S_in[:], in_=SCR[:, :]),
                     deps=[cc_dn])

    last_g = cc_cast
    for z in range(NCH):
        zl = tsl(z, CH)
        yld = b.step("gpsimd", (lambda z: lambda e: e.dma_start(
            out=Ytc[:, :], in_=g["ytok_dram"][:, z * DI:(z + 1) * DI]))(z),
            deps=[last_g], dma=True)
        szl = b.step("gpsimd", (lambda zl=zl: lambda e: e.dma_start(
            out=szt[:].rearrange("p (j l) -> p j l", l=64),
            in_=g["silu_z_dram"][:, zl].rearrange("(j p) l -> p j l", p=128)))(),
            deps=[yld], dma=True)
        for gq in range(4):
            gsl = tsl(gq, 1024)
            mmC = b.step("tensor", (lambda zl=zl, gsl=gsl: lambda e: [
                e.matmul(psB[:, 0:512], C_fm[:, zl], S_in[:, gsl][:, 0:512],
                         start=True, stop=True),
                e.matmul(psB[:, 512:1024], C_fm[:, zl], S_in[:, gsl][:, 512:1024],
                         start=True, stop=True)][-1])(), deps=[cc_cast, yld, last_g])
            t2c = b.step("vector", (lambda z, gq: lambda e: e.tensor_tensor(
                out=uevY[:].rearrange("p (h q) -> p h q", q=64),
                in0=psB[:, :].rearrange("p (h q) -> p h q", q=64),
                in1=cdec_tab[:].rearrange("p (h z2) -> p h z2", z2=NCH)
                    [:, gq * 16:(gq + 1) * 16, z:z + 1].to_broadcast([64, 16, 64]),
                op=OP.mult))(z, gq), deps=[mmC])
            ycf = b.step("vector", (lambda gsl=gsl: lambda e: e.tensor_tensor(
                out=Ytc[:, gsl], in0=uevY[:], in1=Ytc[:, gsl], op=OP.add))(),
                deps=[t2c])
            last_g = ycf
        if t_Yf is not None:
            b.step("gpsimd", (lambda z: lambda e: e.dma_start(
                out=g["t_Yf"][:, z * DI:(z + 1) * DI], in_=Ytc[:, :]))(z),
                deps=[last_g], dma=True)
        for jb in range(4):
            cst = b.step("vector", (lambda jb: lambda e: e.tensor_copy(
                out=SCR[0:64, 3072:4096], in_=Ytc[:, jb * 1024:(jb + 1) * 1024]))(jb),
                deps=[last_g])

            def f_trY(e, jb=jb):
                last = None
                for i in range(8):
                    last = e.transpose(psT[:, i * 64:(i + 1) * 64],
                                       SCR[0:64, 3072 + i * 128:3072 + (i + 1) * 128],
                                       ident[0:64, 0:64])
                return last
            mmT = b.step("tensor", f_trY, deps=[cst])
            gt = b.step("vector", (lambda jb: lambda e: e.tensor_tensor(
                out=gst[:, :], in0=psT[:, :], in1=szt[:, jb * 512:(jb + 1) * 512],
                op=OP.mult))(jb), deps=[mmT, szl])
            gp = b.step("gpsimd", (lambda z, jb, zl=zl: lambda e: e.dma_start(
                out=g["gated_dram"][jb * 1024:(jb + 1) * 1024, zl]
                    .rearrange("(i p) l -> p i l", p=128),
                in_=gst[:].rearrange("p (i l) -> p i l", l=64)))(z, jb),
                deps=[gt], dma=True)
            last_g = gp
    if t_gated is not None:
        for jj in range(32):
            b.step("gpsimd", (lambda jj: lambda e: e.dma_start(
                out=g["t_gated"][tsl(jj, 128), :],
                in_=g["gated_dram"][tsl(jj, 128), :]))(jj), deps=[last_g], dma=True)

    bar_D = b.barrier()
    if upto == 'D':
        b.finalize(); b.close(); return nc, g

    for t in [gst, S_in, r_run, SCR, rb, BzT,
              identb, Mm_sb, d_tab, cdec_tab, lam_tab, cdecl, zer16, lamv, lamlog,
              dtpos, dtT, decT, sdo_lh, cs_hl, cs_lh, aT_all, Ytc, XB3, XB2, XB1, Gt, Dm]:
        b.release(t)
    for t in [psS, psT, psE, psB, psA]:
        b.release(t)
    for t in [dt_bf, a_sc, dtneg, C_fm, B_tok, B_fm, AX]:
        b.release(t)

    po = [b.psum("po%d" % i, [128, 512]) for i in range(8)]
    h1b = b.sbuf("h1b", [128, 16 * 1024], BF16, side="right")
    wstE = b.sbuf("wstE", [128, 1024], side="right")
    wsbE = b.sbuf("wsbE", [128, 1024], BF16, side="right")
    gstE = b.sbuf("gstE", [128, 2048], BF16, side="right")
    h_pre = b.sbuf("h_pre", [128, 16 * 1024], side="right")
    xrel = b.sbuf("xrel", [128, 1024], side="right")
    sqb = b.sbuf("sqb", [128, 1024], BF16)
    rmu = b.sbuf("rmu", [1, 1024])
    rvar = b.sbuf("rvar", [1, 1024])
    mub = b.sbuf("mub", [128, 1024])
    ivb = b.sbuf("ivb", [128, 1024])
    rtmp = b.sbuf("rtmp", [1, 1024])

    t_h1 = tap("t_h1", [D, T])

    last_ev = None
    for slab in range(4):
        last_mm = None
        mm_hist = []
        for k in range(32):
            par = k % 2
            lk = mm_hist[-2] if len(mm_hist) >= 2 else (last_ev if last_ev else None)
            wl = b.step("sync", (lambda slab, k, par: lambda e: e.dma_start(
                out=wstE[:, par * 512:(par + 1) * 512],
                in_=g["w_out"][k * 128:(k + 1) * 128, tsl(slab, 512)]))(slab, k, par),
                deps=[lk] if lk else [], dma=True)
            wc = b.step("vector", (lambda par: lambda e: e.tensor_copy(
                out=wsbE[:, par * 512:(par + 1) * 512],
                in_=wstE[:, par * 512:(par + 1) * 512]))(par),
                deps=[wl] + ([lk] if lk else []))
            gl = b.step("gpsimd", (lambda k, par: lambda e: e.dma_start(
                out=gstE[:, par * 1024:(par + 1) * 1024],
                in_=g["gated_dram"][k * 128:(k + 1) * 128, :]))(k, par),
                deps=[lk] if lk else [], dma=True)

            def f_mm(e, slab=slab, k=k, par=par):
                last = None
                for oc in range(4):
                    for th in range(2):
                        last = e.matmul(po[oc * 2 + th][:, :],
                                        wsbE[:, par * 512 + oc * 128:par * 512 + oc * 128 + 128],
                                        gstE[:, par * 1024 + th * 512:par * 1024 + th * 512 + 512],
                                        start=(k == 0), stop=(k == 31))
                return last
            deps = [wc, gl]
            if last_mm is not None:
                deps.append(last_mm)
            if k == 0 and last_ev is not None:
                deps.append(last_ev)
            last_mm = b.step("tensor", f_mm, deps=deps)
            mm_hist.append(last_mm)
        for oc in range(4):
            dt_i = slab * 4 + oc
            xdeps = [last_mm]
            if last_ev is not None:
                xdeps.append(last_ev)
            xld = b.step("sync", (lambda dt_i: lambda e: e.dma_start(
                out=xrel[:, :], in_=g["x_in"][tsl(dt_i, 128), 4:4 + T]))(dt_i),
                deps=xdeps, dma=True)
            for th in range(2):
                ev = b.step("vector", (lambda dt_i, oc, th: lambda e: e.tensor_tensor(
                    out=h_pre[:, dt_i * 1024 + th * 512:dt_i * 1024 + th * 512 + 512],
                    in0=po[oc * 2 + th][:, :], in1=xrel[:, th * 512:(th + 1) * 512],
                    op=OP.add))(dt_i, oc, th), deps=[xld, last_mm])
                last_ev = ev

    hb_c = None
    for i in range(16):
        hb_c = b.step("vector", (lambda i: lambda e: e.tensor_copy(
            out=h1b[:, tsl(i, 1024)], in_=h_pre[:, tsl(i, 1024)]))(i), deps=[last_ev])
    for t in reversed(po):
        b.release(t)
    psq = b.psum("psq", [1, 1024])
    ps1 = b.psum("ps1", [1, 1024])
    psb2 = b.psum("psb2", [128, 512])
    ones1c = b.sbuf("ones1c", [128, 1], BF16)
    ones1f = b.sbuf("ones1f", [128, 1])
    c_o1 = b.step("vector", lambda e: e.memset(ones1c[:], 1.0), deps=[])
    c_o1f = b.step("vector", lambda e: e.memset(ones1f[:], 1.0), deps=[])

    def ln_stats(src_bf, dep, nfeat=D):
        mm_prev = None
        for i in range(16):
            c2 = b.step("scalar", (lambda i: lambda e: e.activation(
                out=sqb[:], in_=src_bf[:, tsl(i, 1024)], func=AF.Square))(i),
                deps=[dep] + ([mm_prev] if mm_prev else []))
            mm_prev = b.step("tensor", (lambda i: lambda e: [
                e.matmul(psq[0:1, 0:512], ones1c[:], sqb[:, 0:512],
                         start=(i == 0), stop=(i == 15)),
                e.matmul(psq[0:1, 512:1024], ones1c[:], sqb[:, 512:1024],
                         start=(i == 0), stop=(i == 15)),
                e.matmul(ps1[0:1, 0:512], ones1c[:], src_bf[:, i * 1024:i * 1024 + 512],
                         start=(i == 0), stop=(i == 15)),
                e.matmul(ps1[0:1, 512:1024], ones1c[:], src_bf[:, i * 1024 + 512:(i + 1) * 1024],
                         start=(i == 0), stop=(i == 15))][-1])(i),
                deps=[c2, c_o1])
        s1 = b.step("vector", lambda e: e.tensor_scalar_mul(rmu[:], ps1[0:1, :], 1.0 / nfeat),
                    deps=[mm_prev])
        s2 = b.step("vector", lambda e: e.tensor_scalar_mul(rvar[:], psq[0:1, :], 1.0 / nfeat),
                    deps=[s1])
        s3 = b.step("vector", lambda e: e.tensor_tensor(out=rtmp[:], in0=rmu[:], in1=rmu[:],
                                                        op=OP.mult), deps=[s2])
        s4 = b.step("vector", lambda e: e.tensor_tensor(out=rvar[:], in0=rvar[:], in1=rtmp[:],
                                                        op=OP.subtract), deps=[s3])
        s5 = b.step("scalar", lambda e: e.activation(out=rtmp[:], in_=rvar[:], func=AF.Ln,
                                                     bias=epsr[:], scale=1.0), deps=[s4])
        s6 = b.step("scalar", lambda e: e.activation(out=rvar[:], in_=rtmp[:], func=AF.Exp,
                                                     scale=mhalfr[:]), deps=[s5])
        bdep = [s6]
        for (row, dst) in ((rmu, mub), (rvar, ivb)):
            for j in range(2):
                mmx = b.step("tensor", (lambda row, j: lambda e: e.matmul(
                    psb2[:, :], onesr[:], row[:, j * 512:(j + 1) * 512],
                    start=True, stop=True))(row, j), deps=bdep)
                evx = b.step("vector", (lambda dst, j: lambda e: e.tensor_copy(
                    out=dst[:, j * 512:(j + 1) * 512], in_=psb2[:, :]))(dst, j), deps=[mmx])
                bdep = [evx]
        return bdep[0]

    lnst = ln_stats(h1b, hb_c)
    last_ap = lnst
    t_h1 = g.get("t_h1")
    for i in range(16):
        a1 = b.step("vector", (lambda i: lambda e: e.tensor_tensor(
            out=h_pre[:, tsl(i, 1024)], in0=h_pre[:, tsl(i, 1024)], in1=mub[:],
            op=OP.subtract))(i), deps=[last_ap])
        a2 = b.step("vector", (lambda i: lambda e: e.tensor_tensor(
            out=h_pre[:, tsl(i, 1024)], in0=h_pre[:, tsl(i, 1024)], in1=ivb[:],
            op=OP.mult))(i), deps=[a1])
        a3 = b.step("vector", (lambda i: lambda e: e.tensor_copy(
            out=h1b[:, tsl(i, 1024)], in_=h_pre[:, tsl(i, 1024)]))(i), deps=[a2])
        a4 = b.step("gpsimd", (lambda i: lambda e: e.dma_start(
            out=g["h1_dram"][tsl(i, 128), :], in_=h_pre[:, tsl(i, 1024)]))(i),
            deps=[a3], dma=True)
        if t_h1 is not None:
            b.step("gpsimd", (lambda i: lambda e: e.dma_start(
                out=g["t_h1"][tsl(i, 128), :], in_=h_pre[:, tsl(i, 1024)]))(i),
                deps=[a3], dma=True)
        last_ap = a4

    bar_E = b.barrier()
    if upto == 'E':
        b.finalize(); b.close(); return nc, g

    for t in [xrel, h_pre, gstE, wsbE, wstE]:
        b.release(t)
    b.release(psb2)
    b.release(ps1)
    b.release(psq)
    A1 = b.sbuf("A1", [128, 8 * NH], BF16)
    invb2 = b.sbuf("invb2", [128, 1])
    czb2 = b.sbuf("czb2", [128, 1])
    czr = b.sbuf("czr", [1, 2])
    zst2 = b.sbuf("zst2", [128, 256], side="right")
    sabs = b.sbuf("sabs", [128, 1], side="right")
    srow = b.sbuf("srow", [1, 4], side="right")
    invb = b.sbuf("invb", [128, 1], side="right")
    czrow = b.sbuf("czrow", [1, 4], side="right")
    czb = b.sbuf("czb", [128, 1], side="right")
    bsrow = b.sbuf("bsrow", [1, 256], side="right")
    wstF = b.sbuf("wstF", [128, 2048], side="right")
    sgn1 = b.sbuf("sgn1", [128, 2048], BF16, side="right")
    tern = b.sbuf("tern", [128, 4096], BF16, side="right")
    pz = [b.psum("pz%d" % i, [128, 512]) for i in range(8)]

    t_s = tap("t_s", [1, 4])
    t_z = tap("t_z", [T, NH])
    t_thr = tap("t_thr", [128, 8 * 4])

    b._accs = []

    def scale_pass3(wname, rows, cols, dep0, sidx):
        accv = b.sbuf("acc_" + wname, [128, 1], side="right")
        b._accs.append(accv)
        sa1 = b.step("vector", lambda e: e.memset(accv[:], 0.0), deps=[dep0])
        prev = [sa1]
        for rb_ in range(rows // 128):
            for cb_ in range(cols // 2048):
                ld = b.step("sync", (lambda rb_, cb_: lambda e: e.dma_start(
                    out=wstF[:, :],
                    in_=g[wname][rb_ * 128:(rb_ + 1) * 128,
                                 cb_ * 2048:(cb_ + 1) * 2048]))(rb_, cb_),
                    deps=prev, dma=True)
                ab = b.step("scalar", lambda e: e.activation(
                    out=wstF[:, :], in_=wstF[:, :], func=AF.Abs, accum_out=sabs[:]),
                    deps=[ld])
                ac = b.step("vector", lambda e: e.tensor_tensor(
                    out=accv[:], in0=accv[:], in1=sabs[:], op=OP.add), deps=[ab])
                prev = [ac, ab]
        mmt = b.step("tensor", lambda e: e.matmul(
            pz[0][0:1, 0:1], ones1f[:], accv[:], start=True, stop=True),
            deps=prev + [c_o1f])
        evt = b.step("vector", (lambda sidx: lambda e: e.tensor_scalar(
            out=srow[0:1, sidx:sidx + 1], in0=pz[0][0:1, 0:1],
            scalar1=1.0 / (rows * cols), scalar2=None, op0=OP.mult))(sidx), deps=[mmt])
        return evt

    sp1 = b.step("vector", lambda e: e.memset(srow[:], 0.0), deps=[])
    e1 = scale_pass3("w_syn", D, NH, sp1, 0)
    e2 = scale_pass3("w_o2", NH, D, e1, 1)
    sprep = b.step("vector", lambda e: e.tensor_scalar(
        out=czrow[0:1, 0:2], in0=srow[0:1, 0:2], scalar1=0.5, scalar2=None,
        op0=OP.mult), deps=[e2])
    s_eps = b.step("vector", lambda e: e.tensor_scalar(
        out=srow[0:1, 2:4], in0=srow[0:1, 0:2], scalar1=1e-8, scalar2=None, op0=OP.add),
        deps=[sprep])
    s_rec = b.step("vector", lambda e: e.reciprocal(out=srow[0:1, 2:4], in_=srow[0:1, 2:4]),
                   deps=[s_eps])
    if t_s is not None:
        b.step("gpsimd", lambda e: e.dma_start(out=g["t_s"][:, :], in_=srow[:, :]),
               deps=[s_rec], dma=True)
    s_rc2 = b.step("vector", lambda e: e.reciprocal(out=czr[:, 0:2], in_=czrow[0:1, 0:2]),
                   deps=[sprep, s_rec])
    bc_pairs = [(srow[0:1, 2:3], invb), (srow[0:1, 3:4], invb2),
                (czrow[0:1, 0:1], czb), (czrow[0:1, 1:2], czb2)]
    bdep = [s_rc2]
    for (rw, dst) in bc_pairs:
        mmi = b.step("tensor", (lambda rw: lambda e: e.matmul(
            pz[1][:, 0:1], onesr[:], rw, start=True, stop=True))(rw), deps=bdep)
        evi = b.step("vector", (lambda dst: lambda e: e.tensor_copy(
            out=dst[:], in_=pz[1][:, 0:1]))(dst), deps=[mmi])
        bdep = [evi]
    sc_done = bdep[0]

    last_ev = sc_done
    last_st = None
    last_mmz = None
    last_sg = None
    prev_halves = []
    for og in range(32):
        halves = []
        for hh in range(2):
            wdeps = [last_sg] if last_sg else [sc_done]
            wl = b.step("sync", (lambda og, hh: lambda e: e.dma_start(
                out=wstF[:, :].rearrange("p (k f) -> p k f", k=8),
                in_=g["w_syn"][hh * 1024:(hh + 1) * 1024, tsl(og, 256)]
                .rearrange("(k p) f -> p k f", p=128)))(og, hh),
                deps=wdeps, dma=True)
            sdeps = [wl, sc_done]
            if last_mmz is not None:
                sdeps.append(last_mmz)
            if halves:
                sdeps.append(halves[-1])
            elif prev_halves:
                sdeps.append(prev_halves[-1])
            sg = b.step("scalar", (lambda hh: lambda e: [
                e.activation(out=sgn1[:, :], in_=wstF[:, :], func=AF.Sign,
                             bias=halfn[:], scale=invb[:]),
                e.activation(out=tern[:, hh * 2048:(hh + 1) * 2048], in_=wstF[:, :],
                             func=AF.Sign, bias=halfp[:], scale=invb[:])][-1])(hh),
                deps=sdeps)
            last_sg = sg
            tn = b.step("vector", (lambda hh: lambda e: e.tensor_tensor(
                out=tern[:, hh * 2048:(hh + 1) * 2048], in0=sgn1[:, :],
                in1=tern[:, hh * 2048:(hh + 1) * 2048], op=OP.add))(hh), deps=[sg])
            halves.append(tn)
            last_ev = tn
        bl = b.step("sync", (lambda og: lambda e: e.dma_start(
            out=bsrow[0:1, 0:256], in_=g["b_syn"].rearrange("(one n) -> one n", one=1)
            [:, tsl(og, 256)]))(og), deps=[last_ev], dma=True)
        bs = b.step("vector", lambda e: e.tensor_scalar(
            out=bsrow[0:1, 0:256], in0=bsrow[0:1, 0:256], scalar1=czr[0:1, 0:1],
            scalar2=None, op0=OP.mult), deps=[bl, s_rc2])

        def f_mm(e, og=og):
            last = None
            for tt in range(8):
                for k in range(16):
                    kk, khh = k % 8, k // 8
                    last = e.matmul(pz[tt][:, 0:256],
                                    h1b[:, k * 1024 + tt * 128:k * 1024 + tt * 128 + 128],
                                    tern[:, khh * 2048 + kk * 256:khh * 2048 + (kk + 1) * 256],
                                    start=(k == 0), stop=False)
                last = e.matmul(pz[tt][:, 0:256], onesr[0:1, :], bsrow[0:1, 0:256],
                                start=False, stop=True)
            return last
        mm_deps = halves + [bs]
        if last_st is not None:
            mm_deps.append(last_st)
        mmz = b.step("tensor", f_mm, deps=mm_deps)
        last_mmz = mmz
        prev_halves = halves
        st_hist = []
        for tt in range(8):
            zp = 0
            sdep = last_st
            evz = b.step("vector", (lambda tt, og, zp: lambda e: [
                e.tensor_scalar(out=zst2[:, zp:zp + 256], in0=pz[tt][:, 0:256], scalar1=czb[:],
                                scalar2=None, op0=OP.mult),
                e.tensor_copy(out=A1[:, tt * NH + og * 256:tt * NH + (og + 1) * 256],
                              in_=zst2[:, zp:zp + 256])][-1])(tt, og, zp),
                deps=[mmz] + ([sdep] if sdep else []))
            stz = b.step("gpsimd", (lambda tt, og, zp: lambda e: e.dma_start(
                out=g["z_dram"][tt * 128:(tt + 1) * 128, tsl(og, 256)],
                in_=zst2[:, zp:zp + 256]))(tt, og, zp), deps=[evz], dma=True)
            st_hist.append(stz)
            last_st = stz
    if t_z is not None:
        for tt in range(8):
            b.step("gpsimd", (lambda tt: lambda e: e.dma_start(
                out=g["t_z"][tt * 128:(tt + 1) * 128, :],
                in_=g["z_dram"][tt * 128:(tt + 1) * 128, :]))(tt), deps=[last_st], dma=True)

    bar_F1 = b.barrier()
    if upto == 'F1':
        b.finalize(); b.close(); return nc, g

    for t in [b._accs[1], b._accs[0], tern, sgn1, wstF, bsrow, czb, czrow, invb,
              srow, sabs, zst2, h1b]:
        b.release(t)
    msk = b.sbuf("msk", [128, NH], mybir.dt.float8e4, side="right")
    lo_t = b.sbuf("lo_t", [128, 8], side="right")
    hi_t = b.sbuf("hi_t", [128, 8], side="right")
    mid_t = b.sbuf("mid_t", [128, 8], side="right")
    cnt_t = b.sbuf("cnt_t", [128, 8], side="right")
    cge_t = b.sbuf("cge_t", [128, 8], mybir.dt.uint8, side="right")
    Tuse = b.sbuf("Tuse", [128, 8], side="right")
    Tst = b.sbuf("Tst", [128, 8], side="right")
    c2_t = b.sbuf("c2_t", [128, 8], side="right")
    jr_t = b.sbuf("jr_t", [128, 8], side="right")
    mr_t = b.sbuf("mr_t", [128, 8], mybir.dt.uint8, side="right")
    iot = b.sbuf("iot", [128, 32], side="right")
    zfs = b.sbuf("zfs", [128, 2048], side="right")
    zneg = b.sbuf("zneg", [128, 2048], side="right")
    hidb = b.sbuf("hidb", [128, 2048], BF16, side="right")
    cand2 = b.sbuf("cand2", [128, 8 * 512], side="right")
    srt32 = b.sbuf("srt32", [128, 8 * 32], side="right")

    c_io = b.step("sync", lambda e: e.dma_start(out=iot[:], in_=g["iota32"][:, :]), dma=True)
    sel0 = b.step("vector", lambda e: e.memset(lo_t[:], 0.0), deps=[])
    prevh = sel0
    for tt in range(8):
        prevh = b.step("vector", (lambda tt: lambda e: e.reduce_max(
            out=hi_t[:, tt:tt + 1], in_=A1[:, tt * NH:(tt + 1) * NH],
            axis=AXL.X))(tt), deps=[prevh])
    selh = b.step("vector", lambda e: e.tensor_scalar(
        out=hi_t[:], in0=hi_t[:], scalar1=1.0, scalar2=None, op0=OP.add), deps=[prevh])

    ITERS = 14
    prev_it = [selh, sel0]
    for it in range(ITERS):
        m1a = b.step("vector", lambda e: e.tensor_tensor(
            out=mid_t[:], in0=lo_t[:], in1=hi_t[:], op=OP.add), deps=prev_it)
        m1 = b.step("vector", lambda e: e.tensor_scalar_mul(mid_t[:], mid_t[:], 0.5),
                    deps=[m1a])
        cts = []
        for tt in range(8):
            ct = b.step("vector", (lambda tt: lambda e: e.tensor_scalar(
                out=msk[:, :], in0=A1[:, tt * NH:(tt + 1) * NH],
                scalar1=mid_t[:, tt:tt + 1], scalar2=None,
                op0=OP.is_ge, op1=OP.add, accum_out=cnt_t[:, tt:tt + 1]))(tt), deps=[m1])
            cts.append(ct)
        u1 = b.step("vector", lambda e: e.tensor_scalar(
            out=cge_t[:], in0=cnt_t[:], scalar1=float(KTOP) - 0.5,
            scalar2=None, op0=OP.is_ge), deps=cts)
        u2 = b.step("vector", lambda e: e.select(
            out=lo_t[:], mask=cge_t[:], on_true=mid_t[:], on_false=lo_t[:]), deps=[u1])
        up = b.step("vector", lambda e: e.select(
            out=hi_t[:], mask=cge_t[:], on_true=hi_t[:], on_false=mid_t[:]), deps=[u2])
        prev_it = [up]

    selp = prev_it[0]
    c20 = b.step("vector", lambda e: e.memset(c2_t[:], 0.0), deps=[selp])
    prevs = c20
    for tt in range(8):
        for cc in range(4):
            zl2 = b.step("gpsimd", (lambda tt, cc: lambda e: e.dma_start(
                out=zfs[:, :], in_=g["z_dram"][tt * 128:(tt + 1) * 128, tsl(cc, 2048)]))(tt, cc),
                deps=[prevs], dma=True)
            cpa = b.step("vector", (lambda tt: lambda e: e.tensor_scalar(
                out=msk[:, 0:2048], in0=zfs[:, :],
                scalar1=hi_t[:, tt:tt + 1], scalar2=None,
                op0=OP.is_ge, op1=OP.add, accum_out=cnt_t[:, tt:tt + 1]))(tt), deps=[zl2])
            cpb = b.step("vector", (lambda tt: lambda e: e.tensor_tensor(
                out=c2_t[:, tt:tt + 1], in0=c2_t[:, tt:tt + 1],
                in1=cnt_t[:, tt:tt + 1], op=OP.add))(tt), deps=[cpa])
            cp = b.step("vector", (lambda tt: lambda e: e.tensor_scalar(
                out=zneg[:, :], in0=zfs[:, :],
                scalar1=hi_t[:, tt:tt + 1], scalar2=-BIG,
                op0=OP.is_ge, op1=OP.mult))(tt), deps=[cpb])
            cp2 = b.step("vector", lambda e: e.tensor_tensor(
                out=zfs[:, :], in0=zfs[:, :], in1=zneg[:, :], op=OP.add), deps=[cp])

            def f_m8(e, tt=tt, cc=cc):
                last = None
                for w in range(16):
                    last = e.max(out=cand2[:, tt * 512 + cc * 128 + w * 8:
                                          tt * 512 + cc * 128 + w * 8 + 8],
                                 in_=zfs[:, w * 128:(w + 1) * 128])
                return last
            prevs = b.step("vector", f_m8, deps=[cp2])

    rnd_prev = prevs
    for rnd in range(4):
        for tt in range(8):
            mx = b.step("vector", (lambda rnd, tt: lambda e: e.max(
                out=srt32[:, tt * 32 + rnd * 8:tt * 32 + rnd * 8 + 8],
                in_=cand2[:, tsl(tt, 512)]))(rnd, tt), deps=[rnd_prev])
            if rnd < 3:
                rma = b.step("vector", (lambda rnd, tt: lambda e: e.tensor_scalar(
                    out=zfs[:, 0:512], in0=cand2[:, tsl(tt, 512)],
                    scalar1=srt32[:, tt * 32 + rnd * 8 + 7:tt * 32 + rnd * 8 + 8],
                    scalar2=-BIG, op0=OP.is_ge, op1=OP.mult))(rnd, tt), deps=[mx])
                rm = b.step("vector", (lambda rnd, tt: lambda e: e.tensor_tensor(
                    out=cand2[:, tsl(tt, 512)], in0=cand2[:, tsl(tt, 512)],
                    in1=zfs[:, 0:512], op=OP.add))(rnd, tt), deps=[rma])
                rnd_prev = rm
            else:
                rnd_prev = mx
    r1 = b.step("vector", lambda e: e.tensor_scalar(
        out=jr_t[:], in0=c2_t[:], scalar1=-1.0, scalar2=float(KTOP - 1),
        op0=OP.mult, op1=OP.add), deps=[rnd_prev])
    r2 = None
    for tt in range(8):
        r2a = b.step("vector", (lambda tt: lambda e: e.tensor_scalar(
            out=zfs[:, 0:32], in0=iot[:], scalar1=jr_t[:, tt:tt + 1],
            scalar2=None, op0=OP.is_equal))(tt), deps=[r1, c_io] + ([r2] if r2 else []))
        r2b = b.step("vector", (lambda tt: lambda e: e.tensor_tensor(
            out=zfs[:, 32:64], in0=zfs[:, 0:32],
            in1=srt32[:, tsl(tt, 32)], op=OP.mult))(tt), deps=[r2a])
        r2 = b.step("vector", (lambda tt: lambda e: e.reduce_sum(
            out=Tst[:, tt:tt + 1], in_=zfs[:, 32:64], axis=AXL.X))(tt), deps=[r2b])
    r3a = b.step("vector", lambda e: e.tensor_scalar(
        out=mr_t[:], in0=jr_t[:], scalar1=-0.5, scalar2=None, op0=OP.is_ge), deps=[r2])
    r3b = b.step("vector", lambda e: e.tensor_scalar(
        out=cge_t[:], in0=jr_t[:], scalar1=31.5, scalar2=None, op0=OP.is_le), deps=[r3a])
    r3c = b.step("vector", lambda e: e.tensor_tensor(
        out=mr_t[:], in0=mr_t[:], in1=cge_t[:], op=OP.mult), deps=[r3b])
    r3 = b.step("vector", lambda e: e.select(
        out=Tuse[:], mask=mr_t[:], on_true=Tst[:], on_false=hi_t[:]), deps=[r3c])
    if t_thr is not None:
        b.step("gpsimd", lambda e: e.dma_start(out=g["t_thr"][:, 0:8], in_=Tuse[:, :]),
               deps=[r3], dma=True)
        b.step("gpsimd", lambda e: e.dma_start(out=g["t_thr"][:, 8:16], in_=c2_t[:, :]),
               deps=[r3], dma=True)
        b.step("gpsimd", lambda e: e.dma_start(out=g["t_thr"][:, 16:24], in_=lo_t[:, :]),
               deps=[r3], dma=True)
        b.step("gpsimd", lambda e: e.dma_start(out=g["t_thr"][:, 24:32], in_=hi_t[:, :]),
               deps=[r3], dma=True)

    b.release(srt32)
    b.release(cand2)
    hid_prev = r3
    for tt in range(8):
        for cc in range(4):
            zl3 = b.step("gpsimd", (lambda tt, cc: lambda e: e.dma_start(
                out=zfs[:, :], in_=g["z_dram"][tt * 128:(tt + 1) * 128, tsl(cc, 2048)]))(tt, cc),
                deps=[hid_prev], dma=True)
            hm = b.step("vector", (lambda tt: lambda e: e.scalar_tensor_tensor(
                out=zneg[:, :], in0=zfs[:, :], scalar=Tuse[:, tt:tt + 1], in1=zfs[:, :],
                op0=OP.is_ge, op1=OP.mult))(tt), deps=[zl3, r3])
            hr = b.step("scalar", lambda e: e.activation(out=zneg[:, :], in_=zneg[:, :],
                                                         func=AF.Relu), deps=[hm])

            def f_trh(e, tt=tt, cc=cc):
                insts = []
                for bb in range(16):
                    pzi = pz[bb % 4]
                    insts.append(e.transpose(pzi[:, (bb // 4) * 128:(bb // 4) * 128 + 128],
                                             zneg[:, bb * 128:(bb + 1) * 128], ident[:]))
                return insts[-1]
            trh = b.step("tensor", f_trh, deps=[hr, hid_prev])
            def f_evh(e, tt=tt, cc=cc):
                last = None
                for bb in range(16):
                    kk = cc * 16 + bb
                    pzi = pz[bb % 4]
                    last = e.tensor_copy(
                        out=A1[:, kk * 1024 + tt * 128:kk * 1024 + tt * 128 + 128],
                        in_=pzi[:, (bb // 4) * 128:(bb // 4) * 128 + 128])
                return last
            hid_prev = b.step("vector", f_evh, deps=[trh])

    bar_F2 = b.barrier()
    if upto == 'F':
        b.finalize(); b.close(); return nc, g

    for t in [hidb, zneg, zfs, iot, mr_t, jr_t, c2_t, Tst, Tuse, cge_t, cnt_t,
              mid_t, hi_t, lo_t, msk]:
        b.release(t)
    v2b = b.sbuf("v2b", [128, 16 * 1024], BF16)
    h1rel = b.sbuf("h1rel", [128, 1024], side="right")
    wstG = b.sbuf("wstG", [128, 1024], side="right")
    ternG = b.sbuf("ternG", [128, 1024], BF16, side="right")
    sgnG = b.sbuf("sgnG", [128, 512], BF16, side="right")
    borow = b.sbuf("borow", [1, 512], side="right")
    ones512 = b.sbuf("ones512", [1, 512], side="right")
    c_o512 = b.step("vector", lambda e: e.memset(ones512[:], 1.0), deps=[])

    t_v2 = tap("t_v2", [D, T])

    last_evG = None
    prev_slab_mm = None
    for slab in range(4):
        bl = b.step("sync", (lambda slab: lambda e: e.dma_start(
            out=borow[0:1, :], in_=g["b_out"].rearrange("(one n) -> one n", one=1)
            [:, tsl(slab, 512)]))(slab),
            deps=[prev_slab_mm] if prev_slab_mm else [], dma=True)
        bs = b.step("vector", lambda e: e.tensor_scalar(
            out=borow[0:1, :], in0=borow[0:1, :], scalar1=czr[0:1, 1:2],
            scalar2=None, op0=OP.mult), deps=[bl, s_rc2])
        last_mm = None
        mm_histG = []
        prev_tn = None
        for k in range(64):
            par = k % 2
            lk = mm_histG[-2] if len(mm_histG) >= 2 else (last_evG if last_evG else None)
            wl = b.step("sync", (lambda slab, k, par: lambda e: e.dma_start(
                out=wstG[:, par * 512:par * 512 + 512],
                in_=g["w_o2"][k * 128:(k + 1) * 128, tsl(slab, 512)]))(slab, k, par),
                deps=[lk] if lk else [], dma=True)
            sg = b.step("scalar", (lambda par: lambda e: [
                e.activation(out=sgnG[:, 0:512],
                             in_=wstG[:, par * 512:par * 512 + 512], func=AF.Sign,
                             bias=halfn[:], scale=invb2[:]),
                e.activation(out=ternG[:, par * 512:par * 512 + 512],
                             in_=wstG[:, par * 512:par * 512 + 512], func=AF.Sign,
                             bias=halfp[:], scale=invb2[:])][-1])(par),
                deps=[wl] + ([lk] if lk else []) + ([prev_tn] if prev_tn else []))
            tn = b.step("vector", (lambda par: lambda e: e.tensor_tensor(
                out=ternG[:, par * 512:par * 512 + 512],
                in0=sgnG[:, 0:512],
                in1=ternG[:, par * 512:par * 512 + 512], op=OP.add))(par), deps=[sg])
            prev_tn = tn

            def f_mm(e, slab=slab, k=k, par=par):
                last = None
                for oc in range(4):
                    for th in range(2):
                        last = e.matmul(pz[oc * 2 + th][:, :],
                                        ternG[:, par * 512 + oc * 128:par * 512 + (oc + 1) * 128],
                                        A1[:, k * 1024 + th * 512:k * 1024 + th * 512 + 512],
                                        start=(k == 0), stop=False)
                if k == 63:
                    for oc in range(4):
                        for th in range(2):
                            last = e.matmul(pz[oc * 2 + th][:, :],
                                            borow[0:1, oc * 128:(oc + 1) * 128],
                                            ones512[0:1, 0:512],
                                            start=False, stop=True)
                return last
            deps = [tn, bs, c_o512]
            if last_mm is not None:
                deps.append(last_mm)
            if k == 0 and last_evG is not None:
                deps.append(last_evG)
            last_mm = b.step("tensor", f_mm, deps=deps)
            mm_histG.append(last_mm)
        prev_slab_mm = last_mm
        for oc in range(4):
            dt_i = slab * 4 + oc
            hdeps = [last_mm]
            if last_evG is not None:
                hdeps.append(last_evG)
            hl = b.step("gpsimd", (lambda dt_i: lambda e: e.dma_start(
                out=h1rel[:, :], in_=g["h1_dram"][tsl(dt_i, 128), :]))(dt_i),
                deps=hdeps, dma=True)
            for th in range(2):
                ev = b.step("vector", (lambda dt_i, oc, th: lambda e: e.scalar_tensor_tensor(
                    out=v2b[:, dt_i * 1024 + th * 512:dt_i * 1024 + th * 512 + 512],
                    in0=pz[oc * 2 + th][:, :], scalar=czb2[:],
                    in1=h1rel[:, th * 512:(th + 1) * 512],
                    op0=OP.mult, op1=OP.add))(dt_i, oc, th), deps=[hl, last_mm])
                last_evG = ev
    if t_v2 is not None:
        for i in range(16):
            b.step("gpsimd", (lambda i: lambda e: e.dma_start(
                out=g["t_v2"][tsl(i, 128), :], in_=v2b[:, tsl(i, 1024)]))(i),
                deps=[last_evG], dma=True)

    for t in [ones512, borow, sgnG, ternG, wstG]:
        b.release(t)
    ysto = b.sbuf("ysto", [128, 1024], side="right")
    lnscr = b.sbuf("lnscr", [128, 1024], side="right")
    for t in reversed(pz):
        b.release(t)
    psq2 = b.psum("psq2", [1, 1024])
    ps12 = b.psum("ps12", [1, 1024])
    psb22 = b.psum("psb22", [128, 512])

    def ln_stats2(src_bf, dep):
        mm_prev = None
        for i in range(16):
            c2 = b.step("scalar", (lambda i: lambda e: e.activation(
                out=sqb[:], in_=src_bf[:, tsl(i, 1024)], func=AF.Square))(i),
                deps=[dep] + ([mm_prev] if mm_prev else []))
            mm_prev = b.step("tensor", (lambda i: lambda e: [
                e.matmul(psq2[0:1, 0:512], ones1c[:], sqb[:, 0:512],
                         start=(i == 0), stop=(i == 15)),
                e.matmul(psq2[0:1, 512:1024], ones1c[:], sqb[:, 512:1024],
                         start=(i == 0), stop=(i == 15)),
                e.matmul(ps12[0:1, 0:512], ones1c[:], src_bf[:, i * 1024:i * 1024 + 512],
                         start=(i == 0), stop=(i == 15)),
                e.matmul(ps12[0:1, 512:1024], ones1c[:],
                         src_bf[:, i * 1024 + 512:(i + 1) * 1024],
                         start=(i == 0), stop=(i == 15))][-1])(i),
                deps=[c2, c_o1])
        s1 = b.step("vector", lambda e: e.tensor_scalar_mul(rmu[:], ps12[0:1, :], 1.0 / D),
                    deps=[mm_prev])
        s2 = b.step("vector", lambda e: e.tensor_scalar_mul(rvar[:], psq2[0:1, :], 1.0 / D),
                    deps=[s1])
        s3 = b.step("vector", lambda e: e.tensor_tensor(out=rtmp[:], in0=rmu[:], in1=rmu[:],
                                                        op=OP.mult), deps=[s2])
        s4 = b.step("vector", lambda e: e.tensor_tensor(out=rvar[:], in0=rvar[:], in1=rtmp[:],
                                                        op=OP.subtract), deps=[s3])
        s5 = b.step("scalar", lambda e: e.activation(out=rtmp[:], in_=rvar[:], func=AF.Ln,
                                                     bias=epsr[:], scale=1.0), deps=[s4])
        s6 = b.step("scalar", lambda e: e.activation(out=rvar[:], in_=rtmp[:], func=AF.Exp,
                                                     scale=mhalfr[:]), deps=[s5])
        bdep = [s6]
        for (row, dst) in ((rmu, mub), (rvar, ivb)):
            for j in range(2):
                mmx = b.step("tensor", (lambda row, j: lambda e: e.matmul(
                    psb22[:, :], onesr[:], row[:, j * 512:(j + 1) * 512],
                    start=True, stop=True))(row, j), deps=bdep)
                evx = b.step("vector", (lambda dst, j: lambda e: e.tensor_copy(
                    out=dst[:, j * 512:(j + 1) * 512], in_=psb22[:, :]))(dst, j), deps=[mmx])
                bdep = [evx]
        return bdep[0]

    ln2 = ln_stats2(v2b, last_evG)
    lastg = ln2
    for i in range(16):
        hl = b.step("gpsimd", (lambda i: lambda e: e.dma_start(
            out=h1rel[:, :], in_=g["h1_dram"][tsl(i, 128), :]))(i), deps=[lastg], dma=True)
        g1 = b.step("vector", (lambda i: lambda e: [
            e.tensor_scalar(out=lnscr[:, :], in0=ivb[:, :], scalar1=gam[:, i:i + 1],
                            scalar2=None, op0=OP.mult),
            e.tensor_tensor(out=v2b[:, tsl(i, 1024)], in0=v2b[:, tsl(i, 1024)], in1=mub[:],
                            op=OP.subtract),
            e.tensor_tensor(out=v2b[:, tsl(i, 1024)], in0=v2b[:, tsl(i, 1024)], in1=lnscr[:],
                            op=OP.mult),
            e.tensor_scalar(out=v2b[:, tsl(i, 1024)], in0=v2b[:, tsl(i, 1024)],
                            scalar1=bet[:, i:i + 1], scalar2=None, op0=OP.add),
            e.tensor_tensor(out=v2b[:, tsl(i, 1024)], in0=v2b[:, tsl(i, 1024)], in1=h1rel[:],
                            op=OP.add)][-1])(i), deps=[hl, lastg])
        lastg = g1
    ln3 = ln_stats2(v2b, lastg)
    lasty = ln3
    for i in range(16):
        y1 = b.step("vector", (lambda i: lambda e: [
            e.tensor_tensor(out=ysto[:, :], in0=v2b[:, tsl(i, 1024)], in1=mub[:],
                            op=OP.subtract),
            e.tensor_tensor(out=ysto[:, :], in0=ysto[:, :], in1=ivb[:],
                            op=OP.mult)][-1])(i), deps=[lasty])
        yd = b.step("gpsimd", (lambda i: lambda e: e.dma_start(
            out=g["y_out"][tsl(i, 128), :], in_=ysto[:, :]))(i), deps=[y1], dma=True)
        lasty = yd
    b.barrier()
    b.finalize()
    b.close()
    return nc, g


def _prep_inputs(inputs):
    x = np.ascontiguousarray(inputs["L0_attn_x"])
    w_in = np.ascontiguousarray(inputs["L0_attn_in_proj_w"].T)
    w_xp = np.ascontiguousarray(inputs["L0_attn_x_proj_w"].T)
    w_dt = np.ascontiguousarray(inputs["L0_attn_dt_proj_w"].T)
    w_out = np.ascontiguousarray(inputs["L0_attn_out_proj_w"].T)
    w_syn = np.ascontiguousarray(inputs["L0_ffn_W_syn"].T)
    w_o2 = np.ascontiguousarray(inputs["L0_ffn_W_out"].T)
    common = dict(
        w_in=w_in, w_xp=w_xp, w_dt=w_dt, w_out=w_out, w_syn=w_syn, w_o2=w_o2,
        conv_w=np.ascontiguousarray(
            inputs["L0_attn_conv_w"].reshape(32, 128, 4).transpose(1, 0, 2).reshape(128, 128)),
        conv_b=np.ascontiguousarray(inputs["L0_attn_conv_b"].reshape(32, 128).T),
        b_dt=np.ascontiguousarray(inputs["L0_attn_dt_proj_b"]),
        a_log=np.ascontiguousarray(inputs["L0_attn_A_log"]),
        d_vec=np.ascontiguousarray(inputs["L0_attn_D"]),
        b_syn=np.ascontiguousarray(inputs["L0_ffn_b_syn"]),
        b_out=np.ascontiguousarray(inputs["L0_ffn_b_out"]),
        gamma=np.ascontiguousarray(inputs["L0_ffn_gamma"].reshape(16, 128).T),
        beta=np.ascontiguousarray(inputs["L0_ffn_beta"].reshape(16, 128).T),
        iota32=np.tile(np.arange(32, dtype=np.float32), (128, 1)),
        d_row=np.ascontiguousarray(inputs["L0_attn_D"].reshape(1, 64)),
    )
    in_maps = []
    for c in range(8):
        seq, half = c // 2, c % 2
        if half == 0:
            xs = np.concatenate([x[seq, 0:4], x[seq, 0:1024]], axis=0).T
            hm = np.zeros((128, 1), np.float32)
            rl = np.zeros((128, 1), np.float32)
        else:
            xs = x[seq, 1020:2048].T
            hm = np.ones((128, 1), np.float32)
            rl = np.ones((128, 1), np.float32)
        m = dict(common)
        m["x_in"] = np.ascontiguousarray(xs)
        m["halo_mask"] = hm
        m["role"] = rl
        in_maps.append(m)
    return in_maps


def kernel(**inputs):
    in_maps = _prep_inputs(inputs)
    nc, _ = build_nc(taps=DEBUG_TAPS)
    res = run_bass_kernel_spmd(nc, in_maps, core_ids=list(range(8)))
    out = np.empty((4, 2048, 2048), np.float32)
    for c in range(8):
        seq, half = c // 2, c % 2
        out[seq, half * 1024:(half + 1) * 1024, :] = res.results[c]["y_out"].T
    return out


# version 03
# speedup vs baseline: 1.3301x; 1.0879x over previous
import numpy as np
import concourse.bass as bass
import concourse.mybir as mybir
from concourse.bass_utils import run_bass_kernel_spmd

F32 = mybir.dt.float32
BF16 = mybir.dt.bfloat16
AF = mybir.ActivationFunctionType
OP = mybir.AluOpType
AXL = mybir.AxisListType

D = 2048
DI = 4096
H = 64
P = 64
NST = 128
CH = 64
NCH = 16
T = 1024
TH = T + 4
XP = 4416
NH = 8192
KTOP = 819
RG = [[0, 1], [2, 3], [4, 5], [6, 7]]
BIG = 1e30

DEBUG_TAPS = []


def tsl(i, n):
    return slice(i * n, (i + 1) * n)


class B:
    ENGINES = ("sync", "gpsimd", "tensor", "vector", "scalar")

    def __init__(self, nc):
        self.nc = nc
        self.ops = {e: [] for e in self.ENGINES}
        self.cm = []
        self.sems = {}
        self.tick = {}
        for e in self.ENGINES:
            c = nc.semaphore("tk_" + e)
            self.sems[e] = c.__enter__()
            self.cm.append(c)
            self.tick[e] = 0

    def sbuf(self, name, shape, dtype=F32, side="left"):
        c = self.nc.sbuf_tensor(name, shape, dtype, side=side)
        t = c.__enter__()
        self.cm.append(c)
        self.by_handle = getattr(self, "by_handle", {})
        self.by_handle[id(t)] = c
        return t

    def release(self, t):
        c = self.by_handle.pop(id(t))
        self.cm.remove(c)
        c.__exit__(None, None, None)

    def psum(self, name, shape, dtype=F32):
        c = self.nc.psum_tensor(name, shape, dtype)
        t = c.__enter__()
        self.cm.append(c)
        self.by_handle = getattr(self, "by_handle", {})
        self.by_handle[id(t)] = c
        return t

    def step(self, eng, fn, deps=(), dma=False):
        inc = 16 if dma else 1
        self.tick[eng] += inc
        val = self.tick[eng]
        sems = self.sems

        def run(e, fn=fn, deps=tuple(deps), eng=eng, val=val, dma=dma):
            for de, dv in deps:
                if de == eng and dv <= val - inc:
                    if eng in ("vector", "scalar", "gpsimd"):
                        e.wait_ge(sems[de], dv)
                    continue
                e.wait_ge(sems[de], dv)
            last = fn(e)
            if dma:
                assert last is not None, "dma step must return the dma instr"
                last.then_inc(sems[eng], 16)
            else:
                if last is not None:
                    last.then_inc(sems[eng], 1)
                else:
                    e.nop().then_inc(sems[eng], 1)
        self.ops[eng].append(run)
        return (eng, val)

    def barrier(self):
        snap = dict(self.tick)
        out = []
        for e in self.ENGINES:
            deps = [(o, snap[o]) for o in self.ENGINES if o != e]
            out.append(self.step(e, lambda ee: None, deps=deps))
        return dict(self.tick)

    def finalize(self):
        nc = self.nc
        with nc.Block() as block:
            @block.sync
            def _(e):
                for f in self.ops["sync"]:
                    f(e)

            @block.gpsimd
            def _(e):
                for f in self.ops["gpsimd"]:
                    f(e)

            @block.tensor
            def _(e):
                for f in self.ops["tensor"]:
                    f(e)

            @block.vector
            def _(e):
                for f in self.ops["vector"]:
                    f(e)

            @block.scalar
            def _(e):
                for f in self.ops["scalar"]:
                    f(e)

    def close(self):
        for c in reversed(self.cm):
            c.__exit__(None, None, None)


def build_nc(taps=(), upto='ALL', ncx=16):
    TAP = set(taps)
    nc = bass.Bass("TRN2", target_bir_lowering=False)
    b = B(nc)
    g = {}

    def din(name, shape, dtype=F32):
        g[name] = nc.dram_tensor(name, shape, dtype, kind="ExternalInput").ap()

    def dout(name, shape, dtype=F32):
        g[name] = nc.dram_tensor(name, shape, dtype, kind="ExternalOutput").ap()

    def dint(name, shape, dtype=F32):
        g[name] = nc.dram_tensor(name, shape, dtype, kind="Internal").ap()

    din("x_in", [D, TH])
    din("halo_mask", [128, 1])
    din("role", [128, 1])
    din("w_in", [D, 2 * DI])
    din("conv_w", [128, 128])
    din("conv_b", [128, 32])
    din("w_xp", [DI, XP])
    din("w_dt", [H, H])
    din("b_dt", [H])
    din("a_log", [H])
    din("d_vec", [H])
    din("w_out", [DI, D])
    din("w_syn", [D, NH])
    din("b_syn", [NH])
    din("w_o2", [NH, D])
    din("b_out", [D])
    din("gamma", [128, 16])
    din("beta", [128, 16])
    din("iota32", [128, 32])
    din("d_row", [1, H])
    dout("y_out", [D, T])
    dint("silu_z_dram", [DI, T])
    g["gated_dram"] = nc.dram_tensor("gated_dram", [DI, T], BF16, kind="Internal").ap()
    dint("h1_dram", [D, T])
    dint("z_dram", [T, NH])
    dint("cc_in", [NST, H * P])
    dint("cc_out", [2 * NST, H * P])

    def tap(name, shape, dtype=F32):
        if name in TAP:
            dout(name, shape, dtype)
            return g[name]
        return None

    convw = b.sbuf("convw", [128, 32 * 4])
    convb = b.sbuf("convb", [128, 32])
    bdtn = b.sbuf("bdtn", [H, 1])
    expa = b.sbuf("expa", [H, 1])
    dvec = b.sbuf("dvec", [H, 1])
    wdtb = b.sbuf("wdtb", [H, H], BF16)
    gam = b.sbuf("gam", [128, 16])
    bet = b.sbuf("bet", [128, 16])
    hmask = b.sbuf("hmask_sb", [128, 1])
    role = b.sbuf("role_sb", [128, 1])
    onesr = b.sbuf("onesr", [1, 128])
    epsr = b.sbuf("epsr", [1, 1])
    mhalfr = b.sbuf("mhalfr", [1, 1])
    halfp = b.sbuf("halfp", [128, 1])
    halfn = b.sbuf("halfn", [128, 1])
    onesK = b.sbuf("onesK", [128, 128])
    acs0 = b.sbuf("acs0", [128, 1])
    acs1 = b.sbuf("acs1", [128, 1])
    sab0 = b.sbuf("sab0", [128, 1])
    sab1 = b.sbuf("sab1", [128, 1])
    tril = b.sbuf("tril", [CH, CH])
    ident = b.sbuf("ident", [128, 128])
    identb128 = b.sbuf("identb128", [128, 128], BF16)

    lds = []
    pairs = [
        (convw, g["conv_w"][:, :]),
        (convb, g["conv_b"][:, :]),
        (expa, g["a_log"].rearrange("(h one) -> h one", one=1)),
        (dvec, g["d_vec"].rearrange("(h one) -> h one", one=1)),
        (wdtb, None),
        (gam, g["gamma"][:, :]),
        (bet, g["beta"][:, :]),
        (hmask, g["halo_mask"][:, :]),
        (role, g["role"][:, :]),
        (bdtn, g["b_dt"].rearrange("(h one) -> h one", one=1)),
    ]

    def f_lda(e):
        last = None
        for dst, src in pairs:
            if src is None:
                continue
            last = e.dma_start(out=dst[:], in_=src)
        return last
    a_deps = []
    for dst, src in pairs:
        if src is None:
            continue
        a_deps.append(b.step("sync", (lambda dst, src: lambda e: e.dma_start(out=dst[:], in_=src))(dst, src), dma=True))
    wdtf = b.sbuf("wdtf", [H, H])
    a_wdt = b.step("sync", lambda e: e.dma_start(out=wdtf[:], in_=g["w_dt"][:, :]), dma=True)
    a_wdtc = b.step("vector", lambda e: e.tensor_copy(out=wdtb[:], in_=wdtf[:]), deps=[a_wdt])

    def f_const(e):
        e.memset(onesr[:], 1.0)
        e.memset(epsr[:], 1e-5)
        e.memset(mhalfr[:], -0.5)
        e.memset(halfp[:], 0.5)
        e.memset(halfn[:], -0.5)
        e.memset(onesK[:], 1.0)
        e.memset(tril[:], 0.0)
        e.affine_select(out=tril[:], in_=tril[:], compare_op=OP.is_gt,
                        fill=1.0, base=0, pattern=[[-1, CH]], channel_multiplier=1)
        e.memset(ident[:], 0.0)
        e.affine_select(out=ident[:], in_=ident[:], compare_op=OP.not_equal,
                        fill=1.0, base=0, pattern=[[-1, 128]], channel_multiplier=1)
        return None
    a_const = b.step("gpsimd", f_const)

    a_all = list(a_deps) + [a_wdtc, a_const]
    a_expa = b.step("scalar", lambda e: e.activation(out=expa[:], in_=expa[:], func=AF.Exp),
                    deps=a_deps)
    a_bdtn = b.step("vector", lambda e: e.tensor_scalar_mul(bdtn[:], bdtn[:], -1.0),
                    deps=a_deps)

    AX = b.sbuf("AX", [128, 32 * T], BF16)
    x_bf16 = AX[:, 0:16 * TH]

    wst = b.sbuf("wst", [128, 8192], side="right")
    wsb = b.sbuf("wsb", [128, 8192], BF16, side="right")

    xst = b.sbuf("xst", [128, TH], side="right")
    xbr = b.sbuf("xbr", [128, TH], side="right")
    uev = b.sbuf("uev", [128, 1024], side="right")

    B_fm = b.sbuf("B_fm", [NST, T], BF16)
    B_tok = b.sbuf("B_tok", [128, 8 * NST], BF16)
    C_fm = b.sbuf("C_fm", [NST, T], BF16)
    dtneg = b.sbuf("dtneg", [H, T])
    a_sc = b.sbuf("a_sc", [H, T])
    dt_bf = b.sbuf("dt_bf", [H, T], BF16)
    x_conv = b.sbuf("x_conv", [128, 32 * T], BF16)

    prev_cast = None
    cast_ticks = []
    for i in range(16):
        deps = [a_deps[-1]] if i == 0 else [cast_ticks[i - 1]]
        ldt = b.step("sync", (lambda i: lambda e: e.dma_start(
            out=xst[:, :], in_=g["x_in"][tsl(i, 128), :]))(i),
            deps=deps, dma=True)

        mkt = b.step("vector", lambda e: e.tensor_scalar(
            out=xst[:, 0:4], in0=xst[:, 0:4], scalar1=hmask[:], scalar2=None, op0=OP.mult),
            deps=[ldt] + ([a_deps[-1]] if i == 0 else []))
        ct = b.step("vector", (lambda i: lambda e: e.tensor_copy(
            out=x_bf16[:, i * TH:(i + 1) * TH], in_=xst[:, :]))(i), deps=[mkt])
        cast_ticks.append(ct)
    b_xcast = cast_ticks[-1]

    ppB0 = b.psum("ppB0", [128, 512])
    ppB1 = b.psum("ppB1", [128, 512])
    ppB2 = b.psum("ppB2", [128, 4])

    t_silu_z = tap("t_silu_z", [DI, T])
    t_xconv = tap("t_xconv", [DI, T])

    ev_t = None
    prev_wc = None
    prev_mm_last = None
    prev_sl = None
    for oc in range(16):
        is_z = oc >= 8
        wl = b.step("sync", (lambda oc: lambda e: e.dma_start(
            out=wst[:].rearrange("p (k f) -> p k f", k=16),
            in_=g["w_in"][:, tsl(oc, 512)].rearrange("(k p) f -> p k f", p=128)))(oc),
            deps=[prev_wc] if prev_wc else [], dma=True)
        wc = b.step("vector", lambda e: e.tensor_copy(out=wsb[:], in_=wst[:]),
                    deps=[wl] + ([prev_mm_last] if prev_mm_last else []))
        prev_wc = wc

        for sub in range(4):
            def f_mm(e, sub=sub):
                last = None
                for grp, (f0, fn, pp) in enumerate(((0, 512, ppB0), (512, 512, ppB1), (1024, 4, ppB2))):
                    for k in range(16):
                        lhsT = wsb[:, k * 512 + sub * 128:k * 512 + sub * 128 + 128]
                        rhs = x_bf16[:, k * TH + f0:k * TH + f0 + fn]
                        last = e.matmul(pp[:, 0:fn], lhsT, rhs, start=(k == 0), stop=(k == 15))
                return last
            mm_deps = [wc, b_xcast]
            if ev_t is not None:
                mm_deps.append(ev_t)
            mm = b.step("tensor", f_mm, deps=mm_deps)
            if sub == 3:
                prev_mm_last = mm

            ch = (oc % 8) * 4 + sub

            if not is_z:
                def f_cp(e, ch=ch):
                    e.tensor_copy(out=xbr[:, 0:512], in_=ppB0[:])
                    e.tensor_copy(out=xbr[:, 512:1024], in_=ppB1[:])
                    return e.tensor_copy(out=xbr[:, 1024:1028], in_=ppB2[:])
                cv0 = b.step("vector", f_cp, deps=[mm] + ([prev_sl] if prev_sl else []))

                def f_conv(e, ch=ch):
                    acc = uev[:, 0:1024]
                    e.tensor_scalar(out=acc, in0=xbr[:, 1:1025],
                                    scalar1=convw[:, ch * 4:ch * 4 + 1],
                                    scalar2=None, op0=OP.mult)
                    last = None
                    for j in (1, 2, 3):
                        last = e.scalar_tensor_tensor(out=acc, in0=xbr[:, 1 + j:1 + j + 1024],
                                               scalar=convw[:, ch * 4 + j:ch * 4 + j + 1],
                                               in1=acc, op0=OP.mult, op1=OP.add)
                    return last
                cv = b.step("vector", f_conv, deps=[cv0])
                sl = b.step("scalar", (lambda ch: lambda e: e.activation(
                    out=x_conv[:, ch * T:(ch + 1) * T], in_=uev[:, 0:1024],
                    func=AF.Silu, bias=convb[:, ch:ch + 1], scale=1.0))(ch),
                    deps=[cv])
                prev_sl = sl
                ev_t = cv0
                if t_xconv is not None:
                    b.step("gpsimd", (lambda ch: lambda e: e.dma_start(
                        out=g["t_xconv"][tsl(ch, 128), :], in_=x_conv[:, ch * T:(ch + 1) * T]))(ch),
                        deps=[sl], dma=True)
            else:
                def f_zev(e, ch=ch):
                    dz = uev[:, 0:1024]
                    e.activation(out=dz[:, 0:508], in_=ppB0[:, 4:512], func=AF.Silu)
                    e.activation(out=dz[:, 508:1020], in_=ppB1[:], func=AF.Silu)
                    return e.activation(out=dz[:, 1020:1024], in_=ppB2[:], func=AF.Silu)
                sl = b.step("scalar", f_zev, deps=[mm])
                st = b.step("gpsimd", (lambda ch: lambda e: e.dma_start(
                    out=g["silu_z_dram"][tsl(ch, 128), :], in_=uev[:, 0:1024]))(ch),
                    deps=[sl], dma=True)
                ev_t = st
                if t_silu_z is not None:
                    b.step("gpsimd", (lambda ch: lambda e: e.dma_start(
                        out=g["t_silu_z"][tsl(ch, 128), :], in_=uev[:, 0:1024]))(ch),
                        deps=[sl], dma=True)

    if upto == 'B':
        b.finalize(); b.close(); return nc, g
    bar_B = b.barrier()
    b.release(ppB2)
    b.release(ppB1)
    b.release(ppB0)

    ppC0h = b.psum("ppC0", [128, 512])
    ppC1h = b.psum("ppC1", [128, 512])
    ppC0 = ppC0h[:, 0:256]
    ppC1 = ppC1h[:, 0:256]

    t_X = tap("t_X", [8 * 128, DI], BF16)

    ev_t = None
    wc_prev = None
    prev_mm_last = None
    for oc in range(ncx):
        wl = b.step("sync", (lambda oc: lambda e: e.dma_start(
            out=wst[:].rearrange("p (k f) -> p k f", k=32),
            in_=g["w_xp"][:, tsl(oc, 256)].rearrange("(k p) f -> p k f", p=128)))(oc),
            deps=[wc_prev] if wc_prev else [], dma=True)
        wc = b.step("vector", lambda e: e.tensor_copy(out=wsb[:], in_=wst[:]),
                    deps=[wl] + ([prev_mm_last] if prev_mm_last else []))
        wc_prev = wc

        for tt in range(8):
            pp = (ppC0, ppC1)[tt % 2]

            def f_mm(e, oc=oc, tt=tt, pp=pp):
                last = None
                for k in range(32):
                    lhsT = x_conv[:, k * T + tt * 128:k * T + tt * 128 + 128]
                    rhs = wsb[:, k * 256:(k + 1) * 256]
                    last = e.matmul(pp[:], lhsT, rhs, start=(k == 0), stop=(k == 31))
                return last
            deps = [wc]
            if ev_t is not None:
                deps.append(ev_t)
            mm = b.step("tensor", f_mm, deps=deps)
            if tt == 7:
                prev_mm_last = mm
            ev = b.step("vector", (lambda oc, tt, pp: lambda e: e.tensor_copy(
                out=AX[:, tt * DI + oc * 256: tt * DI + oc * 256 + 256], in_=pp[:]))(oc, tt, pp),
                deps=[mm])
            ev_t = ev
    x_done = ev_t
    if t_X is not None:
        for tt in range(8):
            b.step("gpsimd", (lambda tt: lambda e: e.dma_start(
                out=g["t_X"][tsl(tt, 128), :], in_=AX[:, tt * DI:(tt + 1) * DI]))(tt),
                deps=[x_done], dma=True)

    if upto == 'CX':
        b.finalize(); b.close(); return nc, g

    b.release(ppC1h)
    b.release(ppC0h)
    pp6 = [b.psum("pp6_%d" % i, [128, 512]) for i in range(6)]
    ppBt = [b.psum("ppBt%d" % i, [128, 512]) for i in range(2)]
    t_Bfm = tap("t_Bfm", [NST, T])
    t_Cfm = tap("t_Cfm", [NST, T])
    t_dt = tap("t_dt", [H, T])

    parts = ((0, 64, "dt"), (64, 128, "B"), (192, 128, "C"))
    mm_half = []
    wc = None
    for kh in range(2):
        deps = [x_done] if kh == 0 else list(mm_half)
        wl = b.step("sync", (lambda kh: lambda e: e.dma_start(
            out=wst[:, 0:16 * 320].rearrange("p (k f) -> p k f", k=16),
            in_=g["w_xp"][kh * 2048:(kh + 1) * 2048, DI:XP].rearrange("(k p) f -> p k f", p=128)))(kh),
            deps=deps, dma=True)
        wc = b.step("vector", lambda e: e.tensor_copy(
            out=wsb[:, 0:16 * 320], in_=wst[:, 0:16 * 320]), deps=[wl])
        mm_half = []
        for pi, (off, width, pname) in enumerate(parts):
            for tg in range(2):
                def f_mm(e, off=off, width=width, tg=tg, kh=kh, pi=pi):
                    pp = pp6[pi * 2 + tg]
                    last = None
                    for kk in range(16):
                        lhsT = wsb[:, kk * 320 + off: kk * 320 + off + width]
                        rhs = x_conv[:, (kh * 16 + kk) * T + tg * 512: (kh * 16 + kk) * T + tg * 512 + 512]
                        last = e.matmul(pp[0:width, :], lhsT, rhs,
                                 start=(kh == 0 and kk == 0), stop=(kh == 1 and kk == 15))
                    return last
                mm = b.step("tensor", f_mm, deps=[wc])
                mm_half.append(mm)
        def f_bt(e, kh=kh):
            last = None
            for tt in range(8):
                pp = ppBt[tt // 4]
                for kk in range(16):
                    k = kh * 16 + kk
                    lhsT = x_conv[:, k * T + tt * 128:k * T + tt * 128 + 128]
                    rhs = wsb[:, kk * 320 + 64: kk * 320 + 192]
                    last = e.matmul(pp[:, (tt % 4) * 128:(tt % 4) * 128 + 128], lhsT, rhs,
                                    start=(kh == 0 and kk == 0), stop=(kh == 1 and kk == 15))
            return last
        mmbt = b.step("tensor", f_bt, deps=[wc])
        mm_half.append(mmbt)

    part_ev = []
    for pi, (off, width, pname) in enumerate(parts):
        for tg in range(2):
            def f_ev(e, off=off, width=width, tg=tg, pname=pname, pi=pi):
                pp = pp6[pi * 2 + tg]
                if pname == "dt":
                    return e.tensor_copy(out=dtneg[:, tg * 512:(tg + 1) * 512], in_=pp[0:64, :])
                dst = B_fm if pname == "B" else C_fm
                return e.tensor_copy(out=dst[:, tg * 512:(tg + 1) * 512], in_=pp[0:128, :])
            ev = b.step("vector", f_ev, deps=[mm_half[pi * 2 + tg]])
            part_ev.append(ev)

    for i in range(2):
        part_ev.append(b.step("vector", (lambda i: lambda e: e.tensor_copy(
            out=B_tok[:, i * 512:(i + 1) * 512], in_=ppBt[i][:, :]))(i),
            deps=[mm_half[-1]]))

    c_dtc = b.step("vector", lambda e: e.tensor_copy(out=dt_bf[:], in_=dtneg[:]),
                   deps=[part_ev[1]])

    ppdt = pp6[0]
    dt_mms = []
    for tg in range(2):
        mm = b.step("tensor", (lambda tg: lambda e: e.matmul(
            ppdt[0:H, :], wdtb[:], dt_bf[:, tg * 512:(tg + 1) * 512], start=True, stop=True))(tg),
            deps=[c_dtc, a_wdtc, part_ev[-1]] + ([] if tg == 0 else [dt_mms[-1][1]]))
        sg = b.step("scalar", (lambda tg: lambda e: e.activation(
            out=dtneg[:, tg * 512:(tg + 1) * 512], in_=ppdt[0:H, :], func=AF.Sigmoid,
            bias=bdtn[:], scale=-1.0))(tg), deps=[mm, a_bdtn])
        dt_mms.append((mm, sg))
    c_ln = b.step("scalar", lambda e: e.activation(out=dtneg[:], in_=dtneg[:], func=AF.Ln),
                  deps=[dt_mms[-1][1]])
    c_ln = b.step("vector", lambda e: e.tensor_scalar_max(dtneg[:], dtneg[:], -87.0),
                  deps=[c_ln])
    c_asc = b.step("vector", lambda e: e.tensor_scalar(
        out=a_sc[:], in0=dtneg[:], scalar1=expa[:], scalar2=None, op0=OP.mult),
        deps=[c_ln, a_expa])
    if t_dt is not None:
        b.step("gpsimd", lambda e: e.dma_start(out=g["t_dt"][:, :], in_=dtneg[:]),
               deps=[c_ln], dma=True)
    if t_Bfm is not None:
        b.step("gpsimd", lambda e: e.dma_start(out=g["t_Bfm"][:, :], in_=B_fm[:]),
               deps=[part_ev[3]], dma=True)
    if t_Cfm is not None:
        b.step("gpsimd", lambda e: e.dma_start(out=g["t_Cfm"][:, :], in_=C_fm[:]),
               deps=[part_ev[5]], dma=True)

    bar_C = b.barrier()
    bar_C = b.barrier()
    if upto == 'C':
        b.finalize(); b.close(); return nc, g

    b.release(uev)
    b.release(xbr)
    b.release(xst)
    b.release(wsb)
    b.release(wst)
    b.release(x_conv)
    for t in reversed(ppBt):
        b.release(t)
    for t in reversed(pp6):
        b.release(t)

    z0 = b.step("vector", lambda e: e.memset(acs0[:], 0.0), deps=[])
    z1 = b.step("vector", lambda e: e.memset(acs1[:], 0.0), deps=[z0])
    scl_state = {"acs": [z1, z1]}
    psA = b.psum("psA", [64, 1024])
    psB = b.psum("psB", [64, 1024])
    psE = b.psum("psE", [64, 512])
    psT = b.psum("psT", [128, 512])
    psS = b.psum("psS", [128, 1024])

    Dm = b.sbuf("Dm", [64, 1024], side="right")
    Gt = b.sbuf("Gt", [64, DI], BF16, side="right")
    XB1 = b.sbuf("XB1", [64, DI], BF16, side="right")
    XB2 = b.sbuf("XB2", [64, DI], BF16, side="right")
    XB3 = b.sbuf("XB3", [64, DI], BF16, side="right")
    Ytc = b.sbuf("Ytc", [64, DI], BF16, side="right")
    aT_all = b.sbuf("aT_all", [64, NCH * CH], side="right")
    cs_lh = b.sbuf("cs_lh", [64, NCH * CH], side="right")
    cs_hl = b.sbuf("cs_hl", [64, NCH * CH], side="right")
    sdo_lh = b.sbuf("sdo_lh", [64, NCH * CH], side="right")
    decT = b.sbuf("decT", [64, NCH * CH], side="right")
    dtT = b.sbuf("dtT", [64, NCH * CH], side="right")
    dtpos = b.sbuf("dtpos", [H, T], side="right")
    lamlog = b.sbuf("lamlog", [H, NCH], side="right")
    lamv = b.sbuf("lamv", [H, NCH], side="right")
    zer16 = b.sbuf("zer16", [H, NCH], side="right")
    cdecl = b.sbuf("cdecl", [H, NCH], side="right")
    lam_tab = b.sbuf("lam_tab", [128, H * NCH], side="right")
    cdec_tab = b.sbuf("cdec_tab", [64, H * NCH], side="right")
    d_tab = b.sbuf("d_tab", [64, H], side="right")
    Mm_sb = b.sbuf("Mm_sb", [64, CH], side="right")
    identb = b.sbuf("identb", [64, 64], BF16, side="right")
    BzT = b.sbuf("BzT", [64, NST], BF16, side="right")
    rb = b.sbuf("rb", [128, 1024], BF16, side="right")
    SCR = b.sbuf("SCR", [128, DI], side="right")
    r_run = b.sbuf("r_run", [128, DI], side="right")
    S_in = b.sbuf("S_in", [128, DI], BF16, side="right")
    szt = SCR[:, 0:2048]
    uevY = SCR[0:64, 2048:3072]
    lrow = SCR[0:1, 0:1024]
    crow = SCR[0:1, 1024:2048]
    drow = SCR[0:1, 2048:2048 + H]
    gst = b.sbuf("gst", [128, 512], BF16, side="right")

    t_Y = tap("t_Y", [CH, NCH * DI], BF16)
    t_Yf = tap("t_Yf", [CH, NCH * DI], BF16)
    t_gated = tap("t_gated", [DI, T], BF16)
    g["ytok_dram"] = nc.dram_tensor("ytok_dram", [CH, NCH * DI], BF16, kind="Internal").ap()

    c_dtp = b.step("vector", lambda e: e.tensor_scalar_mul(dtpos[:], dtneg[:], -1.0),
                   deps=[c_asc])
    c_zer = b.step("vector", lambda e: e.memset(zer16[:], 0.0), deps=[])
    c_idb = b.step("vector", lambda e: e.tensor_copy(out=identb[:], in_=ident[0:64, 0:64]),
                   deps=[a_const])
    c_idb2 = b.step("vector", lambda e: e.tensor_copy(out=identb128[:], in_=ident[:]),
                    deps=[a_const])
    c_drow = b.step("sync", lambda e: e.dma_start(out=drow[:], in_=g["d_row"][:, :]), dma=True)

    pre_dep = [c_asc, c_dtp]
    for half in range(2):
        def f_tr_a(e, half=half):
            last = None
            for j in range(8):
                z = half * 8 + j
                last = e.transpose(psT[0:64, j * 64:(j + 1) * 64],
                                   a_sc[:, z * CH:(z + 1) * CH], ident[0:64, 0:64])
            return last
        mm1 = b.step("tensor", f_tr_a, deps=pre_dep)
        ev1 = b.step("vector", (lambda half: lambda e: e.tensor_copy(
            out=aT_all[:, half * 512:(half + 1) * 512], in_=psT[0:64, :]))(half), deps=[mm1])

        def f_cs(e, half=half):
            last = None
            for j in range(8):
                z = half * 8 + j
                last = e.matmul(psT[0:64, j * 64:(j + 1) * 64], tril[:],
                                aT_all[:, z * 64:(z + 1) * 64], start=True, stop=True)
            return last
        mm2 = b.step("tensor", f_cs, deps=[ev1])
        ev2 = b.step("vector", (lambda half: lambda e: e.tensor_copy(
            out=cs_lh[:, half * 512:(half + 1) * 512], in_=psT[0:64, :]))(half), deps=[mm2])
        sd2 = b.step("scalar", (lambda half: lambda e: e.activation(
            out=sdo_lh[:, half * 512:(half + 1) * 512],
            in_=cs_lh[:, half * 512:(half + 1) * 512], func=AF.Exp))(half), deps=[ev2])

        def f_tr_cs(e, half=half):
            last = None
            for j in range(8):
                z = half * 8 + j
                last = e.transpose(psT[0:64, j * 64:(j + 1) * 64],
                                   cs_lh[:, z * 64:(z + 1) * 64], ident[0:64, 0:64])
            return last
        mm3 = b.step("tensor", f_tr_cs, deps=[ev2])
        ev3 = b.step("vector", (lambda half: lambda e: e.tensor_copy(
            out=cs_hl[:, half * 512:(half + 1) * 512], in_=psT[0:64, :]))(half), deps=[mm3])

        def f_ll(e, half=half):
            last = None
            for j in range(8):
                z = half * 8 + j
                last = e.tensor_copy(out=lamlog[:, z:z + 1],
                                     in_=cs_hl[:, z * 64 + 63:z * 64 + 64])
            return last
        ev3b = b.step("vector", f_ll, deps=[ev3])

        def f_dec(e, half=half):
            last = None
            for j in range(8):
                z = half * 8 + j
                last = e.activation(out=Dm[:, j * 64:(j + 1) * 64],
                                    in_=cs_hl[:, z * 64:(z + 1) * 64], func=AF.Exp,
                                    bias=lamlog[:, z:z + 1], scale=-1.0)
            return last
        ev4 = b.step("scalar", f_dec, deps=[ev3b])

        def f_tr_dec(e, half=half):
            last = None
            for j in range(8):
                last = e.transpose(psT[0:64, j * 64:(j + 1) * 64],
                                   Dm[:, j * 64:(j + 1) * 64], ident[0:64, 0:64])
            return last
        mm4 = b.step("tensor", f_tr_dec, deps=[ev4])
        ev5 = b.step("vector", (lambda half: lambda e: e.tensor_copy(
            out=decT[:, half * 512:(half + 1) * 512], in_=psT[0:64, :]))(half), deps=[mm4])

        def f_tr_dt(e, half=half):
            last = None
            for j in range(8):
                z = half * 8 + j
                last = e.transpose(psT[0:64, j * 64:(j + 1) * 64],
                                   dtpos[:, z * CH:(z + 1) * CH], ident[0:64, 0:64])
            return last
        mm5 = b.step("tensor", f_tr_dt, deps=[c_dtp, ev5])
        ev6 = b.step("vector", (lambda half: lambda e: e.tensor_copy(
            out=dtT[:, half * 512:(half + 1) * 512], in_=psT[0:64, :]))(half), deps=[mm5])
        pre_dep = [ev6]

    c_cd0 = b.step("vector", lambda e: e.memset(cdecl[:, 0:1], 0.0), deps=pre_dep)
    c_cd1 = b.step("vector", lambda e: e.tensor_tensor_scan(
        out=cdecl[:, 1:NCH], data0=lamlog[:, 0:NCH - 1], data1=zer16[0:H, 1:NCH],
        initial=0.0, op0=OP.add, op1=OP.add), deps=[c_cd0, c_zer])
    c_cd2 = b.step("scalar", lambda e: e.activation(out=cdecl[:], in_=cdecl[:], func=AF.Exp),
                   deps=[c_cd1])
    c_cd3 = b.step("vector", lambda e: e.tensor_scalar(
        out=cdecl[:], in0=cdecl[:], scalar1=role[0:H, :], scalar2=None, op0=OP.mult),
        deps=[c_cd2, a_deps[-1]])
    c_lam = b.step("scalar", lambda e: e.activation(out=lamv[:], in_=lamlog[:], func=AF.Exp),
                   deps=[c_cd1])

    c_fl1 = b.step("gpsimd", lambda e: e.dma_start(
        out=lrow[:].rearrange("one (h z) -> one h z", z=NCH), in_=lamv[:, :]),
        deps=[c_lam], dma=True)
    c_fl2 = b.step("gpsimd", lambda e: e.dma_start(
        out=crow[:].rearrange("one (h z) -> one h z", z=NCH), in_=cdecl[:, :]),
        deps=[c_cd3], dma=True)

    bc_deps = [c_fl1, c_fl2, c_drow, a_const]
    for j in range(2):
        mmb = b.step("tensor", (lambda j: lambda e: e.matmul(
            psT[:, 0:512], onesr[:], lrow[:, j * 512:(j + 1) * 512], start=True, stop=True))(j),
            deps=bc_deps)
        evb = b.step("vector", (lambda j: lambda e: e.tensor_copy(
            out=lam_tab[:, j * 512:(j + 1) * 512], in_=psT[:, :]))(j), deps=[mmb])
        mmc = b.step("tensor", (lambda j: lambda e: e.matmul(
            psT[0:64, 0:512], onesr[0:1, 0:64], crow[:, j * 512:(j + 1) * 512],
            start=True, stop=True))(j), deps=[evb])
        evc = b.step("vector", (lambda j: lambda e: e.tensor_copy(
            out=cdec_tab[:, j * 512:(j + 1) * 512], in_=psT[0:64, :]))(j), deps=[mmc])
        bc_deps = [evc]
    mmd = b.step("tensor", lambda e: e.matmul(psT[0:64, 0:64], onesr[0:1, 0:64], drow[:, :],
                                              start=True, stop=True), deps=bc_deps)
    evd = b.step("vector", lambda e: e.tensor_copy(out=d_tab[:], in_=psT[0:64, 0:64]),
                 deps=[mmd])
    c_r0 = b.step("vector", lambda e: e.memset(r_run[:], 0.0), deps=[])

    def hrep(viewsrc, nh, np_=64):
        return viewsrc.rearrange("p (h one) -> p h one", one=1).to_broadcast([64, nh, np_])

    last_mm = evd
    last_v = evd
    last_park = None
    scan_done = None
    for z in range(NCH):
        zl = tsl(z, CH)
        dep_xz = [x_done]
        if last_park is not None:
            dep_xz.append(last_park)
        xz_cp = b.step("gpsimd", (lambda z: lambda e: e.dma_start(
            out=XB1[:, :], in_=AX[(z % 2) * 64:(z % 2) * 64 + 64,
                                  (z // 2) * DI:(z // 2 + 1) * DI]))(z),
            deps=dep_xz, dma=True)
        mmM = b.step("tensor", (lambda zl=zl: lambda e: e.matmul(
            psE[0:64, 0:64], B_fm[:, zl], C_fm[:, zl], start=True, stop=True))(),
            deps=[part_ev[-1], last_v])
        evM = b.step("vector", lambda e: e.tensor_tensor(
            out=Mm_sb[:], in0=psE[0:64, 0:64], in1=tril[:], op=OP.mult), deps=[mmM])
        evBt = b.step("gpsimd", (lambda z: lambda e: e.dma_start(
            out=BzT[:, :], in_=B_tok[(z % 2) * 64:(z % 2) * 64 + 64,
                                     (z // 2) * NST:(z // 2 + 1) * NST]))(z),
            deps=[part_ev[-1], xz_cp], dma=True)
        am_st = b.step("vector", (lambda z: lambda e: e.tensor_tensor(
            out=SCR[0:64, :].rearrange("p (h l) -> p h l", l=64),
            in0=hrep(aT_all[:, z * 64:(z + 1) * 64], 64),
            in1=tril[:].rearrange("p (one l) -> p one l", one=1).to_broadcast([64, 64, 64]),
            op=OP.mult))(z), deps=[evBt, last_v])
        x_d = b.step("vector", (lambda z: lambda e: e.tensor_tensor(
            out=XB2[:, :].rearrange("p (h q) -> p h q", q=64),
            in0=XB1[:, :].rearrange("p (h q) -> p h q", q=64),
            in1=hrep(d_tab[:, :], 64), op=OP.mult))(z), deps=[xz_cp, evd, am_st])
        x_s = b.step("vector", (lambda z: lambda e: e.tensor_tensor(
            out=XB3[:, :].rearrange("p (h q) -> p h q", q=64),
            in0=XB1[:, :].rearrange("p (h q) -> p h q", q=64),
            in1=hrep(dtT[:, z * 64:(z + 1) * 64], 64), op=OP.mult))(z), deps=[x_d])
        x_dc = b.step("vector", (lambda z: lambda e: e.tensor_tensor(
            out=XB1[:, :].rearrange("p (h q) -> p h q", q=64),
            in0=XB3[:, :].rearrange("p (h q) -> p h q", q=64),
            in1=hrep(decT[:, z * 64:(z + 1) * 64], 64), op=OP.mult))(z), deps=[x_s])
        g_dep = [am_st, evM]
        for gg in range(8):
            h0 = gg * 8
            mmE = b.step("tensor", (lambda gg: lambda e: e.matmul(
                psE[0:64, :], onesK[0:64, 0:64], SCR[0:64, gg * 512:(gg + 1) * 512],
                start=True, stop=True))(gg), deps=g_dep)
            gpar = (gg % 2) * 512
            evE = b.step("vector", (lambda z, gg, h0, gpar=gpar: lambda e: e.tensor_tensor(
                out=Dm[:, gpar:gpar + 512].rearrange("p (h l) -> p h l", l=64),
                in0=psE[0:64, :].rearrange("p (h l) -> p h l", l=64),
                in1=hrep(cs_lh[:, z * 64 + h0:z * 64 + h0 + 8], 8),
                op=OP.subtract))(z, gg, h0), deps=[mmE])
            exE = b.step("scalar", (lambda gg, gpar=gpar: lambda e: e.activation(
                out=Dm[:, gpar:gpar + 512], in_=Dm[:, gpar:gpar + 512],
                func=AF.Exp))(gg), deps=[evE])
            gfin = b.step("vector", (lambda gg, gpar=gpar: lambda e: e.scalar_tensor_tensor(
                out=Gt[:, gg * 512:(gg + 1) * 512].rearrange("p (h l) -> p h l", l=64),
                in0=Dm[:, gpar:gpar + 512].rearrange("p (h l) -> p h l", l=64),
                scalar=1.0,
                in1=Mm_sb[:].rearrange("p (one l) -> p one l", one=1).to_broadcast([64, 8, 64]),
                op0=OP.min, op1=OP.mult))(gg), deps=[exE])
            g_dep = [gfin]
        g_done = g_dep[0]
        for gq in range(4):
            gsl = tsl(gq, 1024)
            rbc = b.step("vector", (lambda gsl=gsl: lambda e: e.tensor_copy(
                out=rb[:, :], in_=r_run[:, gsl]))(), deps=[last_v if z == 0 else scan_done, g_done])
            mmO = b.step("tensor", (lambda zl=zl: lambda e: [
                e.matmul(psB[:, 0:512], C_fm[:, zl], rb[:, 0:512], start=True, stop=True),
                e.matmul(psB[:, 512:1024], C_fm[:, zl], rb[:, 512:1024], start=True, stop=True)][-1])(),
                deps=[rbc])
            mmS = b.step("tensor", (lambda gsl=gsl: lambda e: [
                e.matmul(psS[:, 0:512], BzT[:], XB1[:, gsl][:, 0:512], start=True, stop=True),
                e.matmul(psS[:, 512:1024], BzT[:], XB1[:, gsl][:, 512:1024], start=True, stop=True)][-1])(),
                deps=[x_dc, evBt, mmO])
            t1s = b.step("vector", (lambda z, gq, gsl=gsl: lambda e: e.tensor_tensor(
                out=SCR[:, 0:1024].rearrange("p (h q) -> p h q", q=64),
                in0=r_run[:, gsl].rearrange("p (h q) -> p h q", q=64),
                in1=lam_tab[:].rearrange("p (h z2) -> p h z2", z2=NCH)
                    [:, gq * 16:(gq + 1) * 16, z:z + 1].to_broadcast([128, 16, 64]),
                op=OP.mult))(z, gq), deps=[rbc, g_done])
            scan_done = b.step("vector", (lambda gsl=gsl: lambda e: e.tensor_tensor(
                out=r_run[:, gsl], in0=SCR[:, 0:1024], in1=psS[:, :], op=OP.add))(),
                deps=[t1s, mmS])
            def f_yd(e, z=z, gq=gq, gsl=gsl):
                last = e.matmul(psA[:, 0:512], identb[:], XB2[:, gsl][:, 0:512],
                                start=True, stop=False)
                last = e.matmul(psA[:, 512:1024], identb[:], XB2[:, gsl][:, 512:1024],
                                start=True, stop=False)
                for hl in range(16):
                    h = gq * 16 + hl
                    last = e.matmul(psA[:, hl * 64:(hl + 1) * 64],
                                    Gt[:, h * 64:(h + 1) * 64],
                                    XB3[:, h * 64:(h + 1) * 64],
                                    start=False, stop=(hl == 15))
                return last
            mmY = b.step("tensor", f_yd, deps=[x_s, g_done, scan_done])
            t2s = b.step("vector", (lambda z, gq: lambda e: e.tensor_tensor(
                out=uevY[:].rearrange("p (h q) -> p h q", q=64),
                in0=psB[:, :].rearrange("p (h q) -> p h q", q=64),
                in1=hrep(sdo_lh[:, z * 64 + gq * 16:z * 64 + gq * 16 + 16], 16),
                op=OP.mult))(z, gq), deps=[mmO, mmY])
            evY = b.step("vector", (lambda gsl=gsl: lambda e: e.tensor_tensor(
                out=Ytc[:, gsl], in0=uevY[:], in1=psA[:, :], op=OP.add))(),
                deps=[t2s])
            last_v = evY
        last_park = b.step("gpsimd", (lambda z: lambda e: e.dma_start(
            out=g["ytok_dram"][:, z * DI:(z + 1) * DI], in_=Ytc[:, :]))(z),
            deps=[last_v], dma=True)
        if t_Y is not None:
            b.step("gpsimd", (lambda z: lambda e: e.dma_start(
                out=g["t_Y"][:, z * DI:(z + 1) * DI], in_=Ytc[:, :]))(z),
                deps=[last_v], dma=True)

    cc_up = b.step("gpsimd", lambda e: e.dma_start(out=g["cc_in"][:, :], in_=r_run[:, :]),
                   deps=[scan_done], dma=True)
    cc_go = b.step("gpsimd", lambda e: e.collective_compute(
        "AllGather", OP.bypass, replica_groups=RG, ins=[g["cc_in"][:, :]],
        outs=[g["cc_out"][:, :]]), deps=[cc_up])
    cc_dn = b.step("gpsimd", lambda e: e.dma_start(out=SCR[:, :], in_=g["cc_out"][0:NST, :]),
                   deps=[cc_go], dma=True)
    cc_cast = b.step("vector", lambda e: e.tensor_copy(out=S_in[:], in_=SCR[:, :]),
                     deps=[cc_dn])

    last_g = cc_cast
    for z in range(NCH):
        zl = tsl(z, CH)
        yld = b.step("gpsimd", (lambda z: lambda e: e.dma_start(
            out=Ytc[:, :], in_=g["ytok_dram"][:, z * DI:(z + 1) * DI]))(z),
            deps=[last_g], dma=True)
        szl = b.step("gpsimd", (lambda zl=zl: lambda e: e.dma_start(
            out=szt[:].rearrange("p (j l) -> p j l", l=64),
            in_=g["silu_z_dram"][:, zl].rearrange("(j p) l -> p j l", p=128)))(),
            deps=[yld], dma=True)
        for gq in range(4):
            gsl = tsl(gq, 1024)
            mmC = b.step("tensor", (lambda zl=zl, gsl=gsl: lambda e: [
                e.matmul(psB[:, 0:512], C_fm[:, zl], S_in[:, gsl][:, 0:512],
                         start=True, stop=True),
                e.matmul(psB[:, 512:1024], C_fm[:, zl], S_in[:, gsl][:, 512:1024],
                         start=True, stop=True)][-1])(), deps=[cc_cast, yld, last_g])
            t2c = b.step("vector", (lambda z, gq: lambda e: e.tensor_tensor(
                out=uevY[:].rearrange("p (h q) -> p h q", q=64),
                in0=psB[:, :].rearrange("p (h q) -> p h q", q=64),
                in1=cdec_tab[:].rearrange("p (h z2) -> p h z2", z2=NCH)
                    [:, gq * 16:(gq + 1) * 16, z:z + 1].to_broadcast([64, 16, 64]),
                op=OP.mult))(z, gq), deps=[mmC])
            ycf = b.step("vector", (lambda gsl=gsl: lambda e: e.tensor_tensor(
                out=Ytc[:, gsl], in0=uevY[:], in1=Ytc[:, gsl], op=OP.add))(),
                deps=[t2c])
            last_g = ycf
        if t_Yf is not None:
            b.step("gpsimd", (lambda z: lambda e: e.dma_start(
                out=g["t_Yf"][:, z * DI:(z + 1) * DI], in_=Ytc[:, :]))(z),
                deps=[last_g], dma=True)
        for jb in range(4):
            cst = b.step("vector", (lambda jb: lambda e: e.tensor_copy(
                out=SCR[0:64, 3072:4096], in_=Ytc[:, jb * 1024:(jb + 1) * 1024]))(jb),
                deps=[last_g])

            def f_trY(e, jb=jb):
                last = None
                for i in range(8):
                    last = e.transpose(psT[:, i * 64:(i + 1) * 64],
                                       SCR[0:64, 3072 + i * 128:3072 + (i + 1) * 128],
                                       ident[0:64, 0:64])
                return last
            mmT = b.step("tensor", f_trY, deps=[cst])
            gt = b.step("vector", (lambda jb: lambda e: e.tensor_tensor(
                out=gst[:, :], in0=psT[:, :], in1=szt[:, jb * 512:(jb + 1) * 512],
                op=OP.mult))(jb), deps=[mmT, szl])
            gp = b.step("gpsimd", (lambda z, jb, zl=zl: lambda e: e.dma_start(
                out=g["gated_dram"][jb * 1024:(jb + 1) * 1024, zl]
                    .rearrange("(i p) l -> p i l", p=128),
                in_=gst[:].rearrange("p (i l) -> p i l", l=64)))(z, jb),
                deps=[gt], dma=True)
            last_g = gp
    if t_gated is not None:
        for jj in range(32):
            b.step("gpsimd", (lambda jj: lambda e: e.dma_start(
                out=g["t_gated"][tsl(jj, 128), :],
                in_=g["gated_dram"][tsl(jj, 128), :]))(jj), deps=[last_g], dma=True)

    bar_D = b.barrier()
    if upto == 'D':
        b.finalize(); b.close(); return nc, g

    for t in [gst, S_in, r_run, SCR, rb, BzT,
              identb, Mm_sb, d_tab, cdec_tab, lam_tab, cdecl, zer16, lamv, lamlog,
              dtpos, dtT, decT, sdo_lh, cs_hl, cs_lh, aT_all, Ytc, XB3, XB2, XB1, Gt, Dm]:
        b.release(t)
    for t in [psS, psT, psE, psB, psA]:
        b.release(t)
    for t in [dt_bf, a_sc, dtneg, C_fm, B_tok, B_fm, AX]:
        b.release(t)

    po = [b.psum("po%d" % i, [128, 512]) for i in range(8)]
    h1b = b.sbuf("h1b", [128, 16 * 1024], BF16, side="right")
    wstE = b.sbuf("wstE", [128, 1024], side="right")
    wsbE = b.sbuf("wsbE", [128, 1024], BF16, side="right")
    gstE = b.sbuf("gstE", [128, 2048], BF16, side="right")
    h_pre = b.sbuf("h_pre", [128, 16 * 1024], side="right")
    xrel = b.sbuf("xrel", [128, 1024], side="right")
    sqb = b.sbuf("sqb", [128, 1024], BF16)
    rmu = b.sbuf("rmu", [1, 1024])
    rvar = b.sbuf("rvar", [1, 1024])
    mub = b.sbuf("mub", [128, 1024])
    ivb = b.sbuf("ivb", [128, 1024])
    rtmp = b.sbuf("rtmp", [1, 1024])

    t_h1 = tap("t_h1", [D, T])

    last_ev = None
    for slab in range(4):
        last_mm = None
        mm_hist = []
        for k in range(32):
            par = k % 2
            lk = mm_hist[-2] if len(mm_hist) >= 2 else (last_ev if last_ev else None)
            wl = b.step("sync", (lambda slab, k, par: lambda e: e.dma_start(
                out=wstE[:, par * 512:(par + 1) * 512],
                in_=g["w_out"][k * 128:(k + 1) * 128, tsl(slab, 512)]))(slab, k, par),
                deps=[lk] if lk else [], dma=True)
            wc = b.step("vector", (lambda par: lambda e: e.tensor_copy(
                out=wsbE[:, par * 512:(par + 1) * 512],
                in_=wstE[:, par * 512:(par + 1) * 512]))(par),
                deps=[wl] + ([lk] if lk else []))
            gl = b.step("gpsimd", (lambda k, par: lambda e: e.dma_start(
                out=gstE[:, par * 1024:(par + 1) * 1024],
                in_=g["gated_dram"][k * 128:(k + 1) * 128, :]))(k, par),
                deps=[lk] if lk else [], dma=True)

            def f_mm(e, slab=slab, k=k, par=par):
                last = None
                for oc in range(4):
                    for th in range(2):
                        last = e.matmul(po[oc * 2 + th][:, :],
                                        wsbE[:, par * 512 + oc * 128:par * 512 + oc * 128 + 128],
                                        gstE[:, par * 1024 + th * 512:par * 1024 + th * 512 + 512],
                                        start=(k == 0), stop=(k == 31))
                return last
            deps = [wc, gl]
            if last_mm is not None:
                deps.append(last_mm)
            if k == 0 and last_ev is not None:
                deps.append(last_ev)
            last_mm = b.step("tensor", f_mm, deps=deps)
            mm_hist.append(last_mm)
        for oc in range(4):
            dt_i = slab * 4 + oc
            xdeps = [last_mm]
            if last_ev is not None:
                xdeps.append(last_ev)
            xld = b.step("sync", (lambda dt_i: lambda e: e.dma_start(
                out=xrel[:, :], in_=g["x_in"][tsl(dt_i, 128), 4:4 + T]))(dt_i),
                deps=xdeps, dma=True)
            for th in range(2):
                ev = b.step("vector", (lambda dt_i, oc, th: lambda e: e.tensor_tensor(
                    out=h_pre[:, dt_i * 1024 + th * 512:dt_i * 1024 + th * 512 + 512],
                    in0=po[oc * 2 + th][:, :], in1=xrel[:, th * 512:(th + 1) * 512],
                    op=OP.add))(dt_i, oc, th), deps=[xld, last_mm])
                last_ev = ev

    hb_c = None
    for i in range(16):
        hb_c = b.step("vector", (lambda i: lambda e: e.tensor_copy(
            out=h1b[:, tsl(i, 1024)], in_=h_pre[:, tsl(i, 1024)]))(i), deps=[last_ev])
    for t in reversed(po):
        b.release(t)
    psq = b.psum("psq", [1, 1024])
    ps1 = b.psum("ps1", [1, 1024])
    psb2 = b.psum("psb2", [128, 512])
    ones1c = b.sbuf("ones1c", [128, 1], BF16)
    ones1f = b.sbuf("ones1f", [128, 1])
    c_o1 = b.step("vector", lambda e: e.memset(ones1c[:], 1.0), deps=[])
    c_o1f = b.step("vector", lambda e: e.memset(ones1f[:], 1.0), deps=[])

    def ln_stats(src_bf, dep, nfeat=D):
        mm_prev = None
        for i in range(16):
            c2 = b.step("scalar", (lambda i: lambda e: e.activation(
                out=sqb[:], in_=src_bf[:, tsl(i, 1024)], func=AF.Square))(i),
                deps=[dep] + ([mm_prev] if mm_prev else []))
            mm_prev = b.step("tensor", (lambda i: lambda e: [
                e.matmul(psq[0:1, 0:512], ones1c[:], sqb[:, 0:512],
                         start=(i == 0), stop=(i == 15)),
                e.matmul(psq[0:1, 512:1024], ones1c[:], sqb[:, 512:1024],
                         start=(i == 0), stop=(i == 15)),
                e.matmul(ps1[0:1, 0:512], ones1c[:], src_bf[:, i * 1024:i * 1024 + 512],
                         start=(i == 0), stop=(i == 15)),
                e.matmul(ps1[0:1, 512:1024], ones1c[:], src_bf[:, i * 1024 + 512:(i + 1) * 1024],
                         start=(i == 0), stop=(i == 15))][-1])(i),
                deps=[c2, c_o1])
        s1 = b.step("vector", lambda e: e.tensor_scalar_mul(rmu[:], ps1[0:1, :], 1.0 / nfeat),
                    deps=[mm_prev])
        s2 = b.step("vector", lambda e: e.tensor_scalar_mul(rvar[:], psq[0:1, :], 1.0 / nfeat),
                    deps=[s1])
        s3 = b.step("vector", lambda e: e.tensor_tensor(out=rtmp[:], in0=rmu[:], in1=rmu[:],
                                                        op=OP.mult), deps=[s2])
        s4 = b.step("vector", lambda e: e.tensor_tensor(out=rvar[:], in0=rvar[:], in1=rtmp[:],
                                                        op=OP.subtract), deps=[s3])
        s5 = b.step("scalar", lambda e: e.activation(out=rtmp[:], in_=rvar[:], func=AF.Ln,
                                                     bias=epsr[:], scale=1.0), deps=[s4])
        s6 = b.step("scalar", lambda e: e.activation(out=rvar[:], in_=rtmp[:], func=AF.Exp,
                                                     scale=mhalfr[:]), deps=[s5])
        bdep = [s6]
        for (row, dst) in ((rmu, mub), (rvar, ivb)):
            for j in range(2):
                mmx = b.step("tensor", (lambda row, j: lambda e: e.matmul(
                    psb2[:, :], onesr[:], row[:, j * 512:(j + 1) * 512],
                    start=True, stop=True))(row, j), deps=bdep)
                evx = b.step("vector", (lambda dst, j: lambda e: e.tensor_copy(
                    out=dst[:, j * 512:(j + 1) * 512], in_=psb2[:, :]))(dst, j), deps=[mmx])
                bdep = [evx]
        return bdep[0]

    lnst = ln_stats(h1b, hb_c)
    last_ap = lnst
    t_h1 = g.get("t_h1")
    for i in range(16):
        a1 = b.step("vector", (lambda i: lambda e: e.tensor_tensor(
            out=h_pre[:, tsl(i, 1024)], in0=h_pre[:, tsl(i, 1024)], in1=mub[:],
            op=OP.subtract))(i), deps=[last_ap])
        a2 = b.step("vector", (lambda i: lambda e: e.tensor_tensor(
            out=h_pre[:, tsl(i, 1024)], in0=h_pre[:, tsl(i, 1024)], in1=ivb[:],
            op=OP.mult))(i), deps=[a1])
        a3 = b.step("vector", (lambda i: lambda e: e.tensor_copy(
            out=h1b[:, tsl(i, 1024)], in_=h_pre[:, tsl(i, 1024)]))(i), deps=[a2])
        a4 = b.step("gpsimd", (lambda i: lambda e: e.dma_start(
            out=g["h1_dram"][tsl(i, 128), :], in_=h_pre[:, tsl(i, 1024)]))(i),
            deps=[a3], dma=True)
        if t_h1 is not None:
            b.step("gpsimd", (lambda i: lambda e: e.dma_start(
                out=g["t_h1"][tsl(i, 128), :], in_=h_pre[:, tsl(i, 1024)]))(i),
                deps=[a3], dma=True)
        last_ap = a4

    bar_E = b.barrier()
    if upto == 'E':
        b.finalize(); b.close(); return nc, g

    for t in [xrel, h_pre, gstE, wsbE, wstE]:
        b.release(t)
    b.release(psb2)
    b.release(ps1)
    b.release(psq)
    A1 = b.sbuf("A1", [128, 8 * NH], BF16)
    invb2 = b.sbuf("invb2", [128, 1])
    czb2 = b.sbuf("czb2", [128, 1])
    czr = b.sbuf("czr", [1, 2])
    zst2 = b.sbuf("zst2", [128, 256], side="right")
    sabs = b.sbuf("sabs", [128, 1], side="right")
    srow = b.sbuf("srow", [1, 4], side="right")
    invb = b.sbuf("invb", [128, 1], side="right")
    czrow = b.sbuf("czrow", [1, 4], side="right")
    czb = b.sbuf("czb", [128, 1], side="right")
    bsrow = b.sbuf("bsrow", [1, 256], side="right")
    wstF = b.sbuf("wstF", [128, 2048], side="right")
    sgn1 = b.sbuf("sgn1", [128, 2048], BF16, side="right")
    tern = b.sbuf("tern", [128, 4096], BF16, side="right")
    pz = [b.psum("pz%d" % i, [128, 512]) for i in range(8)]

    t_s = tap("t_s", [1, 4])
    t_z = tap("t_z", [T, NH])
    t_thr = tap("t_thr", [128, 8 * 4])

    sp1 = b.step("vector", lambda e: e.memset(srow[:], 0.0), deps=[])
    s1ab, s1ac = None, scl_state["acs"][0]
    for rb_ in range(16):
        for cb_ in range(4):
            ld = b.step("sync", (lambda rb_, cb_: lambda e: e.dma_start(
                out=wstF[:, :], in_=g["w_syn"][rb_ * 128:(rb_ + 1) * 128,
                                               cb_ * 2048:(cb_ + 1) * 2048]))(rb_, cb_),
                deps=[s1ab] if s1ab else [], dma=True)
            s1ab = b.step("scalar", lambda e: e.activation(
                out=wstF[:, :], in_=wstF[:, :], func=AF.Abs, accum_out=sab0[:]),
                deps=[ld, s1ac])
            s1ac = b.step("vector", lambda e: e.tensor_tensor(
                out=acs0[:], in0=acs0[:], in1=sab0[:], op=OP.add),
                deps=[s1ab, s1ac])
    mmt = b.step("tensor", lambda e: e.matmul(
        pz[0][0:1, 0:1], ones1f[:], acs0[:], start=True, stop=True),
        deps=[sp1, s1ac, c_o1f])
    e1 = b.step("vector", lambda e: e.tensor_scalar(
        out=srow[0:1, 0:1], in0=pz[0][0:1, 0:1],
        scalar1=1.0 / (D * NH), scalar2=None, op0=OP.mult), deps=[mmt])
    sprep = b.step("vector", lambda e: e.tensor_scalar(
        out=czrow[0:1, 0:1], in0=srow[0:1, 0:1], scalar1=0.5, scalar2=None,
        op0=OP.mult), deps=[e1])
    s_eps = b.step("vector", lambda e: e.tensor_scalar(
        out=srow[0:1, 2:3], in0=srow[0:1, 0:1], scalar1=1e-8, scalar2=None, op0=OP.add),
        deps=[sprep])
    s_rec = b.step("vector", lambda e: e.reciprocal(out=srow[0:1, 2:3], in_=srow[0:1, 2:3]),
                   deps=[s_eps])
    if t_s is not None:
        b.step("gpsimd", lambda e: e.dma_start(out=g["t_s"][:, :], in_=srow[:, :]),
               deps=[s_rec], dma=True)
    s_rc2 = b.step("vector", lambda e: e.reciprocal(out=czr[:, 0:1], in_=czrow[0:1, 0:1]),
                   deps=[sprep, s_rec])
    bc_pairs = [(srow[0:1, 2:3], invb), (czrow[0:1, 0:1], czb)]
    bdep = [s_rc2]
    for (rw, dst) in bc_pairs:
        mmi = b.step("tensor", (lambda rw: lambda e: e.matmul(
            pz[1][:, 0:1], onesr[:], rw, start=True, stop=True))(rw), deps=bdep)
        evi = b.step("vector", (lambda dst: lambda e: e.tensor_copy(
            out=dst[:], in_=pz[1][:, 0:1]))(dst), deps=[mmi])
        bdep = [evi]
    sc_done = bdep[0]

    last_ev = sc_done
    last_st = None
    last_mmz = None
    last_sg = None
    prev_halves = []
    for og in range(32):
        halves = []
        for hh in range(2):
            wdeps = [last_sg] if last_sg else [sc_done]
            wl = b.step("sync", (lambda og, hh: lambda e: e.dma_start(
                out=wstF[:, :].rearrange("p (k f) -> p k f", k=8),
                in_=g["w_syn"][hh * 1024:(hh + 1) * 1024, tsl(og, 256)]
                .rearrange("(k p) f -> p k f", p=128)))(og, hh),
                deps=wdeps, dma=True)
            sdeps = [wl, sc_done]
            if last_mmz is not None:
                sdeps.append(last_mmz)
            if halves:
                sdeps.append(halves[-1])
            elif prev_halves:
                sdeps.append(prev_halves[-1])
            sg = b.step("scalar", (lambda hh: lambda e: [
                e.activation(out=sgn1[:, :], in_=wstF[:, :], func=AF.Sign,
                             bias=halfn[:], scale=invb[:]),
                e.activation(out=tern[:, hh * 2048:(hh + 1) * 2048], in_=wstF[:, :],
                             func=AF.Sign, bias=halfp[:], scale=invb[:])][-1])(hh),
                deps=sdeps)
            last_sg = sg
            tn = b.step("vector", (lambda hh: lambda e: e.tensor_tensor(
                out=tern[:, hh * 2048:(hh + 1) * 2048], in0=sgn1[:, :],
                in1=tern[:, hh * 2048:(hh + 1) * 2048], op=OP.add))(hh), deps=[sg])
            halves.append(tn)
            last_ev = tn
        bl = b.step("sync", (lambda og: lambda e: e.dma_start(
            out=bsrow[0:1, 0:256], in_=g["b_syn"].rearrange("(one n) -> one n", one=1)
            [:, tsl(og, 256)]))(og), deps=[last_ev], dma=True)
        bs = b.step("vector", lambda e: e.tensor_scalar(
            out=bsrow[0:1, 0:256], in0=bsrow[0:1, 0:256], scalar1=czr[0:1, 0:1],
            scalar2=None, op0=OP.mult), deps=[bl, s_rc2])

        def f_mm(e, og=og):
            last = None
            for tt in range(8):
                for k in range(16):
                    kk, khh = k % 8, k // 8
                    last = e.matmul(pz[tt][:, 0:256],
                                    h1b[:, k * 1024 + tt * 128:k * 1024 + tt * 128 + 128],
                                    tern[:, khh * 2048 + kk * 256:khh * 2048 + (kk + 1) * 256],
                                    start=(k == 0), stop=False)
                last = e.matmul(pz[tt][:, 0:256], onesr[0:1, :], bsrow[0:1, 0:256],
                                start=False, stop=True)
            return last
        mm_deps = halves + [bs]
        if last_st is not None:
            mm_deps.append(last_st)
        mmz = b.step("tensor", f_mm, deps=mm_deps)
        last_mmz = mmz
        prev_halves = halves
        st_hist = []
        for tt in range(8):
            zp = 0
            sdep = last_st
            evz = b.step("vector", (lambda tt, og, zp: lambda e: [
                e.tensor_scalar(out=zst2[:, zp:zp + 256], in0=pz[tt][:, 0:256], scalar1=czb[:],
                                scalar2=None, op0=OP.mult),
                e.tensor_copy(out=A1[:, tt * NH + og * 256:tt * NH + (og + 1) * 256],
                              in_=zst2[:, zp:zp + 256])][-1])(tt, og, zp),
                deps=[mmz] + ([sdep] if sdep else []))
            stz = b.step("gpsimd", (lambda tt, og, zp: lambda e: e.dma_start(
                out=g["z_dram"][tt * 128:(tt + 1) * 128, tsl(og, 256)],
                in_=zst2[:, zp:zp + 256]))(tt, og, zp), deps=[evz], dma=True)
            st_hist.append(stz)
            last_st = stz
    if t_z is not None:
        for tt in range(8):
            b.step("gpsimd", (lambda tt: lambda e: e.dma_start(
                out=g["t_z"][tt * 128:(tt + 1) * 128, :],
                in_=g["z_dram"][tt * 128:(tt + 1) * 128, :]))(tt), deps=[last_st], dma=True)

    bar_F1 = b.barrier()
    if upto == 'F1':
        b.finalize(); b.close(); return nc, g

    for t in [tern, sgn1, wstF, bsrow, czb, czrow, invb,
              srow, sabs, zst2, h1b]:
        b.release(t)
    msk2 = b.sbuf("msk2", [128, NH], mybir.dt.float8e4, side="right")
    srow2 = b.sbuf("srow2", [1, 4], side="right")
    negmid = b.sbuf("negmid", [128, 8], side="right")
    msk = b.sbuf("msk", [128, NH], mybir.dt.float8e4, side="right")
    lo_t = b.sbuf("lo_t", [128, 8], side="right")
    hi_t = b.sbuf("hi_t", [128, 8], side="right")
    mid_t = b.sbuf("mid_t", [128, 8], side="right")
    cnt_t = b.sbuf("cnt_t", [128, 8], side="right")
    cge_t = b.sbuf("cge_t", [128, 8], mybir.dt.uint8, side="right")
    Tuse = b.sbuf("Tuse", [128, 8], side="right")
    Tst = b.sbuf("Tst", [128, 8], side="right")
    c2_t = b.sbuf("c2_t", [128, 8], side="right")
    jr_t = b.sbuf("jr_t", [128, 8], side="right")
    mr_t = b.sbuf("mr_t", [128, 8], mybir.dt.uint8, side="right")
    iot = b.sbuf("iot", [128, 32], side="right")
    zfs = b.sbuf("zfs", [128, 2048], side="right")
    zneg = b.sbuf("zneg", [128, 2048], side="right")
    hidb = b.sbuf("hidb", [128, 2048], BF16, side="right")
    cand2 = b.sbuf("cand2", [128, 8 * 512], side="right")
    srt32 = b.sbuf("srt32", [128, 8 * 32], side="right")

    scl2 = {"i": 0, "ab": None, "ac": scl_state["acs"][1]}

    def feed_s2(n):
        for _ in range(n):
            i = scl2["i"]
            if i >= 64:
                return
            ld = b.step("sync", (lambda i: lambda e: e.dma_start(
                out=zfs[:, :], in_=g["w_o2"][i * 128:(i + 1) * 128, :]))(i),
                deps=[scl2["ab"]] if scl2["ab"] else [], dma=True)
            ab = b.step("scalar", lambda e: e.activation(
                out=zfs[:, :], in_=zfs[:, :], func=AF.Abs, accum_out=sab1[:]),
                deps=[ld, scl2["ac"]])
            ac = b.step("vector", lambda e: e.tensor_tensor(
                out=acs1[:], in0=acs1[:], in1=sab1[:], op=OP.add), deps=[ab, scl2["ac"]])
            scl2["ab"] = ab
            scl2["ac"] = ac
            scl2["i"] = i + 1

    c_io = b.step("sync", lambda e: e.dma_start(out=iot[:], in_=g["iota32"][:, :]), dma=True)
    sel0 = b.step("vector", lambda e: e.memset(lo_t[:], 0.0), deps=[])
    prevh = sel0
    for tt in range(8):
        prevh = b.step("vector", (lambda tt: lambda e: e.reduce_max(
            out=hi_t[:, tt:tt + 1], in_=A1[:, tt * NH:(tt + 1) * NH],
            axis=AXL.X))(tt), deps=[prevh])
    selh = b.step("vector", lambda e: e.tensor_scalar(
        out=hi_t[:], in0=hi_t[:], scalar1=1.0, scalar2=None, op0=OP.add), deps=[prevh])

    ITERS = 14
    prev_it = [selh, sel0]
    for it in range(ITERS):
        m1a = b.step("vector", lambda e: e.tensor_tensor(
            out=mid_t[:], in0=lo_t[:], in1=hi_t[:], op=OP.add), deps=prev_it)
        m1 = b.step("vector", lambda e: e.tensor_scalar_mul(mid_t[:], mid_t[:], 0.5),
                    deps=[m1a])
        nm = b.step("vector", lambda e: e.tensor_scalar(
            out=negmid[:], in0=mid_t[:], scalar1=-1.0, scalar2=None, op0=OP.mult),
            deps=[m1])
        cts = []
        for tt in range(4):
            ct = b.step("vector", (lambda tt: lambda e: e.tensor_scalar(
                out=msk[:, :], in0=A1[:, tt * NH:(tt + 1) * NH],
                scalar1=mid_t[:, tt:tt + 1], scalar2=None,
                op0=OP.is_ge, op1=OP.add, accum_out=cnt_t[:, tt:tt + 1]))(tt), deps=[m1])
            cts.append(ct)
        sct = None
        for tt in range(4, 8):
            sct = b.step("scalar", (lambda tt: lambda e: e.activation(
                out=msk2[:, :], in_=A1[:, tt * NH:(tt + 1) * NH], func=AF.Sign,
                bias=negmid[:, tt:tt + 1], scale=1.0,
                accum_out=cnt_t[:, tt:tt + 1]))(tt), deps=[nm])
        cv8 = b.step("vector", lambda e: e.tensor_scalar(
            out=cnt_t[:, 4:8], in0=cnt_t[:, 4:8], scalar1=float(NH), scalar2=0.5,
            op0=OP.add, op1=OP.mult), deps=[sct])
        cts.append(cv8)
        feed_s2(5)
        u1 = b.step("vector", lambda e: e.tensor_scalar(
            out=cge_t[:], in0=cnt_t[:], scalar1=float(KTOP) - 0.5,
            scalar2=None, op0=OP.is_ge), deps=cts)
        u2 = b.step("vector", lambda e: e.select(
            out=lo_t[:], mask=cge_t[:], on_true=mid_t[:], on_false=lo_t[:]), deps=[u1])
        up = b.step("vector", lambda e: e.select(
            out=hi_t[:], mask=cge_t[:], on_true=hi_t[:], on_false=mid_t[:]), deps=[u2])
        prev_it = [up]

    selp = prev_it[0]
    feed_s2(64)
    c20 = b.step("vector", lambda e: e.memset(c2_t[:], 0.0),
                 deps=[selp, scl2["ab"]])
    prevs = c20
    for tt in range(8):
        for cc in range(4):
            zl2 = b.step("gpsimd", (lambda tt, cc: lambda e: e.dma_start(
                out=zfs[:, :], in_=g["z_dram"][tt * 128:(tt + 1) * 128, tsl(cc, 2048)]))(tt, cc),
                deps=[prevs], dma=True)
            cpa = b.step("vector", (lambda tt: lambda e: e.tensor_scalar(
                out=msk[:, 0:2048], in0=zfs[:, :],
                scalar1=hi_t[:, tt:tt + 1], scalar2=None,
                op0=OP.is_ge, op1=OP.add, accum_out=cnt_t[:, tt:tt + 1]))(tt), deps=[zl2])
            cpb = b.step("vector", (lambda tt: lambda e: e.tensor_tensor(
                out=c2_t[:, tt:tt + 1], in0=c2_t[:, tt:tt + 1],
                in1=cnt_t[:, tt:tt + 1], op=OP.add))(tt), deps=[cpa])
            cp = b.step("vector", (lambda tt: lambda e: e.tensor_scalar(
                out=zneg[:, :], in0=zfs[:, :],
                scalar1=hi_t[:, tt:tt + 1], scalar2=-BIG,
                op0=OP.is_ge, op1=OP.mult))(tt), deps=[cpb])
            cp2 = b.step("vector", lambda e: e.tensor_tensor(
                out=zfs[:, :], in0=zfs[:, :], in1=zneg[:, :], op=OP.add), deps=[cp])

            def f_m8(e, tt=tt, cc=cc):
                last = None
                for w in range(16):
                    last = e.max(out=cand2[:, tt * 512 + cc * 128 + w * 8:
                                          tt * 512 + cc * 128 + w * 8 + 8],
                                 in_=zfs[:, w * 128:(w + 1) * 128])
                return last
            prevs = b.step("vector", f_m8, deps=[cp2])

    rnd_prev = prevs
    for rnd in range(4):
        for tt in range(8):
            mx = b.step("vector", (lambda rnd, tt: lambda e: e.max(
                out=srt32[:, tt * 32 + rnd * 8:tt * 32 + rnd * 8 + 8],
                in_=cand2[:, tsl(tt, 512)]))(rnd, tt), deps=[rnd_prev])
            if rnd < 3:
                rma = b.step("vector", (lambda rnd, tt: lambda e: e.tensor_scalar(
                    out=zfs[:, 0:512], in0=cand2[:, tsl(tt, 512)],
                    scalar1=srt32[:, tt * 32 + rnd * 8 + 7:tt * 32 + rnd * 8 + 8],
                    scalar2=-BIG, op0=OP.is_ge, op1=OP.mult))(rnd, tt), deps=[mx])
                rm = b.step("vector", (lambda rnd, tt: lambda e: e.tensor_tensor(
                    out=cand2[:, tsl(tt, 512)], in0=cand2[:, tsl(tt, 512)],
                    in1=zfs[:, 0:512], op=OP.add))(rnd, tt), deps=[rma])
                rnd_prev = rm
            else:
                rnd_prev = mx
    r1 = b.step("vector", lambda e: e.tensor_scalar(
        out=jr_t[:], in0=c2_t[:], scalar1=-1.0, scalar2=float(KTOP - 1),
        op0=OP.mult, op1=OP.add), deps=[rnd_prev])
    r2 = None
    for tt in range(8):
        r2a = b.step("vector", (lambda tt: lambda e: e.tensor_scalar(
            out=zfs[:, 0:32], in0=iot[:], scalar1=jr_t[:, tt:tt + 1],
            scalar2=None, op0=OP.is_equal))(tt), deps=[r1, c_io] + ([r2] if r2 else []))
        r2b = b.step("vector", (lambda tt: lambda e: e.tensor_tensor(
            out=zfs[:, 32:64], in0=zfs[:, 0:32],
            in1=srt32[:, tsl(tt, 32)], op=OP.mult))(tt), deps=[r2a])
        r2 = b.step("vector", (lambda tt: lambda e: e.reduce_sum(
            out=Tst[:, tt:tt + 1], in_=zfs[:, 32:64], axis=AXL.X))(tt), deps=[r2b])
    r3a = b.step("vector", lambda e: e.tensor_scalar(
        out=mr_t[:], in0=jr_t[:], scalar1=-0.5, scalar2=None, op0=OP.is_ge), deps=[r2])
    r3b = b.step("vector", lambda e: e.tensor_scalar(
        out=cge_t[:], in0=jr_t[:], scalar1=31.5, scalar2=None, op0=OP.is_le), deps=[r3a])
    r3c = b.step("vector", lambda e: e.tensor_tensor(
        out=mr_t[:], in0=mr_t[:], in1=cge_t[:], op=OP.mult), deps=[r3b])
    r3 = b.step("vector", lambda e: e.select(
        out=Tuse[:], mask=mr_t[:], on_true=Tst[:], on_false=hi_t[:]), deps=[r3c])
    if t_thr is not None:
        b.step("gpsimd", lambda e: e.dma_start(out=g["t_thr"][:, 0:8], in_=Tuse[:, :]),
               deps=[r3], dma=True)
        b.step("gpsimd", lambda e: e.dma_start(out=g["t_thr"][:, 8:16], in_=c2_t[:, :]),
               deps=[r3], dma=True)
        b.step("gpsimd", lambda e: e.dma_start(out=g["t_thr"][:, 16:24], in_=lo_t[:, :]),
               deps=[r3], dma=True)
        b.step("gpsimd", lambda e: e.dma_start(out=g["t_thr"][:, 24:32], in_=hi_t[:, :]),
               deps=[r3], dma=True)

    b.release(srt32)
    b.release(cand2)
    hid_prev = r3
    for tt in range(8):
        for cc in range(4):
            zl3 = b.step("gpsimd", (lambda tt, cc: lambda e: e.dma_start(
                out=zfs[:, :], in_=g["z_dram"][tt * 128:(tt + 1) * 128, tsl(cc, 2048)]))(tt, cc),
                deps=[hid_prev], dma=True)
            hm = b.step("vector", (lambda tt: lambda e: e.scalar_tensor_tensor(
                out=zneg[:, :], in0=zfs[:, :], scalar=Tuse[:, tt:tt + 1], in1=zfs[:, :],
                op0=OP.is_ge, op1=OP.mult))(tt), deps=[zl3, r3])
            hr = b.step("scalar", lambda e: e.activation(out=zneg[:, :], in_=zneg[:, :],
                                                         func=AF.Relu), deps=[hm])

            def f_trh(e, tt=tt, cc=cc):
                insts = []
                for bb in range(16):
                    pzi = pz[bb % 4]
                    insts.append(e.transpose(pzi[:, (bb // 4) * 128:(bb // 4) * 128 + 128],
                                             zneg[:, bb * 128:(bb + 1) * 128], ident[:]))
                return insts[-1]
            trh = b.step("tensor", f_trh, deps=[hr, hid_prev])
            def f_evh(e, tt=tt, cc=cc):
                last = None
                for bb in range(16):
                    kk = cc * 16 + bb
                    pzi = pz[bb % 4]
                    last = e.tensor_copy(
                        out=A1[:, kk * 1024 + tt * 128:kk * 1024 + tt * 128 + 128],
                        in_=pzi[:, (bb // 4) * 128:(bb // 4) * 128 + 128])
                return last
            hid_prev = b.step("vector", f_evh, deps=[trh])

    mmt2 = b.step("tensor", lambda e: e.matmul(
        pz[0][0:1, 0:1], ones1f[:], acs1[:], start=True, stop=True),
        deps=[scl2["ac"], hid_prev, c_o1f])
    e2v = b.step("vector", lambda e: e.tensor_scalar(
        out=srow2[0:1, 0:1], in0=pz[0][0:1, 0:1],
        scalar1=1.0 / (NH * D), scalar2=None, op0=OP.mult), deps=[mmt2])
    sprep2 = b.step("vector", lambda e: e.tensor_scalar(
        out=srow2[0:1, 1:2], in0=srow2[0:1, 0:1], scalar1=0.5, scalar2=None,
        op0=OP.mult), deps=[e2v])
    s_eps2 = b.step("vector", lambda e: e.tensor_scalar(
        out=srow2[0:1, 2:3], in0=srow2[0:1, 0:1], scalar1=1e-8, scalar2=None,
        op0=OP.add), deps=[sprep2])
    s_rec2 = b.step("vector", lambda e: e.reciprocal(
        out=srow2[0:1, 2:3], in_=srow2[0:1, 2:3]), deps=[s_eps2])
    s_rc2b = b.step("vector", lambda e: e.reciprocal(
        out=czr[:, 1:2], in_=srow2[0:1, 1:2]), deps=[sprep2, s_rec2])
    bdep2 = [s_rc2b]
    for (rw, dst) in ((srow2[0:1, 2:3], invb2), (srow2[0:1, 1:2], czb2)):
        mmi2 = b.step("tensor", (lambda rw: lambda e: e.matmul(
            pz[1][:, 0:1], onesr[:], rw, start=True, stop=True))(rw),
            deps=bdep2 + [hid_prev])
        evi2 = b.step("vector", (lambda dst: lambda e: e.tensor_copy(
            out=dst[:], in_=pz[1][:, 0:1]))(dst), deps=[mmi2])
        bdep2 = [evi2]

    bar_F2 = b.barrier()
    if upto == 'F':
        b.finalize(); b.close(); return nc, g

    for t in [hidb, zneg, zfs, iot, mr_t, jr_t, c2_t, Tst, Tuse, cge_t, cnt_t,
              mid_t, hi_t, lo_t, msk, negmid, srow2, msk2]:
        b.release(t)
    v2b = b.sbuf("v2b", [128, 16 * 1024], BF16)
    h1rel = b.sbuf("h1rel", [128, 1024], side="right")
    wstG = b.sbuf("wstG", [128, 1024], side="right")
    ternG = b.sbuf("ternG", [128, 1024], BF16, side="right")
    sgnG = b.sbuf("sgnG", [128, 512], BF16, side="right")
    borow = b.sbuf("borow", [1, 512], side="right")
    ones512 = b.sbuf("ones512", [1, 512], side="right")
    c_o512 = b.step("vector", lambda e: e.memset(ones512[:], 1.0), deps=[])

    t_v2 = tap("t_v2", [D, T])

    last_evG = None
    prev_slab_mm = None
    for slab in range(4):
        bl = b.step("sync", (lambda slab: lambda e: e.dma_start(
            out=borow[0:1, :], in_=g["b_out"].rearrange("(one n) -> one n", one=1)
            [:, tsl(slab, 512)]))(slab),
            deps=[prev_slab_mm] if prev_slab_mm else [], dma=True)
        bs = b.step("vector", lambda e: e.tensor_scalar(
            out=borow[0:1, :], in0=borow[0:1, :], scalar1=czr[0:1, 1:2],
            scalar2=None, op0=OP.mult), deps=[bl, s_rc2])
        last_mm = None
        mm_histG = []
        prev_tn = None
        for k in range(64):
            par = k % 2
            lk = mm_histG[-2] if len(mm_histG) >= 2 else (last_evG if last_evG else None)
            wl = b.step("sync", (lambda slab, k, par: lambda e: e.dma_start(
                out=wstG[:, par * 512:par * 512 + 512],
                in_=g["w_o2"][k * 128:(k + 1) * 128, tsl(slab, 512)]))(slab, k, par),
                deps=[lk] if lk else [], dma=True)
            sg = b.step("scalar", (lambda par: lambda e: [
                e.activation(out=sgnG[:, 0:512],
                             in_=wstG[:, par * 512:par * 512 + 512], func=AF.Sign,
                             bias=halfn[:], scale=invb2[:]),
                e.activation(out=ternG[:, par * 512:par * 512 + 512],
                             in_=wstG[:, par * 512:par * 512 + 512], func=AF.Sign,
                             bias=halfp[:], scale=invb2[:])][-1])(par),
                deps=[wl] + ([lk] if lk else []) + ([prev_tn] if prev_tn else []))
            tn = b.step("vector", (lambda par: lambda e: e.tensor_tensor(
                out=ternG[:, par * 512:par * 512 + 512],
                in0=sgnG[:, 0:512],
                in1=ternG[:, par * 512:par * 512 + 512], op=OP.add))(par), deps=[sg])
            prev_tn = tn

            def f_mm(e, slab=slab, k=k, par=par):
                last = None
                for oc in range(4):
                    for th in range(2):
                        last = e.matmul(pz[oc * 2 + th][:, :],
                                        ternG[:, par * 512 + oc * 128:par * 512 + (oc + 1) * 128],
                                        A1[:, k * 1024 + th * 512:k * 1024 + th * 512 + 512],
                                        start=(k == 0), stop=False)
                if k == 63:
                    for oc in range(4):
                        for th in range(2):
                            last = e.matmul(pz[oc * 2 + th][:, :],
                                            borow[0:1, oc * 128:(oc + 1) * 128],
                                            ones512[0:1, 0:512],
                                            start=False, stop=True)
                return last
            deps = [tn, bs, c_o512]
            if last_mm is not None:
                deps.append(last_mm)
            if k == 0 and last_evG is not None:
                deps.append(last_evG)
            last_mm = b.step("tensor", f_mm, deps=deps)
            mm_histG.append(last_mm)
        prev_slab_mm = last_mm
        for oc in range(4):
            dt_i = slab * 4 + oc
            hdeps = [last_mm]
            if last_evG is not None:
                hdeps.append(last_evG)
            hl = b.step("gpsimd", (lambda dt_i: lambda e: e.dma_start(
                out=h1rel[:, :], in_=g["h1_dram"][tsl(dt_i, 128), :]))(dt_i),
                deps=hdeps, dma=True)
            for th in range(2):
                ev = b.step("vector", (lambda dt_i, oc, th: lambda e: e.scalar_tensor_tensor(
                    out=v2b[:, dt_i * 1024 + th * 512:dt_i * 1024 + th * 512 + 512],
                    in0=pz[oc * 2 + th][:, :], scalar=czb2[:],
                    in1=h1rel[:, th * 512:(th + 1) * 512],
                    op0=OP.mult, op1=OP.add))(dt_i, oc, th), deps=[hl, last_mm])
                last_evG = ev
    if t_v2 is not None:
        for i in range(16):
            b.step("gpsimd", (lambda i: lambda e: e.dma_start(
                out=g["t_v2"][tsl(i, 128), :], in_=v2b[:, tsl(i, 1024)]))(i),
                deps=[last_evG], dma=True)

    for t in [ones512, borow, sgnG, ternG, wstG]:
        b.release(t)
    ysto = b.sbuf("ysto", [128, 1024], side="right")
    lnscr = b.sbuf("lnscr", [128, 1024], side="right")
    for t in reversed(pz):
        b.release(t)
    psq2 = b.psum("psq2", [1, 1024])
    ps12 = b.psum("ps12", [1, 1024])
    psb22 = b.psum("psb22", [128, 512])

    def ln_stats2(src_bf, dep):
        mm_prev = None
        for i in range(16):
            c2 = b.step("scalar", (lambda i: lambda e: e.activation(
                out=sqb[:], in_=src_bf[:, tsl(i, 1024)], func=AF.Square))(i),
                deps=[dep] + ([mm_prev] if mm_prev else []))
            mm_prev = b.step("tensor", (lambda i: lambda e: [
                e.matmul(psq2[0:1, 0:512], ones1c[:], sqb[:, 0:512],
                         start=(i == 0), stop=(i == 15)),
                e.matmul(psq2[0:1, 512:1024], ones1c[:], sqb[:, 512:1024],
                         start=(i == 0), stop=(i == 15)),
                e.matmul(ps12[0:1, 0:512], ones1c[:], src_bf[:, i * 1024:i * 1024 + 512],
                         start=(i == 0), stop=(i == 15)),
                e.matmul(ps12[0:1, 512:1024], ones1c[:],
                         src_bf[:, i * 1024 + 512:(i + 1) * 1024],
                         start=(i == 0), stop=(i == 15))][-1])(i),
                deps=[c2, c_o1])
        s1 = b.step("vector", lambda e: e.tensor_scalar_mul(rmu[:], ps12[0:1, :], 1.0 / D),
                    deps=[mm_prev])
        s2 = b.step("vector", lambda e: e.tensor_scalar_mul(rvar[:], psq2[0:1, :], 1.0 / D),
                    deps=[s1])
        s3 = b.step("vector", lambda e: e.tensor_tensor(out=rtmp[:], in0=rmu[:], in1=rmu[:],
                                                        op=OP.mult), deps=[s2])
        s4 = b.step("vector", lambda e: e.tensor_tensor(out=rvar[:], in0=rvar[:], in1=rtmp[:],
                                                        op=OP.subtract), deps=[s3])
        s5 = b.step("scalar", lambda e: e.activation(out=rtmp[:], in_=rvar[:], func=AF.Ln,
                                                     bias=epsr[:], scale=1.0), deps=[s4])
        s6 = b.step("scalar", lambda e: e.activation(out=rvar[:], in_=rtmp[:], func=AF.Exp,
                                                     scale=mhalfr[:]), deps=[s5])
        bdep = [s6]
        for (row, dst) in ((rmu, mub), (rvar, ivb)):
            for j in range(2):
                mmx = b.step("tensor", (lambda row, j: lambda e: e.matmul(
                    psb22[:, :], onesr[:], row[:, j * 512:(j + 1) * 512],
                    start=True, stop=True))(row, j), deps=bdep)
                evx = b.step("vector", (lambda dst, j: lambda e: e.tensor_copy(
                    out=dst[:, j * 512:(j + 1) * 512], in_=psb22[:, :]))(dst, j), deps=[mmx])
                bdep = [evx]
        return bdep[0]

    ln2 = ln_stats2(v2b, last_evG)
    lastg = ln2
    for i in range(16):
        hl = b.step("gpsimd", (lambda i: lambda e: e.dma_start(
            out=h1rel[:, :], in_=g["h1_dram"][tsl(i, 128), :]))(i), deps=[lastg], dma=True)
        g1 = b.step("vector", (lambda i: lambda e: [
            e.tensor_scalar(out=lnscr[:, :], in0=ivb[:, :], scalar1=gam[:, i:i + 1],
                            scalar2=None, op0=OP.mult),
            e.tensor_tensor(out=v2b[:, tsl(i, 1024)], in0=v2b[:, tsl(i, 1024)], in1=mub[:],
                            op=OP.subtract),
            e.tensor_tensor(out=v2b[:, tsl(i, 1024)], in0=v2b[:, tsl(i, 1024)], in1=lnscr[:],
                            op=OP.mult),
            e.tensor_scalar(out=v2b[:, tsl(i, 1024)], in0=v2b[:, tsl(i, 1024)],
                            scalar1=bet[:, i:i + 1], scalar2=None, op0=OP.add),
            e.tensor_tensor(out=v2b[:, tsl(i, 1024)], in0=v2b[:, tsl(i, 1024)], in1=h1rel[:],
                            op=OP.add)][-1])(i), deps=[hl, lastg])
        lastg = g1
    ln3 = ln_stats2(v2b, lastg)
    lasty = ln3
    for i in range(16):
        y1 = b.step("vector", (lambda i: lambda e: [
            e.tensor_tensor(out=ysto[:, :], in0=v2b[:, tsl(i, 1024)], in1=mub[:],
                            op=OP.subtract),
            e.tensor_tensor(out=ysto[:, :], in0=ysto[:, :], in1=ivb[:],
                            op=OP.mult)][-1])(i), deps=[lasty])
        yd = b.step("gpsimd", (lambda i: lambda e: e.dma_start(
            out=g["y_out"][tsl(i, 128), :], in_=ysto[:, :]))(i), deps=[y1], dma=True)
        lasty = yd
    b.barrier()
    b.finalize()
    b.close()
    return nc, g


def _prep_inputs(inputs):
    x = np.ascontiguousarray(inputs["L0_attn_x"])
    w_in = np.ascontiguousarray(inputs["L0_attn_in_proj_w"].T)
    w_xp = np.ascontiguousarray(inputs["L0_attn_x_proj_w"].T)
    w_dt = np.ascontiguousarray(inputs["L0_attn_dt_proj_w"].T)
    w_out = np.ascontiguousarray(inputs["L0_attn_out_proj_w"].T)
    w_syn = np.ascontiguousarray(inputs["L0_ffn_W_syn"].T)
    w_o2 = np.ascontiguousarray(inputs["L0_ffn_W_out"].T)
    common = dict(
        w_in=w_in, w_xp=w_xp, w_dt=w_dt, w_out=w_out, w_syn=w_syn, w_o2=w_o2,
        conv_w=np.ascontiguousarray(
            inputs["L0_attn_conv_w"].reshape(32, 128, 4).transpose(1, 0, 2).reshape(128, 128)),
        conv_b=np.ascontiguousarray(inputs["L0_attn_conv_b"].reshape(32, 128).T),
        b_dt=np.ascontiguousarray(inputs["L0_attn_dt_proj_b"]),
        a_log=np.ascontiguousarray(inputs["L0_attn_A_log"]),
        d_vec=np.ascontiguousarray(inputs["L0_attn_D"]),
        b_syn=np.ascontiguousarray(inputs["L0_ffn_b_syn"]),
        b_out=np.ascontiguousarray(inputs["L0_ffn_b_out"]),
        gamma=np.ascontiguousarray(inputs["L0_ffn_gamma"].reshape(16, 128).T),
        beta=np.ascontiguousarray(inputs["L0_ffn_beta"].reshape(16, 128).T),
        iota32=np.tile(np.arange(32, dtype=np.float32), (128, 1)),
        d_row=np.ascontiguousarray(inputs["L0_attn_D"].reshape(1, 64)),
    )
    in_maps = []
    for c in range(8):
        seq, half = c // 2, c % 2
        if half == 0:
            xs = np.concatenate([x[seq, 0:4], x[seq, 0:1024]], axis=0).T
            hm = np.zeros((128, 1), np.float32)
            rl = np.zeros((128, 1), np.float32)
        else:
            xs = x[seq, 1020:2048].T
            hm = np.ones((128, 1), np.float32)
            rl = np.ones((128, 1), np.float32)
        m = dict(common)
        m["x_in"] = np.ascontiguousarray(xs)
        m["halo_mask"] = hm
        m["role"] = rl
        in_maps.append(m)
    return in_maps


def kernel(**inputs):
    in_maps = _prep_inputs(inputs)
    nc, _ = build_nc(taps=DEBUG_TAPS)
    res = run_bass_kernel_spmd(nc, in_maps, core_ids=list(range(8)))
    out = np.empty((4, 2048, 2048), np.float32)
    for c in range(8):
        seq, half = c // 2, c % 2
        out[seq, half * 1024:(half + 1) * 1024, :] = res.results[c]["y_out"].T
    return out
